# version 04
# baseline (speedup 1.0000x reference)
import contextlib
import numpy as np
import ml_dtypes
import concourse.bass as bass
import concourse.mybir as mybir
from concourse.bass import IndirectOffsetOnAxis
from concourse.bass_utils import run_bass_kernel_spmd

F32 = mybir.dt.float32
BF16 = mybir.dt.bfloat16
I32 = mybir.dt.int32
AF = mybir.ActivationFunctionType
ALU = mybir.AluOpType
AX = mybir.AxisListType

COMPUTE_ENGINES = ("tensor", "vector", "scalar", "gpsimd")
DMA_QUEUES = ("sync", "scalar", "gpsimd")
DMABUS = "_DMABUS_"


def _key(x):
    if isinstance(x, bass.AP):
        return x.tensor.name
    if isinstance(x, str):
        return x
    return x.name


DT_BYTES = {F32: 4, BF16: 2, I32: 4}


class SbAlloc:

    BASE = 16384

    def __init__(self, prog, budget=204 * 1024):
        self.p = prog
        self.budget = budget
        self.free_list = [(self.BASE, budget)]
        self.regions = {}
        self.dead = []

    def alloc(self, name, shape, dtype=F32, align=64):
        nbytes = int(np.prod(shape[1:])) * DT_BYTES[dtype] if len(shape) > 1 else DT_BYTES[dtype]
        nbytes = (nbytes + align - 1) // align * align
        for k, (off, sz) in enumerate(self.free_list):
            aoff = (off + align - 1) // align * align
            if aoff + nbytes <= off + sz:
                newfree = []
                if aoff > off:
                    newfree.append((off, aoff - off))
                if off + sz > aoff + nbytes:
                    newfree.append((aoff + nbytes, off + sz - (aoff + nbytes)))
                self.free_list[k:k + 1] = newfree
                t = self.p.nc.alloc_sbuf_tensor_at(name, list(shape),
                                                   dtype, offset=aoff)
                name = t.name
                self.regions[name] = (aoff, nbytes)
                olds = [n for (o, s, n) in self.dead if not (o + s <= aoff or aoff + nbytes <= o)]
                if olds:
                    self.p._alias[name] = olds
                return t
        raise MemoryError(f"SBUF alloc {name} {nbytes}B failed; free={self.free_list}")

    def free(self, t):
        name = t if isinstance(t, str) else t.name
        off, sz = self.regions.pop(name)
        self.dead.append((off, sz, name))
        self.free_list.append((off, sz))
        self.free_list.sort()
        merged = []
        for off, sz in self.free_list:
            if merged and merged[-1][0] + merged[-1][1] == off:
                merged[-1] = (merged[-1][0], merged[-1][1] + sz)
            else:
                merged.append((off, sz))
        self.free_list = [tuple(x) for x in merged]

    def free_all(self, ts):
        for t in ts:
            self.free(t)


class Prog:
    def __init__(self, name="prog"):
        self.name = name
        self.nc = bass.Bass("TRN2", target_bir_lowering=False)
        self.ops = []
        self.const_inputs = {}
        self._namectr = 0
        self._alias = {}
        self.sba = SbAlloc(self)
        self.psum_pool = None

    def uname(self, prefix):
        self._namectr += 1
        return f"{prefix}_{self._namectr}"

    def sb(self, name, shape, dtype=F32):
        return self.sba.alloc(name, shape, dtype)

    def free(self, *ts):
        for t in ts:
            if isinstance(t, (list, tuple)):
                self.free(*t)
            else:
                self.sba.free(t)

    def psums(self, n=8):
        if self.psum_pool is None:
            self.psum_pool = [self.nc.alloc_psum_tensor(f"P{k}", [128, 512], F32)
                              for k in range(n)]
        return self.psum_pool

    def ps(self, name, shape, dtype=F32):
        return self.nc.alloc_psum_tensor(name, list(shape), dtype)

    def din(self, name, shape, dtype=F32):
        return self.nc.dram_tensor(name, list(shape), dtype, kind="ExternalInput").ap()

    def dout(self, name, shape, dtype=F32):
        return self.nc.dram_tensor(name, list(shape), dtype, kind="ExternalOutput").ap()

    def dscratch(self, name, shape, dtype=F32, addr_space="Local"):
        return self.nc.dram_tensor(name, list(shape), dtype, kind="Internal",
                                   addr_space=addr_space).ap()

    def const(self, name, arr):
        dt = {np.dtype(np.float32): F32, np.dtype(ml_dtypes.bfloat16): BF16,
              np.dtype(np.int32): I32}[arr.dtype]
        self.const_inputs[name] = arr
        return self.din(name, arr.shape, dt)

    def _wkeys(self, writes):
        out = []
        for w in writes:
            k = _key(w)
            out.append(k)
            if k in self._alias:
                out.extend(self._alias[k])
        return out

    def op(self, engine, emit, reads=(), writes=(), tag=""):
        assert engine in COMPUTE_ENGINES, engine
        self.ops.append(dict(kind="c", engine=engine, emit=emit,
                             reads=[_key(r) for r in reads],
                             writes=self._wkeys(writes), tag=tag))

    def dma(self, queue, out, in_, tag="dma", **kw):
        assert queue in DMA_QUEUES
        self.ops.append(dict(kind="d", engine=queue,
                             emit=lambda eng, out=out, in_=in_, kw=kw: eng.dma_start(out=out, in_=in_, **kw),
                             reads=[_key(in_), DMABUS], writes=self._wkeys([out]), tag=tag))

    def dma_transpose(self, queue, out, in_, tag="dmat"):
        assert queue in ("sync", "scalar")
        self.ops.append(dict(kind="d", engine=queue,
                             emit=lambda eng, out=out, in_=in_: eng.dma_start_transpose(out, in_),
                             reads=[_key(in_)], writes=self._wkeys([out]) + [DMABUS], tag=tag))

    def indirect(self, out, out_off, in_, in_off, extra_reads=(), tag="ind", **kw):
        kw.setdefault("oob_is_err", False)
        if not hasattr(self, "_bc_regs"):
            self._bc_regs = {}

        def emit(eng, kw=kw):
            bc = kw.get("bounds_check")
            if isinstance(bc, int):
                if bc not in self._bc_regs:
                    self._bc_regs[bc] = eng.to_reg(bc)
                kw = dict(kw, bounds_check=self._bc_regs[bc])
            return eng.indirect_dma_start(out, out_off, in_, in_off, **kw)
        self.ops.append(dict(kind="d", engine="gpsimd", emit=emit,
                             reads=[_key(in_), DMABUS] + [_key(r) for r in extra_reads],
                             writes=self._wkeys([out]), tag=tag))

    def collective(self, kind_, op_, ins, outs, n_cores=8, tag="cc"):
        def emit(eng):
            return eng.collective_compute(
                kind_, op_, replica_groups=[list(range(n_cores))],
                ins=list(ins), outs=list(outs))
        self.ops.append(dict(kind="d", engine="gpsimd", emit=emit,
                             reads=[_key(a) for a in ins] + [DMABUS],
                             writes=self._wkeys(list(outs)), tag=tag, cc=True))

    def copy(self, eng, out, in_, tag="copy"):
        if eng == "scalar":
            self.op(eng, lambda e: e.copy(out, in_), [in_], [out], tag)
        else:
            self.op(eng, lambda e: e.tensor_copy(out, in_), [in_], [out], tag)

    def memset(self, eng, ap, v, tag="memset"):
        self.op(eng, lambda e: e.memset(ap, v), [], [ap], tag)

    def tt(self, eng, out, a, b, alu, tag="tt"):
        self.op(eng, lambda e: e.tensor_tensor(out, a, b, alu), [a, b], [out], tag)

    def ts(self, eng, out, in_, s1, op0, s2=None, op1=None, tag="ts"):
        reads = [in_] + [s for s in (s1, s2) if isinstance(s, bass.AP)]
        if op1 is None:
            self.op(eng, lambda e: e.tensor_scalar(out, in_, s1, None, op0), reads, [out], tag)
        else:
            self.op(eng, lambda e: e.tensor_scalar(out, in_, s1, s2, op0, op1), reads, [out], tag)

    def stt(self, eng, out, in0, scalar, in1, op0, op1, tag="stt"):
        reads = [in0, in1] + ([scalar] if isinstance(scalar, bass.AP) else [])
        self.op(eng, lambda e: e.scalar_tensor_tensor(out, in0, scalar, in1, op0, op1),
                reads, [out], tag)

    def kcol(self, v, parts=128):
        if not hasattr(self, "_ktile") or self._ktile is None:
            KV = 32
            self._ktile = self.sb("kconst_auto", (128, KV), F32)
            self._kmax = KV
            self._karr = np.zeros((128, KV), np.float32)
            self._kvals = {}
            self._kh = self.const("c_kconst_auto", self._karr)
            self.dma("sync", self._ktile[:, :], self._kh)
        if v not in self._kvals:
            idx = len(self._kvals)
            assert idx < self._kmax
            self._kvals[v] = idx
            self._karr[:, idx] = v
        i = self._kvals[v]
        return self._ktile[0:parts, i:i + 1]

    def act(self, out, in_, func, bias=0.0, scale=1.0, accum_out=None, tag="act"):
        if isinstance(bias, (float, int)) and func not in (AF.Copy,):
            bias = self.kcol(float(bias), out.shape[0])
        reads = [in_] + [b for b in (bias, scale) if isinstance(b, bass.AP)]
        writes = [out] + ([accum_out] if accum_out is not None else [])
        self.op("scalar", lambda e: e.activation(out, in_, func, bias=bias, scale=scale,
                                                 accum_out=accum_out),
                reads, writes, tag)

    def reduce(self, eng, out, in_, alu, axis=AX.X, negate=None, tag="red"):
        if negate is None:
            self.op(eng, lambda e: e.tensor_reduce(out, in_, axis, alu), [in_], [out], tag)
        else:
            self.op(eng, lambda e: e.tensor_reduce(out, in_, axis, alu, negate=negate),
                    [in_], [out], tag)

    def scan(self, out, d0, d1, init, op0, op1, tag="scan"):
        reads = [d0, d1] + ([init] if isinstance(init, bass.AP) else [])
        self.op("vector", lambda e: e.tensor_tensor_scan(out, d0, d1, init, op0, op1),
                reads, [out], tag)

    def recip(self, out, in_, tag="recip"):
        self.op("vector", lambda e: e.reciprocal(out, in_), [in_], [out], tag)

    def select(self, out, mask, on_true, on_false, tag="sel"):
        self.op("vector", lambda e: e.select(out, mask, on_true, on_false),
                [mask, on_true, on_false], [out], tag)

    def mm(self, out, lhsT, rhs, start=True, stop=True, tag="mm"):
        self.op("tensor", lambda e: e.matmul(out, lhsT, rhs, start=start, stop=stop),
                [lhsT, rhs], [out], tag)

    def mm_acc(self, out, pairs, tag="mma"):
        pairs = list(pairs)
        reads = []
        for l, r in pairs:
            reads += [l, r]

        def emit(e):
            ins = None
            n = len(pairs)
            for i, (l, r) in enumerate(pairs):
                ins = e.matmul(out, l, r, start=(i == 0), stop=(i == n - 1))
            return ins
        self.op("tensor", emit, reads, [out], tag)

    def transpose(self, out_psum, in_sbuf, ident, tag="pet"):
        self.op("tensor", lambda e: e.transpose(out_psum, in_sbuf, ident),
                [in_sbuf, ident], [out_psum], tag)

    def iota(self, out, pattern, base=0, channel_multiplier=0, tag="iota"):
        self.op("gpsimd", lambda e: e.iota(out, pattern=pattern, base=base,
                                           channel_multiplier=channel_multiplier),
                [], [out], tag)

    def build(self, same_engine_sync=True):
        nc = self.nc
        ops = self.ops
        CTR = {e: i for i, e in enumerate(COMPUTE_ENGINES)}
        for i, q in enumerate(DMA_QUEUES):
            CTR["q" + q] = 4 + i
        CTR["cc"] = 7
        NCTR = 8
        T_PE = CTR["tensor"]

        tick = [0] * NCTR
        last_write = {}
        readers = {}
        op_waits = [None] * len(ops)
        op_inc = [None] * len(ops)

        for i, o in enumerate(ops):
            eng = o["engine"]
            if o.get("cc"):
                ctr, inc = CTR["cc"], 1
            elif o["kind"] == "d":
                ctr, inc = CTR["q" + eng], 16
            else:
                ctr, inc = CTR[eng], 1
            waits = {}

            def need(dep):
                dep_ctr, dep_val = dep
                if o["kind"] == "c":
                    if dep_ctr == ctr:
                        if not same_engine_sync or ctr == T_PE:
                            return
                waits[dep_ctr] = max(waits.get(dep_ctr, 0), dep_val)

            for kk in o["reads"]:
                lw = last_write.get(kk)
                if lw:
                    need(lw)
            for kk in o["writes"]:
                lw = last_write.get(kk)
                if lw:
                    need(lw)
                for rd in readers.get(kk, {}).items():
                    need(rd)
            tick[ctr] += inc
            myval = tick[ctr]
            if o["kind"] == "d" and not o.get("cc") and myval > 16:
                waits[ctr] = max(waits.get(ctr, 0), myval - inc)
            for kk in o["writes"]:
                last_write[kk] = (ctr, myval)
                readers[kk] = {}
            for kk in o["reads"]:
                readers.setdefault(kk, {})[ctr] = myval
            op_waits[i] = waits
            op_inc[i] = (ctr, inc, myval)
        self._dbg_waits = op_waits
        self._dbg_inc = op_inc

        streams = {e: [] for e in COMPUTE_ENGINES + ("sync",)}
        for i, o in enumerate(ops):
            streams[o["engine"]].append(i)

        nwaits = 0
        for e, idxs in streams.items():
            known = {}
            for i in idxs:
                pruned = {}
                for c, v in sorted(op_waits[i].items()):
                    if known.get(c, 0) >= v:
                        continue
                    pruned[c] = v
                    known[c] = v
                op_waits[i] = pruned
                nwaits += len(pruned)

        final_waits = {c: tick[c] for c in range(NCTR) if tick[c] > 0}

        with contextlib.ExitStack() as st:
            sem_handles = [st.enter_context(nc.semaphore(f"s{c}")) for c in range(NCTR)]
            block = st.enter_context(nc.Block())

            def emit_stream(eng_name):
                def body(eng):
                    for i in streams[eng_name]:
                        o = ops[i]
                        for c, v in sorted(op_waits[i].items()):
                            eng.wait_ge(sem_handles[c], v)
                        try:
                            ins = o["emit"](eng)
                        except Exception as ex:
                            import traceback
                            traceback.print_exc()
                            raise RuntimeError(f"emit failed for op tag={o['tag']} "
                                               f"engine={eng_name}: {ex}") from ex
                        if ins is None:
                            raise RuntimeError(f"op {o['tag']} emit returned None")
                        c, inc, myval = op_inc[i]
                        ins.then_inc(sem_handles[c], inc)
                return body
            if streams["sync"]:
                block.sync(emit_stream("sync"))
            for e in COMPUTE_ENGINES:
                if streams[e]:
                    getattr(block, e)(emit_stream(e))

            def fin(eng):
                for c, v in sorted(final_waits.items()):
                    eng.wait_ge(sem_handles[c], v)
            block.sync(fin)
        return nc

    def run(self, in_maps, core_ids=None, **kw):
        if core_ids is None:
            core_ids = list(range(len(in_maps)))
        for m in in_maps:
            m.update(self.const_inputs)
        return run_bass_kernel_spmd(self.nc, in_maps, core_ids=core_ids, **kw)


"""Per-block builders for the mk_prb_d4 megakernel.

Conventions:
- Residual stream x is FEATURE-MAJOR in HBM: (1024, Tloc) fp32 per core.
- fm SBUF activation = list of 8 tiles (128, T): features on partitions,
  tokens on the free axis.
- matmul: out = lhsT.T @ rhs; weights W (in,out) load naturally as lhsT chunks.
- PSUM pool P[0..7] (128,512) fp32: P0/P1 matmul rotation, P2/P3/P4 row reps,
  P5 spare, P6/P7 stats rows.
"""
import numpy as np
import ml_dtypes
DBG = True


D = 1024
NI = D // 128
TC = 512


def tchunks(T, c=TC):
    return [(t0, min(c, T - t0)) for t0 in range(0, T, c)]


class Common:
    def __init__(self, p: Prog):
        self.p = p
        self.ones_col = p.sb("ones_col", (128, 1), F32)
        self.ones_row = p.sb("ones_row", (1, 128), F32)
        self.ident = p.sb("ident", (128, 128), BF16)
        self.ones_row16 = p.sb("ones_row16", (1, 128), BF16)
        self.ones_col16 = p.sb("ones_col16", (128, 1), BF16)
        p.dma("sync", self.ones_col[:, :], p.const("c_ones_col", np.ones((128, 1), np.float32)))
        p.dma("sync", self.ones_row[:, :], p.const("c_ones_row", np.ones((1, 128), np.float32)))
        p.dma("sync", self.ident[:, :], p.const("c_ident", np.eye(128, dtype=ml_dtypes.bfloat16)))
        p.dma("sync", self.ones_row16[:, :], p.const("c_ones_row16", np.ones((1, 128), ml_dtypes.bfloat16)))
        p.dma("sync", self.ones_col16[:, :], p.const("c_ones_col16", np.ones((128, 1), ml_dtypes.bfloat16)))
        self.P = p.psums()


def load_vec_cols(p, name, hbm_ap, n=D, dtype=F32):
    t = p.sb(name, (128, n // 128), dtype)
    src = hbm_ap.rearrange("(i j) -> j i", j=128)
    p.dma("sync", t[:, :], src, tag=f"ld_{name}", allow_slow_non_contiguous=True)
    return [t[:, i:i + 1] for i in range(n // 128)]


def load_w_bf16(p, name, w_ap, I=D, O=D):
    tiles = []
    for i in range(I // 128):
        t = p.sb(f"{name}_{i}", (128, O), BF16)
        p.dma("gpsimd", t[:, :], w_ap[i * 128:(i + 1) * 128, :], tag=f"ld_{name}{i}")
        tiles.append(t)
    return tiles


def ln_rows(p, cm, x_tiles, T, name, eps=1e-5, inner=False, affine_rows=None):
    P = cm.P
    rows = p.sb(f"{name}_rows", (1, 6 * T), F32)
    m = rows[:, 0:T]; v = rows[:, T:2 * T]; ra = rows[:, 2 * T:3 * T]
    rb = rows[:, 3 * T:4 * T]; r2 = rows[:, 4 * T:5 * T]; tmp = rows[:, 5 * T:6 * T]
    sq = [p.sb(p.uname(f"{name}_sq"), (128, TC), F32) for k in range(2)]
    for (t0, tl) in tchunks(T):
        p.mm_acc(P[6][0:1, :tl],
                 [(cm.ones_col[:, :], x[:, t0:t0 + tl]) for x in x_tiles], tag="lnS1")
        p.copy("vector", m[:, t0:t0 + tl], P[6][0:1, :tl], tag="lnS1c")
        for i, x in enumerate(x_tiles):
            b = sq[i % 2]
            p.act(b[:, :tl], x[:, t0:t0 + tl], AF.Square, tag="lnSq")
            p.mm(P[7][0:1, :tl], cm.ones_col[:, :], b[:, :tl],
                 start=(i == 0), stop=(i == NI - 1), tag="lnS2")
        p.copy("vector", v[:, t0:t0 + tl], P[7][0:1, :tl], tag="lnS2c")
    k = 1.0 / D
    p.ts("vector", m, m, k, ALU.mult, tag="ln_m")
    p.ts("vector", tmp, v, k, ALU.mult, tag="ln_ms")
    p.tt("vector", v, m, m, ALU.mult, tag="ln_mm")
    p.tt("vector", v, tmp, v, ALU.subtract, tag="ln_v")
    p.act(tmp, v, AF.Sqrt, bias=p.kcol(eps, 1), tag="ln_sqrt")
    p.recip(ra, tmp, tag="ln_ra")
    if inner:
        p.tt("vector", tmp, ra, ra, ALU.mult, tag="ln_ra2")
        p.tt("vector", tmp, v, tmp, ALU.mult, tag="ln_vr")
        p.act(tmp, tmp, AF.Sqrt, bias=p.kcol(eps, 1), tag="ln_sqrt2")
        p.recip(r2, tmp, tag="ln_r2")
    p.tt("vector", rb, m, ra, ALU.mult, tag="ln_mra")
    p.ts("vector", rb, rb, -1.0, ALU.mult, tag="ln_rb")
    p.free(*sq)
    return rows, dict(rows=rows, ra=ra, rb=rb, r2=r2, m=m, v=v)


def rep_row(p, cm, psum_ap, row_chunk, tag="rep"):
    p.mm(psum_ap, cm.ones_row[:, :], row_chunk, tag=tag)


def build_l0_attn(p: Prog, cm: Common, x_hbm, W, out_hbm, Tloc=4096, nseq=2):
    T = Tloc // nseq
    P = cm.P
    ln1_g = load_vec_cols(p, "l0_ln1g", W["ln1_g"])
    ln1_b = load_vec_cols(p, "l0_ln1b", W["ln1_b"])
    ln2_g = load_vec_cols(p, "l0_ln2g", W["ln2_g"])
    ln2_b = load_vec_cols(p, "l0_ln2b", W["ln2_b"])
    mixk = load_vec_cols(p, "l0_mixk", W["mix_k"])
    mixv = load_vec_cols(p, "l0_mixv", W["mix_v"])
    mixr = load_vec_cols(p, "l0_mixr", W["mix_r"])
    td = load_vec_cols(p, "l0_td", W["time_decay"])
    tf = load_vec_cols(p, "l0_tf", W["time_first"])
    ewt = p.sb("l0_ew", (128, NI), F32)
    etft = p.sb("l0_etf", (128, NI), F32)
    for i in range(NI):
        p.act(ewt[:, i:i + 1], td[i], AF.Exp, tag="ew1")
        p.act(etft[:, i:i + 1], tf[i], AF.Exp, tag="etf")
    p.act(ewt[:, :], ewt[:, :], AF.Exp, scale=-1.0, tag="ew2")
    ew = [ewt[:, i:i + 1] for i in range(NI)]
    etf = [etft[:, i:i + 1] for i in range(NI)]

    xn_hbm = p.dscratch("l0_xn", (D, T), F32)
    wkv_hbm = p.dscratch("l0_wkv", (D, T), BF16)
    ek_hbm = p.dscratch("l0_ek", (D, T), F32)
    a_hbm = p.dscratch("l0_a", (D, T), F32)

    for s in range(nseq):
        xcols = x_hbm[:, s * T:(s + 1) * T]
        ocols = out_hbm[:, s * T:(s + 1) * T]
        sfx = f"s{s}"
        tmp32 = [p.sb(f"l0tmp{k}_{sfx}", (128, TC), F32) for k in range(4)]

        xtile = [p.sb(f"l0x{i}_{sfx}", (128, T), F32) for i in range(NI)]
        for i in range(NI):
            p.dma("sync", xtile[i][:, :], xcols[i * 128:(i + 1) * 128, :], tag="ldx")
        rowsT, rows = ln_rows(p, cm, xtile, T, f"l0ln1_{sfx}", inner=True)
        xn2 = [p.sb(f"l0xn2_{i}_{sfx}", (128, T + 1), BF16) for i in range(NI)]
        for i in range(NI):
            p.memset("vector", xn2[i][:, 0:1], 0.0, tag="xn2pad")
        for (t0, tl) in tchunks(T):
            rep_row(p, cm, P[2][:, :tl], rows["ra"][:, t0:t0 + tl])
            rep_row(p, cm, P[3][:, :tl], rows["rb"][:, t0:t0 + tl])
            rep_row(p, cm, P[4][:, :tl], rows["r2"][:, t0:t0 + tl])
            for i in range(NI):
                xn_c = tmp32[2]
                p.tt("vector", tmp32[3][:, :tl], xtile[i][:, t0:t0 + tl], P[2][:, :tl], ALU.mult)
                p.tt("vector", xn_c[:, :tl], tmp32[3][:, :tl], P[3][:, :tl], ALU.add)
                p.dma("scalar", xn_hbm[i * 128:(i + 1) * 128, t0:t0 + tl],
                      xn_c[:, :tl], tag="sp_xn")
                p.tt("vector", tmp32[3][:, :tl], xn_c[:, :tl], P[4][:, :tl], ALU.mult)
                p.ts("vector", xn2[i][:, t0 + 1:t0 + 1 + tl], tmp32[3][:, :tl],
                     ln1_g[i], ALU.mult, ln1_b[i], ALU.add, tag="xn2")
        p.free(xtile, rowsT)

        mix = [p.sb(f"l0mix{i}_{sfx}", (128, T), BF16) for i in range(NI)]

        def make_mix(mvec):
            for i in range(NI):
                cur = xn2[i][:, 1:T + 1]
                prv = xn2[i][:, 0:T]
                p.tt("vector", mix[i][:, :], cur, prv, ALU.subtract, tag="mixd")
                p.stt("vector", mix[i][:, :], mix[i][:, :], mvec[i], prv,
                      ALU.mult, ALU.add, tag="mixm")

        def proj(Wt, evac):
            for (t0, tl) in tchunks(T):
                for o in range(NI):
                    ps = P[o % 2]
                    p.mm_acc(ps[:, :tl],
                             [(Wt[i][:, o * 128:(o + 1) * 128], mix[i][:, t0:t0 + tl])
                              for i in range(NI)], tag="proj")
                    evac(o, ps[:, :tl], t0, tl)

        Wk = load_w_bf16(p, f"l0Wk_{sfx}", W["Wk"])
        make_mix(mixk)
        def evac_k(o, ps, t0, tl):
            sc = tmp32[o % 2]
            p.act(sc[:, :tl], ps, AF.Exp, tag="ek")
            p.dma("scalar", ek_hbm[o * 128:(o + 1) * 128, t0:t0 + tl], sc[:, :tl], tag="sp_ek")
        proj(Wk, evac_k)
        p.free(Wk)

        vfull = [p.sb(f"l0v{i}_{sfx}", (128, T), F32) for i in range(NI)]
        Wv = load_w_bf16(p, f"l0Wv_{sfx}", W["Wv"])
        make_mix(mixv)
        def evac_v(o, ps, t0, tl):
            p.copy("vector", vfull[o][:, t0:t0 + tl], ps, tag="evv")
        proj(Wv, evac_v)
        p.free(Wv)

        rrg = [p.sb(f"l0rr{i}_{sfx}", (128, T), BF16) for i in range(NI)]
        Wr = load_w_bf16(p, f"l0Wr_{sfx}", W["Wr"])
        make_mix(mixr)
        def evac_r(o, ps, t0, tl):
            p.act(rrg[o][:, t0:t0 + tl], ps, AF.Sigmoid, tag="sigr")
        proj(Wr, evac_r)
        p.free(Wr, xn2, mix)

        wkv = [p.sb(f"l0wkv{i}_{sfx}", (128, T), BF16) for i in range(NI)]
        ektile = [p.sb(f"l0ekt{k}_{sfx}", (128, T), F32) for k in range(2)]
        ptile = [p.sb(f"l0pt{k}_{sfx}", (128, T), F32) for k in range(2)]
        abtile = [p.sb(f"l0ab{k}_{sfx}", (128, T), F32) for k in range(2)]
        for i in range(NI):
            ekt = ektile[i % 2]; pt = ptile[i % 2]; at = abtile[i % 2]
            p.dma("sync", ekt[:, :], ek_hbm[i * 128:(i + 1) * 128, :], tag="ld_ek")
            p.tt("vector", pt[:, :], ekt[:, :], vfull[i][:, :], ALU.mult, tag="pkv")
            p.ts("vector", pt[:, 0:1], pt[:, 0:1], etf[i], ALU.mult, tag="p0")
            p.ts("vector", ekt[:, 0:1], ekt[:, 0:1], etf[i], ALU.mult, tag="e0")
            ewb = ew[i].broadcast_to([128, T])
            p.scan(at[:, :], ewb, pt[:, :], 0.0, ALU.mult, ALU.add, tag="scanA")
            p.scan(pt[:, :], ewb, ekt[:, :], 0.0, ALU.mult, ALU.add, tag="scanB")
            p.ts("vector", pt[:, :], pt[:, :], 1e-8, ALU.add, tag="beps")
            p.recip(pt[:, :], pt[:, :], tag="brec")
            p.tt("vector", at[:, :], at[:, :], pt[:, :], ALU.mult, tag="wkv")
            p.tt("vector", wkv[i][:, :], at[:, :], rrg[i][:, :], ALU.mult, tag="rwkv")
            DBG and p.dma("scalar", wkv_hbm[i * 128:(i + 1) * 128, :], wkv[i][:, :], tag="dbg_wkv")
        p.free(vfull, rrg, ektile, ptile, abtile)

        afm = [p.sb(f"l0a{i}_{sfx}", (128, T), F32) for i in range(NI)]
        Wo = load_w_bf16(p, f"l0Wo_{sfx}", W["Wo"])
        for (t0, tl) in tchunks(T):
            for o in range(NI):
                ps = P[o % 2]
                p.mm_acc(ps[:, :tl],
                         [(Wo[i][:, o * 128:(o + 1) * 128], wkv[i][:, t0:t0 + tl])
                          for i in range(NI)], tag="attnWo")
                xn_c = tmp32[2 + (o % 2)]
                p.dma("sync", xn_c[:, :tl], xn_hbm[o * 128:(o + 1) * 128, t0:t0 + tl],
                      tag="ld_xn")
                p.tt("vector", afm[o][:, t0:t0 + tl], ps[:, :tl], xn_c[:, :tl], ALU.add, tag="ares")
        p.free(Wo, wkv)

        rowsT2, rows2 = ln_rows(p, cm, afm, T, f"l0ln2_{sfx}", inner=False)
        yn = [p.sb(f"l0yn{i}_{sfx}", (128, T), BF16) for i in range(NI)]
        for (t0, tl) in tchunks(T):
            rep_row(p, cm, P[2][:, :tl], rows2["ra"][:, t0:t0 + tl])
            rep_row(p, cm, P[3][:, :tl], rows2["rb"][:, t0:t0 + tl])
            for i in range(NI):
                p.tt("vector", tmp32[3][:, :tl], afm[i][:, t0:t0 + tl], P[2][:, :tl], ALU.mult)
                p.tt("vector", tmp32[2][:, :tl], tmp32[3][:, :tl], P[3][:, :tl], ALU.add)
                p.ts("vector", yn[i][:, t0:t0 + tl], tmp32[2][:, :tl],
                     ln2_g[i], ALU.mult, ln2_b[i], ALU.add, tag="yn")
        for i in range(NI):
            p.dma("scalar", a_hbm[i * 128:(i + 1) * 128, :], afm[i][:, :], tag="sp_a")
        p.free(afm, rowsT2)

        rr_hbm = p.dscratch(f"l0_rr_{sfx}", (D, T), BF16)
        if s == 0:
            kk_hbm = p.dscratch("l0_kk", (4096, T), BF16) if DBG else None
            yn_hbm = p.dscratch("l0_yn", (D, T), BF16) if DBG else None
        Wfr = load_w_bf16(p, f"l0Wfr_{sfx}", W["Wfr"])
        sc16 = [p.sb(f"l0sc16{k}_{sfx}", (128, TC), BF16) for k in range(2)]
        for (t0, tl) in tchunks(T):
            for o in range(NI):
                ps = P[o % 2]
                p.mm_acc(ps[:, :tl],
                         [(Wfr[i][:, o * 128:(o + 1) * 128], yn[i][:, t0:t0 + tl])
                          for i in range(NI)], tag="Wfr")
                sc = sc16[o % 2]
                p.act(sc[:, :tl], ps[:, :tl], AF.Sigmoid, tag="rr")
                p.dma("scalar", rr_hbm[o * 128:(o + 1) * 128, t0:t0 + tl],
                      sc[:, :tl], tag="sp_rr")
        p.free(Wfr, sc16)

        F = 4096
        Wfv = []
        for ic in range(F // 128):
            t = p.sb(f"l0wfv{ic}_{sfx}", (128, D), BF16)
            p.dma("gpsimd", t[:, :], W["Wfv"][ic * 128:(ic + 1) * 128, :], tag="ldWfv")
            Wfv.append(t)
        TH = T // 2
        for th in range(2):
            h0 = th * TH
            if th == 0 and DBG:
                for i in range(NI):
                    p.dma("scalar", yn_hbm[i * 128:(i + 1) * 128, :],
                          yn[i][:, :], tag="dbg_yn")
            zt = [p.sb(f"l0z{i}_{sfx}h{th}", (128, TH), BF16) for i in range(NI)]
            for i in range(NI):
                if h0 == 0:
                    p.copy("vector", zt[i][:, 0:1], yn[i][:, 0:1], tag="z0")
                    p.tt("vector", zt[i][:, 1:TH], yn[i][:, 1:TH],
                         yn[i][:, 0:TH - 1], ALU.add, tag="zadd")
                else:
                    p.tt("vector", zt[i][:, :], yn[i][:, h0:h0 + TH],
                         yn[i][:, h0 - 1:h0 + TH - 1], ALU.add, tag="zadd")
            kk2 = [p.sb(f"l0kk{i}_{sfx}h{th}", (128, TH), BF16) for i in range(F // 128)]
            wfk_buf = [p.sb(f"l0wfk{k}_{sfx}h{th}", (128, NI * 128), BF16) for k in range(2)]
            for o in range(F // 128):
                wb = wfk_buf[o % 2]
                p.dma("gpsimd", wb[:, :].rearrange("p (i n) -> p i n", i=NI),
                      W["Wfk"][:, o * 128:(o + 1) * 128].rearrange("(i p) n -> p i n", p=128),
                      tag="ldWfk")
                for (t0, tl) in tchunks(TH):
                    ps = P[2 + o % 2]
                    p.mm_acc(ps[:, :tl],
                             [(wb[:, i * 128:(i + 1) * 128], zt[i][:, t0:t0 + tl])
                              for i in range(NI)], tag="Wfk")
                    sc = tmp32[o % 2]
                    p.act(sc[:, :tl], ps[:, :tl], AF.Relu, tag="kkrelu")
                    p.act(kk2[o][:, t0:t0 + tl], sc[:, :tl], AF.Square, scale=0.5, tag="kksq")
                    DBG and p.dma("scalar", kk_hbm[o * 128:(o + 1) * 128, h0 + t0:h0 + t0 + tl],
                          kk2[o][:, t0:t0 + tl], tag="dbg_kk")
            p.free(wfk_buf)
            rrb = [p.sb(f"l0rrb{k}_{sfx}h{th}", (128, TC), BF16) for k in range(2)]
            for (t0, tl) in tchunks(TH):
                gt0 = h0 + t0
                for o in range(NI):
                    ps = P[o % 2]
                    p.mm_acc(ps[:, :tl],
                             [(Wfv[ic][:, o * 128:(o + 1) * 128], kk2[ic][:, t0:t0 + tl])
                              for ic in range(F // 128)], tag="Wfv")
                    a_c = tmp32[2 + o % 2]
                    p.dma("sync", a_c[:, :tl], a_hbm[o * 128:(o + 1) * 128, gt0:gt0 + tl],
                          tag="ld_a")
                    rc = rrb[o % 2]
                    p.dma("sync", rc[:, :tl], rr_hbm[o * 128:(o + 1) * 128, gt0:gt0 + tl],
                          tag="ld_rr")
                    p.tt("vector", tmp32[o % 2][:, :tl], ps[:, :tl], rc[:, :tl],
                         ALU.mult, tag="gatefv")
                    p.tt("vector", tmp32[o % 2][:, :tl], tmp32[o % 2][:, :tl], a_c[:, :tl],
                         ALU.add, tag="outadd")
                    p.dma("sync", ocols[o * 128:(o + 1) * 128, gt0:gt0 + tl],
                          tmp32[o % 2][:, :tl], tag="st_out")
            p.free(kk2, zt, rrb)
        p.free(Wfv, yn, tmp32)
    return dict(xn=xn_hbm, ek=ek_hbm, a=a_hbm, wkv=wkv_hbm, rr=rr_hbm,
                kk=kk_hbm, yn=yn_hbm)


def ln_apply_bf16(p, cm, xtiles, T, name, out_tiles=None, inner=False,
                  g=None, b=None):
    P = cm.P
    rowsT, rows = ln_rows(p, cm, xtiles, T, name, inner=inner)
    if out_tiles is None:
        out_tiles = [p.sb(f"{name}_xn{i}", (128, T), BF16) for i in range(NI)]
    t32 = [p.sb(p.uname(f"{name}_t"), (128, TC), F32) for _ in range(2)]
    for (t0, tl) in tchunks(T):
        rep_row(p, cm, P[2][:, :tl], rows["ra"][:, t0:t0 + tl])
        rep_row(p, cm, P[3][:, :tl], rows["rb"][:, t0:t0 + tl])
        if inner:
            rep_row(p, cm, P[4][:, :tl], rows["r2"][:, t0:t0 + tl])
        for i in range(NI):
            p.tt("vector", t32[0][:, :tl], xtiles[i][:, t0:t0 + tl], P[2][:, :tl], ALU.mult)
            p.tt("vector", t32[1][:, :tl], t32[0][:, :tl], P[3][:, :tl], ALU.add)
            src = t32[1]
            if inner:
                p.tt("vector", t32[0][:, :tl], src[:, :tl], P[4][:, :tl], ALU.mult)
                src = t32[0]
            if g is not None:
                p.ts("vector", out_tiles[i][:, t0:t0 + tl], src[:, :tl],
                     g[i], ALU.mult, b[i], ALU.add, tag="lnaff")
            else:
                p.copy("vector", out_tiles[i][:, t0:t0 + tl], src[:, :tl], tag="lncp")
    p.free(rowsT, *t32)
    return out_tiles


def build_l2_ffn(p: Prog, cm: Common, x_hbm, a_hbm, W, out_hbm, Tloc=4096):
    T = Tloc
    P = cm.P
    import numpy as _np
    dg = _np.arange(512); be = _np.arange(512)
    Pm = (dg[:, None] // 16 == _np.arange(32)[None, :]).astype(ml_dtypes.bfloat16)
    P2m = (dg[:, None] % 16 == _np.arange(16)[None, :]).astype(ml_dtypes.bfloat16)
    Qm = (_np.arange(32)[:, None] == be[None, :] // 16).astype(ml_dtypes.bfloat16)
    Q2m = (_np.arange(16)[:, None] == be[None, :] % 16).astype(ml_dtypes.bfloat16)
    PT_h = p.const("l2f_PT", _np.ascontiguousarray(Pm.T))
    P2T_h = p.const("l2f_P2T", _np.ascontiguousarray(P2m.T))
    Q_h = p.const("l2f_Q", Qm)
    Q2_h = p.const("l2f_Q2", Q2m)
    PT = p.sb("l2f_PT", (32, 512), BF16)
    P2T = p.sb("l2f_P2T", (16, 512), BF16)
    Qt = p.sb("l2f_Q", (32, 512), BF16)
    Q2t = p.sb("l2f_Q2", (16, 512), BF16)
    for t, h in [(PT, PT_h), (P2T, P2T_h), (Qt, Q_h), (Q2t, Q2T_h if False else Q2_h)]:
        p.dma("sync", t[:, :], h)
    Gab = p.sb("l2f_gab", (32, 2 * 4 * 32), BF16)
    Geb = p.sb("l2f_geb", (16, 2 * 4 * 16), BF16)
    for ri, (g1, g2) in enumerate([(W["G1_R"], W["G2_R"]), (W["G1_I"], W["G2_I"])]):
        p.dma("gpsimd", Gab[:, ri * 128:(ri + 1) * 128].rearrange("b (c d) -> b c d", c=4),
              g1.rearrange("o b c d -> b c (o d)"), tag="ldG1")
        p.dma("gpsimd", Geb[:, ri * 64:(ri + 1) * 64].rearrange("e (c g) -> e c g", c=4),
              g2.rearrange("c e o g -> e c (o g)"), tag="ldG2")
    m1s = p.sb("l2f_m1", (32, 512), BF16)
    m2s = p.sb("l2f_m2", (16, 512), BF16)
    wacc = [p.sb(f"l2f_wacc{k}", (128, 512), F32) for k in range(4)]
    uv = p.sb("l2f_uv", (128, 512), F32)
    usb = p.sb("l2f_usb", (128, 512), F32)
    WT = {}
    for ri, nm in enumerate(["R", "I"]):
        wt = [p.sb(f"l2f_WT{nm}{k}", (128, 512), BF16) for k in range(4)]
        wtn = [p.sb(f"l2f_WTn{nm}{k}", (128, 512), BF16) for k in range(4)] if nm == "I" else None
        for c in range(4):
            A = Gab[:, ri * 128 + c * 32: ri * 128 + (c + 1) * 32]
            B = Geb[:, ri * 64 + c * 16: ri * 64 + (c + 1) * 16]
            p.mm(P[5][0:32, :], A, Qt[:, :], tag="l2fM1")
            p.copy("vector", m1s[:, :], P[5][0:32, :], tag="l2fM1e")
            p.mm(P[5][0:16, :], B, Q2t[:, :], tag="l2fM2")
            p.copy("vector", m2s[:, :], P[5][0:16, :], tag="l2fM2e")
            for rc in range(4):
                p.mm(P[rc % 2][:, :], PT[:, rc * 128:(rc + 1) * 128], m1s[:, :], tag="l2fU")
                p.mm(P[2 + rc % 2][:, :], P2T[:, rc * 128:(rc + 1) * 128], m2s[:, :], tag="l2fV")
                p.copy("scalar", usb[:, :], P[rc % 2][:, :], tag="l2fUe")
                if c == 0:
                    p.tt("vector", wacc[rc][:, :], usb[:, :], P[2 + rc % 2][:, :],
                         ALU.mult, tag="l2fW0")
                else:
                    p.tt("vector", uv[:, :], usb[:, :], P[2 + rc % 2][:, :],
                         ALU.mult, tag="l2fUV")
                    p.tt("vector", wacc[rc][:, :], wacc[rc][:, :], uv[:, :], ALU.add,
                         tag="l2fWa")
        for rc in range(4):
            p.copy("vector", wt[rc][:, :], wacc[rc][:, :], tag="l2fcast")
            if nm == "I":
                p.ts("vector", wtn[rc][:, :], wacc[rc][:, :], -1.0, ALU.mult, tag="l2fneg")
        WT[nm] = wt
        if nm == "I":
            WT["In"] = wtn
    br = load_vec_cols(p, "l2f_br", W["bias_real"], n=512)
    bi = load_vec_cols(p, "l2f_bi", W["bias_imag"], n=512)

    atile = [p.sb(f"l2fa{i}", (128, TC), F32) for i in range(2)]
    xful = [p.sb(f"l2fxf{i}", (128, T // 2), F32) for i in range(NI)]
    for half in range(2):
        h0 = half * (T // 2)
        TH = T // 2
        for i in range(NI):
            p.dma("sync", xful[i][:, :], x_hbm[i * 128:(i + 1) * 128, h0:h0 + TH], tag="l2f_ldx")
        xn = ln_apply_bf16(p, cm, xful, TH, f"l2fln_h{half}")
        for (t0, tl) in tchunks(TH):
            gt0 = h0 + t0
            for o in range(8):
                ps = P[o % 2]
                oc = o % 4
                if o < 4:
                    pairs = [(WT["R"][i][:, oc * 128:(oc + 1) * 128], xn[i][:, t0:t0 + tl])
                             for i in range(4)]
                    pairs += [(WT["In"][i][:, oc * 128:(oc + 1) * 128], xn[4 + i][:, t0:t0 + tl])
                              for i in range(4)]
                    bias = br[oc]
                else:
                    pairs = [(WT["I"][i][:, oc * 128:(oc + 1) * 128], xn[i][:, t0:t0 + tl])
                             for i in range(4)]
                    pairs += [(WT["R"][i][:, oc * 128:(oc + 1) * 128], xn[4 + i][:, t0:t0 + tl])
                              for i in range(4)]
                    bias = bi[oc]
                p.mm_acc(ps[:, :tl], pairs, tag="l2fmm")
                a_c = atile[o % 2]
                p.dma("scalar", a_c[:, :tl], a_hbm[o * 128:(o + 1) * 128, gt0:gt0 + tl],
                      tag="l2f_lda")
                p.ts("vector", uv[:, :tl], ps[:, :tl], bias, ALU.add, tag="l2fbias")
                p.tt("vector", uv[:, :tl], uv[:, :tl], xful[o][:, t0:t0 + tl], ALU.add, tag="l2faddx")
                p.tt("vector", uv[:, :tl], uv[:, :tl], a_c[:, :tl], ALU.add, tag="l2fadda")
                p.dma("sync", out_hbm[o * 128:(o + 1) * 128, gt0:gt0 + tl], uv[:, :tl],
                      tag="l2f_st")
        p.free(*xn)
    p.free(xful, atile, uv, usb, wacc, m1s, m2s)


def gelu_tanh(p, out_ap, in_ap, sc, eng="vector"):
    p.tt(eng, sc[0], in_ap, in_ap, ALU.mult, tag="gt_x2")
    p.tt(eng, sc[0], sc[0], in_ap, ALU.mult, tag="gt_x3")
    p.stt(eng, sc[0], sc[0], 0.044715, in_ap, ALU.mult, ALU.add, tag="gt_u")
    p.act(sc[1], sc[0], AF.Tanh, scale=0.7978845608, tag="gt_tanh")
    p.ts(eng, sc[1], sc[1], 1.0, ALU.add, tag="gt_1p")
    p.tt(eng, sc[1], sc[1], in_ap, ALU.mult, tag="gt_mx")
    p.ts(eng, out_ap, sc[1], 0.5, ALU.mult, tag="gt_half")


def build_l1_ffn(p: Prog, cm: Common, x_hbm, a_hbm, W, out_hbm, Tloc=4096,
                 n_cores=8):
    T = Tloc
    P = cm.P
    NC = 256
    NL = 32
    E = 8
    CAP = 640
    BIG = 1.0e6

    gb_row = p.sb("l1f_gbrow", (1, E), F32)
    p.dma("sync", gb_row[:, :], W["gate_b"].rearrange("(o e) -> o e", o=1), tag="l1f_gb")
    gw = p.sb("l1f_gw", (128, NI * E), BF16)
    p.dma("gpsimd", gw[:, :].rearrange("p (i e) -> p i e", i=NI),
          W["gate_w"].rearrange("(i p) e -> p i e", p=128), tag="l1f_gw")
    gb16 = p.sb("l1f_gb16", (1, E), BF16)
    p.copy("vector", gb16[:, :], gb_row[:, :], tag="l1f_gbc")
    ltm = p.sb("l1f_ltm", (128, NL * E), F32)
    xn_fm = p.dscratch("l1f_xnfm", (D, T), BF16)
    TH = T // 2
    for hf in range(2):
        h0 = hf * TH
        xful = [p.sb(f"l1fx{i}_h{hf}", (128, TH), F32) for i in range(NI)]
        for i in range(NI):
            p.dma("sync", xful[i][:, :], x_hbm[i * 128:(i + 1) * 128, h0:h0 + TH],
                  tag="l1f_ldx")
        xn = ln_apply_bf16(p, cm, xful, TH, f"l1fln_h{hf}")
        for i in range(NI):
            p.dma("scalar", xn_fm[i * 128:(i + 1) * 128, h0:h0 + TH], xn[i][:, :],
                  tag="l1f_spxn")
        for cl in range(NL // 2):
            c = hf * (NL // 2) + cl
            pairs = [(xn[i][:, cl * 128:(cl + 1) * 128], gw[:, i * E:(i + 1) * E])
                     for i in range(NI)]
            pairs.append((cm.ones_row16[0:1, :], gb16[:, :]))
            p.mm_acc(P[c % 2][:, :E], pairs, tag="l1f_lmm")
            p.copy("vector", ltm[:, c * E:(c + 1) * E], P[c % 2][:, :E], tag="l1f_le")
        p.free(xful, xn)
    xn_tm = [p.sb(f"l1f_tm{c}", (128, D), BF16) for c in range(2)]
    xn_tm_img = p.dscratch("l1f_xntm", (T, D), BF16)
    for c in range(NL):
        buf = xn_tm[c % 2]
        p.dma_transpose("scalar", buf[:, :], xn_fm[:, c * 128:(c + 1) * 128], tag="l1f_tr")
        p.dma("sync", xn_tm_img[c * 128:(c + 1) * 128, :], buf[:, :], tag="l1f_sttm")
    cc_in = p.dscratch("l1f_ccin", (T, E), F32)
    p.dma("sync", cc_in.rearrange("(c q) e -> q c e", q=128),
          ltm[:, :].rearrange("q (c e) -> q c e", c=NL), tag="l1f_stl")
    cc_out = p.dscratch("l1f_ccout", (n_cores * T, E), F32)
    p.collective("AllGather", ALU.bypass, [cc_in], [cc_out], n_cores=n_cores)

    lall = p.sb("l1f_lall", (128, NC * E), F32)
    p.dma("sync", lall[:, :].rearrange("q (c e) -> q c e", c=NC),
          cc_out.rearrange("(c q) e -> q c e", q=128), tag="l1f_ldall")
    maxe = p.sb("l1f_maxe", (128, NC), F32)
    p.reduce("vector", maxe[:, :], lall[:, :].rearrange("q (c e) -> q c e", c=NC),
             ALU.max, tag="l1f_max")
    oh = p.sb("l1f_oh", (128, NC * E), BF16)
    ohf = p.sb("l1f_ohf", (128, NC * E), F32)
    mb = maxe[:, :].rearrange("q (c o) -> q c o", o=1).broadcast_to([128, NC, E])
    p.tt("vector", ohf[:, :].rearrange("q (c e) -> q c e", c=NC),
         lall[:, :].rearrange("q (c e) -> q c e", c=NC), mb, ALU.is_ge, tag="l1f_oh")
    iote = p.const("l1f_iote", np.tile(np.arange(E, dtype=np.float32), (128, NC)))
    iott = p.sb("l1f_iott", (128, NC * E), F32)
    p.dma("sync", iott[:, :], iote, tag="l1f_iot")
    cand = p.sb("l1f_cand", (128, NC * E), F32)
    p.ts("vector", cand[:, :], iott[:, :], -BIG, ALU.add, tag="l1f_c1")
    p.tt("vector", cand[:, :], cand[:, :], ohf[:, :], ALU.mult, tag="l1f_c2")
    p.ts("vector", cand[:, :], cand[:, :], BIG, ALU.add, tag="l1f_c3")
    eidx = p.sb("l1f_eidx", (128, NC), F32)
    p.reduce("vector", eidx[:, :], cand[:, :].rearrange("q (c e) -> q c e", c=NC),
             ALU.min, tag="l1f_eidx")
    eb = eidx[:, :].rearrange("q (c o) -> q c o", o=1).broadcast_to([128, NC, E])
    p.tt("vector", ohf[:, :].rearrange("q (c e) -> q c e", c=NC),
         iott[:, :].rearrange("q (c e) -> q c e", c=NC), eb, ALU.is_equal, tag="l1f_oh2")
    p.copy("vector", oh[:, :], ohf[:, :], tag="l1f_ohc")
    triexc = p.sb("l1f_tri", (128, 128), BF16)
    tri_np = (np.arange(128)[:, None] < np.arange(128)[None, :]).astype(ml_dtypes.bfloat16)
    p.dma("sync", triexc[:, :], p.const("l1f_trih", tri_np), tag="l1f_trid")
    triexcf = p.sb("l1f_trif", (128, 128), F32)
    p.dma("sync", triexcf[:, :], p.const("l1f_trihf", tri_np.astype(np.float32)), tag="l1f_tridf")
    cntrow = p.sb("l1f_cnt", (1, NC * E), F32)
    for (t0, tl) in tchunks(NC * E):
        p.mm(P[5][0:1, :tl], cm.ones_col16[:, :], oh[:, t0:t0 + tl], tag="l1f_cmm")
        p.copy("vector", cntrow[:, t0:t0 + tl], P[5][0:1, :tl], tag="l1f_ce")
    cnt_img = p.dscratch("l1f_cntimg", (NC, E), F32)
    p.dma("sync", cnt_img.rearrange("(o c) e -> o (c e)", o=1), cntrow[:, :], tag="l1f_cst")
    cnt_t = [p.sb(f"l1f_cnt{k}", (128, E), F32) for k in range(2)]
    for k in range(2):
        p.dma("sync", cnt_t[k][:, :], cnt_img[k * 128:(k + 1) * 128, :], tag="l1f_cld")
    pref_t = [p.sb(f"l1f_pref{k}", (128, E), F32) for k in range(2)]
    p.mm(P[4][:, :E], triexcf[:, :], cnt_t[0][:, :], tag="l1f_p0")
    p.copy("vector", pref_t[0][:, :], P[4][:, :E], tag="l1f_p0e")
    p.mm(P[4][:, :E], triexcf[:, :], cnt_t[1][:, :], tag="l1f_p1")
    p.mm(P[5][0:1, :E], cm.ones_col[:, :], cnt_t[0][:, :], tag="l1f_t0")
    tot0 = p.sb("l1f_tot0", (1, E), F32)
    p.copy("vector", tot0[:, :], P[5][0:1, :E], tag="l1f_t0e")
    p.mm(P[6][:, :E], cm.ones_row[:, :], tot0[:, :], tag="l1f_t0r")
    p.copy("vector", pref_t[1][:, :], P[4][:, :E], tag="l1f_p1e")
    p.tt("vector", pref_t[1][:, :], pref_t[1][:, :], P[6][:, :E], ALU.add, tag="l1f_p1a")
    pref_img = p.dscratch("l1f_prefimg", (NC, E), F32)
    for k in range(2):
        p.dma("sync", pref_img[k * 128:(k + 1) * 128, :], pref_t[k][:, :], tag="l1f_pst")
    p.free(lall, maxe, oh, ohf, cand)
    rowsel = p.sb("l1f_rowsel", (8, 1), I32)
    p.dma("sync", rowsel[:, :], W["rowsel"], tag="l1f_rsel")
    rko8 = p.sb("l1f_rko8", (8, E), F32)
    p.indirect(rko8[:, :], None, pref_img, IndirectOffsetOnAxis(rowsel[:, 0:1], 0),
               extra_reads=[rowsel], bounds_check=NC - 1, tag="l1f_rko")
    rko = rko8[0:1, :]
    lexp = p.sb("l1f_lex", (128, NL * E), F32)
    lmax = p.sb("l1f_lmax", (128, NL), F32)
    p.reduce("vector", lmax[:, :], ltm[:, :].rearrange("q (c e) -> q c e", c=NL),
             ALU.max, tag="l1f_lm")
    lohf = p.sb("l1f_lohf", (128, NL * E), F32)
    lcand = p.sb("l1f_lcand", (128, NL * E), F32)
    mbl = lmax[:, :].rearrange("q (c o) -> q c o", o=1).broadcast_to([128, NL, E])
    p.tt("vector", lohf[:, :].rearrange("q (c e) -> q c e", c=NL),
         ltm[:, :].rearrange("q (c e) -> q c e", c=NL), mbl, ALU.is_ge, tag="l1f_loh")
    p.ts("vector", lcand[:, :], iott[:, 0:NL * E], -BIG, ALU.add, tag="l1f_lc1")
    p.tt("vector", lcand[:, :], lcand[:, :], lohf[:, :], ALU.mult, tag="l1f_lc2")
    p.ts("vector", lcand[:, :], lcand[:, :], BIG, ALU.add, tag="l1f_lc3")
    leidx = p.sb("l1f_leidx", (128, NL), F32)
    p.reduce("vector", leidx[:, :], lcand[:, :].rearrange("q (c e) -> q c e", c=NL),
             ALU.min, tag="l1f_leidx")
    ebl = leidx[:, :].rearrange("q (c o) -> q c o", o=1).broadcast_to([128, NL, E])
    p.tt("vector", lohf[:, :].rearrange("q (c e) -> q c e", c=NL),
         iott[:, 0:NL * E].rearrange("q (c e) -> q c e", c=NL), ebl, ALU.is_equal,
         tag="l1f_loh2")
    loh16 = p.sb("l1f_loh16", (128, NL * E), BF16)
    p.copy("vector", loh16[:, :], lohf[:, :], tag="l1f_lohc")
    lB = p.sb("l1f_lB", (128, NL * E), F32)
    p.mm(P[4][:, :NL * E], triexc[:, :], loh16[:, :], tag="l1f_lB")
    p.copy("scalar", lB[:, :], P[4][:, :NL * E], tag="l1f_lBe")
    lcnt = p.sb("l1f_lcnt", (1, NL * E), F32)
    p.mm(P[5][0:1, :NL * E], cm.ones_col16[:, :], loh16[:, :], tag="l1f_lcnt")
    p.copy("vector", lcnt[:, :], P[5][0:1, :NL * E], tag="l1f_lcnte")
    lcnt_img = p.dscratch("l1f_lcntimg", (NL, E), F32)
    p.dma("sync", lcnt_img.rearrange("(o c) e -> o (c e)", o=1), lcnt[:, :], tag="l1f_lcst")
    lcnt_t = p.sb("l1f_lcntt", (NL, E), F32)
    p.dma("sync", lcnt_t[:, :], lcnt_img, tag="l1f_lcld")
    lpref = p.sb("l1f_lpref", (NL, E), F32)
    p.mm(P[4][0:NL, :E], triexcf[0:NL, 0:NL], lcnt_t[:, :], tag="l1f_lpm")
    p.copy("vector", lpref[:, :], P[4][0:NL, :E], tag="l1f_lpe")
    p.mm(P[6][0:NL, :E], cm.ones_row[:, 0:NL], rko,
         tag="l1f_rkrep")
    p.tt("vector", lpref[:, :], lpref[:, :], P[6][0:NL, :E], ALU.add, tag="l1f_base")
    base_img = p.dscratch("l1f_baseimg", (NL, E), F32)
    p.dma("sync", base_img, lpref[:, :], tag="l1f_bst")
    baserow = p.sb("l1f_baserow", (1, NL * E), F32)
    p.dma("sync", baserow[:, :], base_img.rearrange("(o c) e -> o (c e)", o=1), tag="l1f_bld")
    basrep = p.sb("l1f_basrep", (128, NL * E), F32)
    for (t0, tl) in tchunks(NL * E):
        p.mm(P[4][:, :tl], cm.ones_row[:, :], baserow[:, t0:t0 + tl], tag="l1f_brep")
        p.copy("scalar", basrep[:, t0:t0 + tl], P[4][:, :tl], tag="l1f_brepe")
    p.tt("vector", lB[:, :], lB[:, :], basrep[:, :], ALU.add, tag="l1f_pos3")
    p.tt("vector", lB[:, :], lB[:, :], lohf[:, :], ALU.mult, tag="l1f_pos3m")
    lpos = p.sb("l1f_lpos", (128, NL), F32)
    p.reduce("vector", lpos[:, :], lB[:, :].rearrange("q (c e) -> q c e", c=NL),
             ALU.add, tag="l1f_pos")
    lslot = p.sb("l1f_lslot", (128, NL), F32)
    keep = p.sb("l1f_keep", (128, NL), F32)
    p.ts("vector", keep[:, :], lpos[:, :], float(CAP), ALU.is_ge, tag="l1f_keep")
    p.ts("vector", lslot[:, :], leidx[:, :], float(CAP), ALU.mult, tag="l1f_s1")
    p.tt("vector", lslot[:, :], lslot[:, :], lpos[:, :], ALU.add, tag="l1f_s2")
    p.stt("vector", lslot[:, :], keep[:, :], BIG, lslot[:, :], ALU.mult, ALU.add,
          tag="l1f_s3")
    lslot_i = p.sb("l1f_lsloti", (128, NL), I32)
    p.copy("vector", lslot_i[:, :], lslot[:, :], tag="l1f_si")
    lgate = p.sb("l1f_lgate", (128, NL), F32)
    lsum = p.sb("l1f_lsum", (128, NL), F32)
    p.act(lexp[:, :], ltm[:, :], AF.Exp, tag="l1f_lexp")
    p.reduce("vector", lsum[:, :], lexp[:, :].rearrange("q (c e) -> q c e", c=NL),
             ALU.add, tag="l1f_lsm")
    p.act(lgate[:, :], lmax[:, :], AF.Exp, tag="l1f_lgx")
    p.recip(lsum[:, :], lsum[:, :], tag="l1f_lrs")
    p.tt("vector", lgate[:, :], lgate[:, :], lsum[:, :], ALU.mult, tag="l1f_lg")
    disp = p.dscratch("l1f_disp", (E * CAP, D), BF16)
    dtile = p.sb("l1f_dt", (128, D), BF16)
    zt16 = p.sb("l1f_zt16", (128, D), BF16)
    p.memset("vector", zt16[:, :], 0.0, tag="l1f_ztz")
    for zc in range(E * CAP // 128):
        p.dma("scalar", disp[zc * 128:(zc + 1) * 128, :], zt16[:, :], tag="l1f_dz")
    p.free(zt16)
    for c in range(NL):
        p.dma("sync", dtile[:, :], xn_tm_img[c * 128:(c + 1) * 128, :], tag="l1f_dld")
        p.indirect(disp, IndirectOffsetOnAxis(lslot_i[:, c:c + 1], 0), dtile[:, :],
                   None, extra_reads=[lslot_i], bounds_check=E * CAP - 1,
                   tag=f"l1f_scat{c}")
    disp_all = p.dscratch("l1f_dispall", (E * CAP, D), BF16)
    p.collective("AllReduce", ALU.add, [disp], [disp_all], n_cores=n_cores)

    myrows = p.sb("l1f_myrows", (128, 5), I32)
    p.dma("sync", myrows[:, :], W["myrows"], tag="l1f_myr")
    xd_tm = [p.sb(f"l1f_xd{k}", (128, D), BF16) for k in range(5)]
    for k in range(5):
        p.indirect(xd_tm[k][:, :], None, disp_all,
                   IndirectOffsetOnAxis(myrows[:, k:k + 1], 0),
                   extra_reads=[myrows], bounds_check=E * CAP - 1, tag="l1f_gxd")
    xd_fm = [p.sb(f"l1f_xdf{i}", (128, CAP), BF16) for i in range(NI)]
    tps = P[7][:, 0:64].bitcast(BF16)
    for k in range(5):
        for i in range(NI):
            p.transpose(tps, xd_tm[k][:, i * 128:(i + 1) * 128], cm.ident[:, :],
                        tag="l1f_tp")
            p.copy("vector", xd_fm[i][:, k * 128:(k + 1) * 128], tps, tag="l1f_tpe")
    F4 = 4096
    h_fm = [p.sb(f"l1f_h{o}", (128, CAP), BF16) for o in range(F4 // 128)]
    w1b = [p.sb(f"l1f_w1b{k}", (128, NI * 512), BF16) for k in range(2)]
    b1c = load_vec_cols(p, "l1f_b1c", W["b1m"], n=F4)
    gsc = [p.sb(f"l1f_gs{k}", (128, CAP), F32) for k in range(3)]
    for og in range(F4 // 512):
        wb = w1b[og % 2]
        p.dma("gpsimd", wb[:, :].rearrange("q (i n) -> q i n", i=NI),
              W["w1m"][:, og * 512:(og + 1) * 512].rearrange("(i q) n -> q i n", q=128),
              tag="l1f_w1l")
        for oo in range(4):
            o = og * 4 + oo
            for (t0, tl) in tchunks(CAP):
                ps = P[oo % 2]
                p.mm_acc(ps[:, :tl],
                         [(wb[:, i * 512 + oo * 128:i * 512 + (oo + 1) * 128],
                           xd_fm[i][:, t0:t0 + tl]) for i in range(NI)], tag="l1f_h")
                p.ts("vector", gsc[2][:, t0:t0 + tl], ps[:, :tl], b1c[o], ALU.add,
                     tag="l1f_hb")
            gelu_tanh(p, h_fm[o][:, :], gsc[2][:, :], (gsc[0][:, :], gsc[1][:, :]))
    b2c = load_vec_cols(p, "l1f_b2c", W["b2m"], n=D)
    eo_fm = [p.sb(f"l1f_eo{o}", (128, CAP), BF16) for o in range(NI)]
    w2b = [p.sb(f"l1f_w2b{k}", (128, D), BF16) for k in range(2)]
    for (t0, tl) in tchunks(CAP):
        for ic in range(F4 // 128):
            wb = w2b[ic % 2]
            p.dma("gpsimd", wb[:, :], W["w2m"][ic * 128:(ic + 1) * 128, :],
                  tag="l1f_w2l")
            for o in range(NI):
                p.mm(P[o][:, :tl], wb[:, o * 128:(o + 1) * 128], h_fm[ic][:, t0:t0 + tl],
                     start=(ic == 0), stop=(ic == F4 // 128 - 1), tag="l1f_eomm")
        for o in range(NI):
            p.ts("vector", gsc[o % 2][:, t0:t0 + tl], P[o][:, :tl], b2c[o], ALU.add,
                 tag="l1f_eob")
            p.copy("vector", eo_fm[o][:, t0:t0 + tl], gsc[o % 2][:, t0:t0 + tl],
                   tag="l1f_eoc")
    eo_tm = p.sb("l1f_eotm", (128, D), BF16)
    eo_loc = p.dscratch("l1f_eoloc", (CAP, D), BF16)
    for k in range(5):
        for i in range(NI):
            p.transpose(tps, eo_fm[i][:, k * 128:(k + 1) * 128], cm.ident[:, :],
                        tag="l1f_tpo")
            p.copy("vector", eo_tm[:, i * 128:(i + 1) * 128], tps, tag="l1f_tpoe")
        p.dma("sync", eo_loc[k * 128:(k + 1) * 128, :], eo_tm[:, :], tag="l1f_eost")
    eo_all = p.dscratch("l1f_eoall", (E * CAP, D), BF16)
    p.collective("AllGather", ALU.bypass, [eo_loc], [eo_all], n_cores=n_cores)

    f_tm_img = p.dscratch("l1f_ftm", (T, D), BF16)
    ftile = p.sb("l1f_ftile", (128, D), BF16)
    p.ts("vector", keep[:, :], keep[:, :], -1.0, ALU.mult, 1.0, ALU.add, tag="l1f_kinv")
    p.tt("vector", lgate[:, :], lgate[:, :], keep[:, :], ALU.mult, tag="l1f_gk")
    for c in range(NL):
        p.memset("vector", ftile[:, :], 0.0, tag="l1f_fz")
        p.indirect(ftile[:, :], None, eo_all,
                   IndirectOffsetOnAxis(lslot_i[:, c:c + 1], 0),
                   extra_reads=[lslot_i], bounds_check=E * CAP - 1, tag=f"l1f_gf{c}")
        p.ts("vector", ftile[:, :], ftile[:, :], lgate[:, c:c + 1], ALU.mult,
             tag="l1f_fg")
        p.dma("sync", f_tm_img[c * 128:(c + 1) * 128, :], ftile[:, :], tag="l1f_fst")
    fb = [p.sb(f"l1f_fb{k}", (128, TC), BF16) for k in range(2)]
    ab = [p.sb(f"l1f_ab{k}", (128, TC), F32) for k in range(4)]
    ob = [p.sb(f"l1f_ob{k}", (128, TC), F32) for k in range(2)]
    for o in range(NI):
        for (t0, tl) in tchunks(T):
            k = (o * 4 + t0 // TC) % 2
            p.dma_transpose("scalar", fb[k][:, :tl],
                            f_tm_img[t0:t0 + tl, o * 128:(o + 1) * 128], tag="l1f_ftr")
            p.dma("sync", ab[k][:, :tl], a_hbm[o * 128:(o + 1) * 128, t0:t0 + tl],
                  tag="l1f_lda")
            xb = ab[2 + k]
            p.dma("sync", xb[:, :tl], x_hbm[o * 128:(o + 1) * 128, t0:t0 + tl],
                  tag="l1f_ldx2")
            p.tt("vector", ob[k][:, :tl], xb[:, :tl], ab[k][:, :tl],
                 ALU.add, tag="l1f_oxa")
            p.tt("vector", ob[k][:, :tl], ob[k][:, :tl], fb[k][:, :tl], ALU.add,
                 tag="l1f_of")
            p.dma("sync", out_hbm[o * 128:(o + 1) * 128, t0:t0 + tl], ob[k][:, :tl],
                  tag="l1f_out")



def build_l3_ffn(p: Prog, cm: Common, x_hbm, a_hbm, W, out_hbm, Tloc=4096):
    T = Tloc
    P = cm.P
    E = 8
    NL = 32
    BIG = 1.0e6
    NBT = 16
    F4 = 4096

    gw = p.sb("l3f_gw", (128, NI * E), BF16)
    p.dma("gpsimd", gw[:, :].rearrange("p (i e) -> p i e", i=NI),
          W["centT"].rearrange("(i p) e -> p i e", p=128), tag="l3f_gw")
    ltm = p.sb("l3f_ltm", (128, NL * E), F32)
    xn_fm = p.dscratch("l3f_xnfm", (D, T), BF16)
    TH = T // 2
    for hf in range(2):
        h0 = hf * TH
        xful = [p.sb(f"l3fx{i}_h{hf}", (128, TH), F32) for i in range(NI)]
        for i in range(NI):
            p.dma("sync", xful[i][:, :], x_hbm[i * 128:(i + 1) * 128, h0:h0 + TH],
                  tag="l3f_ldx")
        xn = ln_apply_bf16(p, cm, xful, TH, f"l3fln_h{hf}")
        for i in range(NI):
            p.dma("scalar", xn_fm[i * 128:(i + 1) * 128, h0:h0 + TH], xn[i][:, :],
                  tag="l3f_spxn")
        for cl in range(NL // 2):
            c = hf * (NL // 2) + cl
            pairs = [(xn[i][:, cl * 128:(cl + 1) * 128], gw[:, i * E:(i + 1) * E])
                     for i in range(NI)]
            p.mm_acc(P[c % 2][:, :E], pairs, tag="l3f_lmm")
            p.copy("vector", ltm[:, c * E:(c + 1) * E], P[c % 2][:, :E], tag="l3f_le")
        p.free(xful, xn)
    xn_tm = [p.sb(f"l3f_tm{c}", (128, D), BF16) for c in range(2)]
    xn_tm_img = p.dscratch("l3f_xntm", (T, D), BF16)
    for c in range(NL):
        buf = xn_tm[c % 2]
        p.dma_transpose("scalar", buf[:, :], xn_fm[:, c * 128:(c + 1) * 128], tag="l3f_tr")
        p.dma("sync", xn_tm_img[c * 128:(c + 1) * 128, :], buf[:, :], tag="l3f_sttm")
    p.free(xn_tm)

    lmax = p.sb("l3f_lmax", (128, NL), F32)
    p.reduce("vector", lmax[:, :], ltm[:, :].rearrange("q (c e) -> q c e", c=NL),
             ALU.max, tag="l3f_lm")
    alpha = p.sb("l3f_alpha", (128, NL), F32)
    p.act(alpha[:, :], lmax[:, :], AF.Sigmoid, tag="l3f_alpha")
    iote = p.const("l3f_iote", np.tile(np.arange(E, dtype=np.float32), (128, NL)))
    iott = p.sb("l3f_iott", (128, NL * E), F32)
    p.dma("sync", iott[:, :], iote, tag="l3f_iot")
    lohf = p.sb("l3f_lohf", (128, NL * E), F32)
    lcand = p.sb("l3f_lcand", (128, NL * E), F32)
    mbl = lmax[:, :].rearrange("q (c o) -> q c o", o=1).broadcast_to([128, NL, E])
    p.tt("vector", lohf[:, :].rearrange("q (c e) -> q c e", c=NL),
         ltm[:, :].rearrange("q (c e) -> q c e", c=NL), mbl, ALU.is_ge, tag="l3f_oh")
    p.ts("vector", lcand[:, :], iott[:, :], -BIG, ALU.add, tag="l3f_c1")
    p.tt("vector", lcand[:, :], lcand[:, :], lohf[:, :], ALU.mult, tag="l3f_c2")
    p.ts("vector", lcand[:, :], lcand[:, :], BIG, ALU.add, tag="l3f_c3")
    leidx = p.sb("l3f_leidx", (128, NL), F32)
    p.reduce("vector", leidx[:, :], lcand[:, :].rearrange("q (c e) -> q c e", c=NL),
             ALU.min, tag="l3f_eidx")
    ebl = leidx[:, :].rearrange("q (c o) -> q c o", o=1).broadcast_to([128, NL, E])
    p.tt("vector", lohf[:, :].rearrange("q (c e) -> q c e", c=NL),
         iott[:, :].rearrange("q (c e) -> q c e", c=NL), ebl, ALU.is_equal, tag="l3f_oh2")
    loh16 = p.sb("l3f_loh16", (128, NL * E), BF16)
    p.copy("vector", loh16[:, :], lohf[:, :], tag="l3f_ohc")
    triexc = p.sb("l3f_tri", (128, 128), BF16)
    tri_np = (np.arange(128)[:, None] < np.arange(128)[None, :])
    p.dma("sync", triexc[:, :], p.const("l3f_trih", tri_np.astype(ml_dtypes.bfloat16)),
          tag="l3f_trid")
    triexcf = p.sb("l3f_trif", (128, 128), F32)
    p.dma("sync", triexcf[:, :], p.const("l3f_trihf", tri_np.astype(np.float32)),
          tag="l3f_tridf")
    lB = p.sb("l3f_lB", (128, NL * E), F32)
    p.mm(P[4][:, :NL * E], triexc[:, :], loh16[:, :], tag="l3f_lB")
    p.copy("scalar", lB[:, :], P[4][:, :NL * E], tag="l3f_lBe")
    lcnt = p.sb("l3f_lcnt", (1, NL * E), F32)
    p.mm(P[5][0:1, :NL * E], cm.ones_col16[:, :], loh16[:, :], tag="l3f_lcnt")
    p.copy("vector", lcnt[:, :], P[5][0:1, :NL * E], tag="l3f_lcnte")
    lcnt_img = p.dscratch("l3f_lcntimg", (NL, E), F32)
    p.dma("sync", lcnt_img.rearrange("(o c) e -> o (c e)", o=1), lcnt[:, :], tag="l3f_lcst")
    lcnt_t = p.sb("l3f_lcntt", (NL, E), F32)
    p.dma("sync", lcnt_t[:, :], lcnt_img, tag="l3f_lcld")
    lpref = p.sb("l3f_lpref", (NL, E), F32)
    p.mm(P[4][0:NL, :E], triexcf[0:NL, 0:NL], lcnt_t[:, :], tag="l3f_lpm")
    p.copy("vector", lpref[:, :], P[4][0:NL, :E], tag="l3f_lpe")
    tote = p.sb("l3f_tote", (1, E), F32)
    p.mm(P[5][0:1, :E], cm.ones_col[0:NL, :], lcnt_t[:, :], tag="l3f_tot")
    p.copy("vector", tote[:, :], P[5][0:1, :E], tag="l3f_tote")
    rows8 = p.sb("l3f_rows8", (1, 64), F32)
    padtot = rows8[:, 0:E]; base = rows8[:, E:2 * E]; endb = rows8[:, 2 * E:3 * E]
    tmp8 = rows8[:, 3 * E:4 * E]
    tmp8i = p.sb("l3f_tmp8i", (1, E), I32)
    p.ts("vector", tmp8, tote[:, :], 511.0, ALU.add, tag="l3f_pt1")
    p.ts("vector", tmp8, tmp8, 1.0 / 512.0, ALU.mult, -0.5, ALU.add, tag="l3f_pt2")
    p.copy("vector", tmp8i[:, :], tmp8, tag="l3f_pt3")
    p.copy("vector", padtot, tmp8i[:, :], tag="l3f_pt4")
    p.ts("vector", padtot, padtot, 512.0, ALU.mult, tag="l3f_pt5")
    pb_img = p.dscratch("l3f_pbimg", (E, 1), F32)
    p.dma("sync", pb_img.rearrange("(o e) c -> o (e c)", o=1), padtot, tag="l3f_pb1")
    pb_t = p.sb("l3f_pbt", (E, 1), F32)
    p.dma("sync", pb_t[:, :], pb_img, tag="l3f_pb2")
    p.mm(P[4][0:E, 0:1], triexcf[0:E, 0:E], pb_t[:, :], tag="l3f_pb3")
    pbas = p.sb("l3f_pbas", (E, 1), F32)
    p.copy("vector", pbas[:, :], P[4][0:E, 0:1], tag="l3f_pb4")
    pbas_img = p.dscratch("l3f_pbimg2", (E, 1), F32)
    p.dma("sync", pbas_img, pbas[:, :], tag="l3f_pb5")
    p.dma("sync", base, pbas_img.rearrange("(o e) c -> o (e c)", o=1), tag="l3f_pb6")
    p.tt("vector", endb, base, padtot, ALU.add, tag="l3f_endb")
    btstart = p.sb("l3f_bts", (1, NBT * E), F32)
    p.dma("sync", btstart[:, :], p.const(
        "l3f_btsh", (np.arange(NBT)[:, None] * 512 * np.ones((1, E))).reshape(1, -1)
        .astype(np.float32)), tag="l3f_btsld")
    btge = p.sb("l3f_btge", (1, NBT * E), F32)
    p.tt("vector", btge[:, :].rearrange("o (b e) -> o b e", b=NBT), btstart[:, :].rearrange("o (b e) -> o b e", b=NBT),
         endb.rearrange("o (c e) -> o c e", c=1).broadcast_to([1, NBT, E]), ALU.is_ge, tag="l3f_btge")
    bte = p.sb("l3f_bte", (1, NBT), F32)
    p.reduce("vector", bte[:, :], btge[:, :].rearrange("o (b e) -> o b e", b=NBT),
             ALU.add, tag="l3f_bte")
    ecols = p.sb("l3f_ecols", (128, NBT), F32)
    p.mm(P[4][:, :NBT], cm.ones_row[:, :], bte[:, :], tag="l3f_erep")
    p.copy("vector", ecols[:, :], P[4][:, :NBT], tag="l3f_ecol")
    baserep = p.sb("l3f_brep", (128, E), F32)
    p.mm(P[4][:, :E], cm.ones_row[:, :], base, tag="l3f_brepm")
    p.copy("scalar", baserep[:, :], P[4][:, :E], tag="l3f_brepe")
    lpref_img = p.dscratch("l3f_lprefimg", (NL, E), F32)
    p.dma("sync", lpref_img, lpref[:, :], tag="l3f_lpst")
    lprow = p.sb("l3f_lprow", (1, NL * E), F32)
    p.dma("sync", lprow[:, :], lpref_img.rearrange("(o c) e -> o (c e)", o=1), tag="l3f_lpld")
    for (t0, tl) in tchunks(NL * E):
        p.mm(P[4][:, :tl], cm.ones_row[:, :], lprow[:, t0:t0 + tl], tag="l3f_lpr")
        p.tt("vector", lB[:, t0:t0 + tl], lB[:, t0:t0 + tl], P[4][:, :tl], ALU.add,
             tag="l3f_pA")
        p.tt("vector", lB[:, t0:t0 + tl].rearrange("q (c e) -> q c e", e=E),
             lB[:, t0:t0 + tl].rearrange("q (c e) -> q c e", e=E),
             baserep[:, :].rearrange("q (c e) -> q c e", c=1)
             .broadcast_to([128, tl // E, E]), ALU.add, tag="l3f_pB")
    p.tt("vector", lB[:, :], lB[:, :], lohf[:, :], ALU.mult, tag="l3f_pm")
    lslot = p.sb("l3f_lslot", (128, NL), F32)
    p.reduce("vector", lslot[:, :], lB[:, :].rearrange("q (c e) -> q c e", c=NL),
             ALU.add, tag="l3f_pos")
    lslot_i = p.sb("l3f_lsloti", (128, NL), I32)
    p.copy("vector", lslot_i[:, :], lslot[:, :], tag="l3f_si")

    NSLOT = NBT * 512
    slotimg = p.dscratch("l3f_slotimg", (NSLOT, 4), F32)
    fillt = p.sb("l3f_fill", (128, NSLOT * 4 // 128), F32)
    p.memset("vector", fillt[:, :], BIG, tag="l3f_fillm")
    p.dma("sync", slotimg.rearrange("(q c) e -> q (c e)", q=128), fillt[:, :],
          tag="l3f_fill")
    tokid = p.sb("l3f_tokid", (128, NL), F32)
    iotq = p.sb("l3f_iotq", (128, 1), I32)
    p.iota(iotq[:, :], pattern=[[0, 1]], base=0, channel_multiplier=1, tag="l3f_iotq")
    iotqf = p.sb("l3f_iotqf", (128, 1), F32)
    p.copy("vector", iotqf[:, :], iotq[:, :], tag="l3f_iotqf")
    for c in range(NL):
        p.ts("vector", tokid[:, c:c + 1], iotqf[:, :], float(c * 128), ALU.add,
             tag="l3f_tid")
    pay = p.sb("l3f_pay", (128, 4), F32)
    for c in range(NL):
        p.copy("vector", pay[:, 0:1], tokid[:, c:c + 1], tag="l3f_p0")
        p.copy("vector", pay[:, 1:2], alpha[:, c:c + 1], tag="l3f_p1")
        p.copy("vector", pay[:, 2:3], leidx[:, c:c + 1], tag="l3f_p2")
        p.indirect(slotimg, IndirectOffsetOnAxis(lslot_i[:, c:c + 1], 0), pay[:, :],
                   None, extra_reads=[lslot_i], bounds_check=NSLOT - 1,
                   tag=f"l3f_sc{c}")
    p.free(ltm, lmax, lohf, lcand, loh16, lB, lcnt, iott, lcnt_t, lpref)
    p.free(alpha, leidx, lslot, lslot_i, tokid, pay, iotq, iotqf, fillt,
           btstart, btge, bte, triexcf, rows8, tmp8i, tote, lprow,
           pb_t, pbas, gw)

    w1t16 = p.dscratch("l3f_w1t16", (E * 1024, F4), BF16)
    w2t16 = p.dscratch("l3f_w2t16", (E * F4, D), BF16)
    for e in range(E):
        p.dma("gpsimd", w1t16[e * 1024:(e + 1) * 1024, :],
              W["W1T"][e * 1024:(e + 1) * 1024, :], tag="l3f_cw1")
        p.dma("gpsimd", w2t16[e * F4:(e + 1) * F4, :],
              W["W2T"][e * F4:(e + 1) * F4, :], tag="l3f_cw2")

    nw1, nw2 = 8, 32
    NWC = nw1 + nw2 + 4
    Cnp = np.zeros((128, NWC), np.float32)
    Mnp = np.zeros((1, NWC), np.float32)
    for k in range(nw1):
        Cnp[:, k] = k * 128 + np.arange(128); Mnp[0, k] = 1024
    for k in range(nw2):
        Cnp[:, nw1 + k] = k * 128 + np.arange(128); Mnp[0, nw1 + k] = F4
    Cnp[:, nw1 + nw2 + 0] = np.arange(128); Mnp[0, nw1 + nw2 + 0] = 128
    Cnp[:, nw1 + nw2 + 1] = np.arange(128); Mnp[0, nw1 + nw2 + 1] = 128
    Cnp[:, nw1 + nw2 + 2] = 0; Mnp[0, nw1 + nw2 + 2] = 1
    Cnp[:, nw1 + nw2 + 3] = 0; Mnp[0, nw1 + nw2 + 3] = 1
    Cw = p.sb("l3f_Cw", (128, NWC), F32)
    p.dma("sync", Cw[:, :], p.const("l3f_Cwh", Cnp), tag="l3f_Cw")
    Mw = p.sb("l3f_Mw", (128, NWC), F32)
    p.dma("sync", Mw[:, :], p.const("l3f_Mwh", np.tile(Mnp, (128, 1))), tag="l3f_Mw")
    widx = p.sb("l3f_widx", (128, NWC), F32)
    widx_i = p.sb("l3f_widxi", (128, NWC), I32)
    chgc = p.sb("l3f_chg", (128, 1), F32)

    W1c = [p.sb(f"l3f_W1c{k}", (128, F4), BF16) for k in range(nw1)]
    W2c = [p.sb(f"l3f_W2c{k}", (128, D), BF16) for k in range(nw2)]
    b1c = p.sb("l3f_b1c", (128, 32), F32)
    b2c = p.sb("l3f_b2c", (128, 8), F32)
    gamc = p.sb("l3f_gam", (128, D), F32)
    betc = p.sb("l3f_bet", (128, D), F32)
    pay4 = [p.sb(f"l3f_pay4_{s}", (128, 4), F32) for s in range(4)]
    tid4 = [p.sb(f"l3f_tid4_{s}", (128, 1), I32) for s in range(4)]
    xd_tm = [p.sb(f"l3f_xd{s}", (128, D), BF16) for s in range(4)]
    xd_fm = [p.sb(f"l3f_xdf{i}", (128, 512), BF16) for i in range(NI)]
    h_fm = [p.sb(f"l3f_h{o}", (128, 512), BF16) for o in range(F4 // 128)]
    y_tm = [p.sb(f"l3f_y{s}", (128, D), BF16) for s in range(4)]
    sc32 = [p.sb(f"l3f_sc{k}", (128, 512), F32) for k in range(2)]
    scD = [p.sb(f"l3f_scD{k}", (128, D), F32) for k in range(1)]
    stat = p.sb("l3f_stat", (128, 8), F32)
    tps = P[7][:, 0:64].bitcast(BF16)
    f_tm_img = p.dscratch("l3f_ftm", (T, D), BF16)

    for bt in range(NBT):
        ecol = ecols[:, bt:bt + 1]
        if bt == 0:
            p.memset("vector", chgc[:, :], 1.0, tag="l3f_chg0")
        else:
            p.tt("vector", chgc[:, :], ecols[:, bt:bt + 1], ecols[:, bt - 1:bt],
                 ALU.not_equal, tag="l3f_chg")
        p.stt("vector", widx[:, :], Mw[:, :], ecol, Cw[:, :], ALU.mult, ALU.add,
              tag="l3f_wi1")
        p.ts("vector", chgc[:, :], chgc[:, :], -BIG, ALU.mult, BIG, ALU.add,
             tag="l3f_wi2")
        p.ts("vector", widx[:, :], widx[:, :], chgc[:, :], ALU.add, tag="l3f_wi3")
        p.copy("vector", widx_i[:, :], widx[:, :], tag="l3f_wi4")
        for k in range(nw1):
            p.indirect(W1c[k][:, :], None, w1t16,
                       IndirectOffsetOnAxis(widx_i[:, k:k + 1], 0),
                       extra_reads=[widx_i], bounds_check=E * 1024 - 1, tag="l3f_gw1")
        for k in range(nw2):
            p.indirect(W2c[k][:, :], None, w2t16,
                       IndirectOffsetOnAxis(widx_i[:, nw1 + k:nw1 + k + 1], 0),
                       extra_reads=[widx_i], bounds_check=E * F4 - 1, tag="l3f_gw2")
        p.indirect(b1c[:, :], None, W["b1T"],
                   IndirectOffsetOnAxis(widx_i[:, nw1 + nw2:nw1 + nw2 + 1], 0),
                   extra_reads=[widx_i], bounds_check=E * 128 - 1, tag="l3f_gb1")
        p.indirect(b2c[:, :], None, W["b2T"],
                   IndirectOffsetOnAxis(widx_i[:, nw1 + nw2 + 1:nw1 + nw2 + 2], 0),
                   extra_reads=[widx_i], bounds_check=E * 128 - 1, tag="l3f_gb2")
        p.indirect(gamc[:, :], None, W["gamma"],
                   IndirectOffsetOnAxis(widx_i[:, nw1 + nw2 + 2:nw1 + nw2 + 3], 0),
                   extra_reads=[widx_i], bounds_check=E - 1, tag="l3f_gg")
        p.indirect(betc[:, :], None, W["beta"],
                   IndirectOffsetOnAxis(widx_i[:, nw1 + nw2 + 3:nw1 + nw2 + 4], 0),
                   extra_reads=[widx_i], bounds_check=E - 1, tag="l3f_gbet")
        for s in range(4):
            p.dma("sync", pay4[s][:, :],
                  slotimg[bt * 512 + s * 128: bt * 512 + (s + 1) * 128, :],
                  tag="l3f_pld")
            p.copy("vector", tid4[s][:, :], pay4[s][:, 0:1], tag="l3f_tid4")
            p.indirect(xd_tm[s][:, :], None, xn_tm_img,
                       IndirectOffsetOnAxis(tid4[s][:, 0:1], 0),
                       extra_reads=[tid4[s]], bounds_check=T - 1, tag="l3f_gxd")
        for s in range(4):
            xs = xd_tm[s]
            m1 = stat[:, s:s + 1]; v1 = stat[:, 4 + s:5 + s]
            p.reduce("vector", m1, xs[:, :], ALU.add, tag="l3f_sm")
            p.ts("vector", m1, m1, 1.0 / D, ALU.mult, tag="l3f_sm2")
            p.ts("vector", scD[0][:, :], xs[:, :], m1, ALU.subtract, tag="l3f_xc")
            v1b = stat[:, 3:4]
            for hh in range(2):
                p.tt("vector", sc32[1][:, :], scD[0][:, hh * 512:(hh + 1) * 512],
                     scD[0][:, hh * 512:(hh + 1) * 512], ALU.mult, tag="l3f_x2")
                p.reduce("vector", v1 if hh == 0 else v1b, sc32[1][:, :], ALU.add,
                         tag="l3f_sv")
            p.tt("vector", v1, v1, v1b, ALU.add, tag="l3f_svs")
            p.ts("vector", v1, v1, 1.0 / D, ALU.mult, tag="l3f_sv2")
            p.act(v1, v1, AF.Sqrt, bias=p.kcol(1e-5), tag="l3f_sq")
            p.recip(v1, v1, tag="l3f_rs")
            p.ts("vector", scD[0][:, :], scD[0][:, :], v1, ALU.mult, tag="l3f_nm")
            p.tt("vector", scD[0][:, :], scD[0][:, :], gamc[:, :], ALU.mult,
                 tag="l3f_ga")
            p.tt("vector", scD[0][:, :], scD[0][:, :], betc[:, :], ALU.add,
                 tag="l3f_be")
            p.copy("vector", y_tm[s][:, :], xs[:, :], tag="l3f_stash")
            p.copy("vector", xs[:, :], scD[0][:, :], tag="l3f_hcast")
        for s in range(4):
            for i in range(NI):
                p.transpose(tps, xd_tm[s][:, i * 128:(i + 1) * 128], cm.ident[:, :],
                            tag="l3f_tp")
                p.copy("vector", xd_fm[i][:, s * 128:(s + 1) * 128], tps, tag="l3f_tpe")
        for fc in range(F4 // 128):
            ps = P[fc % 2]
            for (t0, tl) in tchunks(512):
                p.mm_acc(ps[:, :tl],
                         [(W1c[fc // 4][:, (fc % 4) * 1024 + i * 128:
                                        (fc % 4) * 1024 + (i + 1) * 128]
                           if False else
                           W1c[i][:, fc * 128:(fc + 1) * 128],
                           xd_fm[i][:, t0:t0 + tl]) for i in range(NI)], tag="l3f_h1")
                p.ts("vector", sc32[0][:, t0:t0 + tl], ps[:, :tl],
                     b1c[:, fc // 16:fc // 16 + 1] if False else b1c[:, fc % 32:fc % 32 + 1],
                     ALU.add, tag="l3f_hb")
            p.op("vector", lambda e, o=h_fm[fc], s=sc32[0]: e.tensor_relu(o[:, :], s[:, :]),
                 [sc32[0]], [h_fm[fc]], "l3f_relu")
        for dc in range(NI):
            ps = P[dc % 2]
            for (t0, tl) in tchunks(512):
                p.mm_acc(ps[:, :tl],
                         [(W2c[ic][:, dc * 128:(dc + 1) * 128], h_fm[ic * 1 if False else ic][:, t0:t0 + tl])
                          for ic in range(nw2)], tag="l3f_y")
                p.ts("vector", sc32[dc % 2][:, t0:t0 + tl], ps[:, :tl],
                     b2c[:, dc:dc + 1], ALU.add, tag="l3f_yb")
            for s in range(4):
                p.copy("vector", xd_fm[dc][:, s * 128:(s + 1) * 128],
                       sc32[dc % 2][:, s * 128:(s + 1) * 128], tag="l3f_yc")
        for s in range(4):
            for dc in range(NI):
                p.transpose(tps, xd_fm[dc][:, s * 128:(s + 1) * 128], cm.ident[:, :],
                            tag="l3f_tpy")
                p.copy("vector", scD[0][:, dc * 128:(dc + 1) * 128], tps, tag="l3f_tpye")
            a1 = pay4[s][:, 1:2]
            p.ts("vector", scD[0][:, :], scD[0][:, :], a1, ALU.mult, tag="l3f_am")
            p.tt("vector", scD[0][:, :], scD[0][:, :], y_tm[s][:, :], ALU.add,
                 tag="l3f_ax")
            p.copy("vector", y_tm[s][:, :], scD[0][:, :], tag="l3f_fcast")
            p.indirect(f_tm_img, IndirectOffsetOnAxis(tid4[s][:, 0:1], 0),
                       y_tm[s][:, :], None, extra_reads=[tid4[s]],
                       bounds_check=T - 1, tag="l3f_scb")

    p.free(W1c, W2c, b1c, b2c, gamc, betc, pay4, tid4, xd_tm, xd_fm, h_fm,
           y_tm, sc32, scD, stat, widx, widx_i, chgc, Cw, Mw, ecols, baserep)
    fb = [p.sb(f"l3f_fb{k}", (128, TC), BF16) for k in range(2)]
    ab = [p.sb(f"l3f_ab{k}", (128, TC), F32) for k in range(4)]
    ob = [p.sb(f"l3f_ob{k}", (128, TC), F32) for k in range(2)]
    for o in range(NI):
        for (t0, tl) in tchunks(T):
            k = (o * 4 + t0 // TC) % 2
            p.dma_transpose("scalar", fb[k][:, :tl],
                            f_tm_img[t0:t0 + tl, o * 128:(o + 1) * 128], tag="l3f_ftr")
            p.dma("sync", ab[k][:, :tl], a_hbm[o * 128:(o + 1) * 128, t0:t0 + tl],
                  tag="l3f_lda")
            xb = ab[2 + k]
            p.dma("sync", xb[:, :tl], x_hbm[o * 128:(o + 1) * 128, t0:t0 + tl],
                  tag="l3f_ldx2")
            p.tt("vector", ob[k][:, :tl], xb[:, :tl], ab[k][:, :tl], ALU.add,
                 tag="l3f_oxa")
            p.tt("vector", ob[k][:, :tl], ob[k][:, :tl], fb[k][:, :tl], ALU.add,
                 tag="l3f_of")
            p.dma("sync", out_hbm[o * 128:(o + 1) * 128, t0:t0 + tl], ob[k][:, :tl],
                  tag="l3f_out")


def build_l0_ffn(p: Prog, cm: Common, x_hbm, a_hbm, W, out_hbm, Tloc=4096, nseq=2):
    T = Tloc // nseq
    P = cm.P
    HEADS, DH, WIN = 8, 64, 512
    INNER = 512
    NW = T // WIN
    HID2 = 2816

    t_ar = np.arange(T, dtype=np.float64)
    half = np.arange(0, DH, 2, dtype=np.float64)
    inv_freq = 1.0 / 10000.0 ** (half / DH)
    freqs = t_ar[:, None] * inv_freq[None, :]
    freqs = np.concatenate([freqs, freqs], -1)
    base = (half + 0.4 * DH) / (1.4 * DH)
    power = (t_ar - T // 2) / (WIN // 2)
    scale = base[None, :] ** power[:, None]
    scale = np.concatenate([scale, scale], -1)
    cosf = np.cos(freqs); sinf = np.sin(freqs)
    QS = 8.0
    tabs = {
        "cq": (QS * cosf * scale), "sq": (QS * sinf * scale),
        "ck": (cosf / scale), "sk": (sinf / scale),
    }
    tabt = {}
    for nm, tb in tabs.items():
        arr = np.tile(tb.T.astype(np.float32), (2, 1))
        t = p.sb(f"l0f_{nm}", (128, T), F32)
        p.dma("sync", t[:, :], p.const(f"l0f_{nm}h", arr), tag="l0f_tab")
        tabt[nm] = t
    R = np.zeros((128, 128), np.float32)
    for b0 in (0, 64):
        for i in range(32):
            R[b0 + 32 + i, b0 + i] = -1.0
            R[b0 + i, b0 + 32 + i] = 1.0
    Rt = p.sb("l0f_R", (128, 128), BF16)
    p.dma("sync", Rt[:, :], p.const("l0f_Rh", R.astype(ml_dtypes.bfloat16)), tag="l0f_R")
    hs = np.zeros((128, 2), np.float32)
    hs[0:64, 0] = 1; hs[64:128, 1] = 1
    HS = p.sb("l0f_HS", (128, 2), F32)
    p.dma("sync", HS[:, :], p.const("l0f_HSh", hs), tag="l0f_HS")
    HR = p.sb("l0f_HR", (2, 128), F32)
    p.dma("sync", HR[:, :], p.const("l0f_HRh", np.ascontiguousarray(hs.T)), tag="l0f_HR")
    mask_tT = []
    for j in range(8):
        krel = j * 128 + np.arange(128)[:, None] - 512
        qq = np.arange(512)[None, :]
        m = ((qq >= krel) & (qq < krel + WIN)).astype(ml_dtypes.bfloat16)
        mt = p.sb(f"l0f_mskT{j}", (128, 512), BF16)
        p.dma("sync", mt[:, :], p.const(f"l0f_mskTh{j}", m), tag="l0f_msk")
        mask_tT.append(mt)

    z_hbm = p.dscratch("l0f_z", (D, T), F32)
    z2_hbm = p.dscratch("l0f_z2", (D, T), F32)
    zmid_hbm = p.dscratch("l0f_zmid", (D, T), F32)

    qsc, ksc = [], []
    for l in range(2):
        qs_t = p.sb(f"l0f_qs{l}", (128, 1), F32)
        p.dma("sync", qs_t[:, :], W[f"qs{l}"], tag="l0f_qs")
        qsc.append(qs_t)
        ks_t = p.sb(f"l0f_ks{l}", (128, 1), F32)
        p.dma("sync", ks_t[:, :], W[f"ks{l}"], tag="l0f_ks")
        ksc.append(ks_t)

    for s in range(nseq):
        xcols = x_hbm[:, s * T:(s + 1) * T]
        sfx = f"s{s}"
        xful = [p.sb(f"l0fx{i}_{sfx}", (128, T), F32) for i in range(NI)]
        for i in range(NI):
            p.dma("sync", xful[i][:, :], xcols[i * 128:(i + 1) * 128, :], tag="l0f_ldx")
        rowsT, rows = ln_rows(p, cm, xful, T, f"l0fln_{sfx}")
        t32 = [p.sb(f"l0ft{k}_{sfx}", (128, TC), F32) for k in range(2)]
        for (t0, tl) in tchunks(T):
            rep_row(p, cm, P[2][:, :tl], rows["ra"][:, t0:t0 + tl])
            rep_row(p, cm, P[3][:, :tl], rows["rb"][:, t0:t0 + tl])
            for i in range(NI):
                p.tt("vector", t32[0][:, :tl], xful[i][:, t0:t0 + tl], P[2][:, :tl],
                     ALU.mult)
                p.tt("vector", t32[1][:, :tl], t32[0][:, :tl], P[3][:, :tl], ALU.add)
                p.dma("scalar", z_hbm[i * 128:(i + 1) * 128, t0:t0 + tl],
                      t32[1][:, :tl], tag="l0f_spz")
        p.free(xful, rowsT, *t32)

        zin, zout = z_hbm, z2_hbm
        for l in range(2):
            lfx = f"{sfx}l{l}"
            hpad = [p.sb(f"l0fh{i}_{lfx}", (128, T + 1), BF16) for i in range(NI)]
            for i in range(NI):
                p.memset("vector", hpad[i][:, 0:1], 0.0, tag="l0f_hp")
                p.dma("gpsimd", hpad[i][:, 1:T + 1], zin[i * 128:(i + 1) * 128, :],
                      tag="l0f_ldz")
            def hap(i, t0, tl):
                if i < 4:
                    return hpad[i][:, 1 + t0:1 + t0 + tl]
                return hpad[i][:, t0:t0 + tl]
            wq = load_w_bf16(p, f"l0f_wq_{lfx}", W["wqkv"][l], I=D, O=1536)
            qf = [p.sb(f"l0f_qf{c}_{lfx}", (128, T), F32) for c in range(4)]
            kf = [p.sb(f"l0f_kf{c}_{lfx}", (128, T), F32) for c in range(4)]
            vt = [p.sb(f"l0f_vt{c}_{lfx}", (128, 520), BF16) for c in range(T // 128)]
            for (t0, tl) in tchunks(T):
                for o in range(8):
                    ps = P[o % 2]
                    p.mm_acc(ps[:, :tl],
                             [(wq[i][:, o * 128:(o + 1) * 128], hap(i, t0, tl))
                              for i in range(NI)], tag="l0f_qk")
                    dst = qf[o] if o < 4 else kf[o - 4]
                    p.copy("vector", dst[:, t0:t0 + tl], ps[:, :tl], tag="l0f_qke")
            for c in range(T // 128):
                ps = P[2 + c % 2]
                p.mm_acc(ps[:, 0:512],
                         [(hap(i, c * 128, 128), wq[i][:, 1024:1536])
                          for i in range(NI)], tag="l0f_v")
                p.copy("vector",
                       vt[c][:, :].rearrange("p (h f) -> p h f", f=65)[:, :, 0:64],
                       ps[:, 0:512].rearrange("p (h f) -> p h f", f=64), tag="l0f_ve")
                p.memset("vector",
                         vt[c][:, :].rearrange("p (h f) -> p h f", f=65)[:, :, 64:65],
                         1.0, tag="l0f_vone")
            p.free(wq, hpad)
            sq32 = p.sb(f"l0f_sq_{lfx}", (128, TC), F32)
            nrm = p.sb(f"l0f_nrm_{lfx}", (2, T), F32)
            qb = [p.sb(f"l0f_qb{c}_{lfx}", (128, T), BF16) for c in range(4)]
            kb = [p.sb(f"l0f_kb{c}_{lfx}", (128, T), BF16) for c in range(4)]
            for (src, dst, ct, st, sccol) in ((qf, qb, "cq", "sq", qsc[l]),
                                              (kf, kb, "ck", "sk", ksc[l])):
                for c in range(4):
                    for (t0, tl) in tchunks(T):
                        p.act(sq32[:, :tl], src[c][:, t0:t0 + tl], AF.Square,
                              tag="l0f_n2")
                        p.mm(P[4][0:2, :tl], HS[:, :], sq32[:, :tl], tag="l0f_ns")
                        p.copy("vector", nrm[:, t0:t0 + tl], P[4][0:2, :tl],
                               tag="l0f_nse")
                    p.act(nrm[:, :], nrm[:, :], AF.Sqrt, bias=p.kcol(0.0, 2),
                          tag="l0f_nsq")
                    p.ts("vector", nrm[:, :], nrm[:, :], 1e-12, ALU.max, tag="l0f_ncl")
                    p.recip(nrm[:, :], nrm[:, :], tag="l0f_nrc")
                    for (t0, tl) in tchunks(T):
                        p.mm(P[4][:, :tl], HR[:, :], nrm[:, t0:t0 + tl], tag="l0f_nrep")
                        p.tt("vector", src[c][:, t0:t0 + tl], src[c][:, t0:t0 + tl],
                             P[4][:, :tl], ALU.mult, tag="l0f_nmul")
                        p.ts("vector", src[c][:, t0:t0 + tl], src[c][:, t0:t0 + tl],
                             sccol[:, 0:1], ALU.mult, tag="l0f_scl")
                    cb = qb[c] if src is qf else kb[c]
                    for (t0, tl) in tchunks(T):
                        srcb = sq32
                        p.copy("vector", cb[:, t0:t0 + tl], src[c][:, t0:t0 + tl],
                               tag="l0f_cc")
                        p.mm(P[5][:, :tl], Rt[:, :], cb[:, t0:t0 + tl], tag="l0f_rot")
                        p.tt("vector", srcb[:, :tl], src[c][:, t0:t0 + tl],
                             tabt[ct][:, t0:t0 + tl], ALU.mult, tag="l0f_rcos")
                        p.tt("vector", cb[:, t0:t0 + tl], P[5][:, :tl],
                             tabt[st][:, t0:t0 + tl], ALU.mult, tag="l0f_rsin")
                        p.tt("vector", cb[:, t0:t0 + tl], cb[:, t0:t0 + tl],
                             srcb[:, :tl], ALU.add, tag="l0f_radd")
            p.free(qf, kf, sq32, nrm)
            ofm = [p.sb(f"l0f_of{c}_{lfx}", (128, T), BF16) for c in range(4)]
            expP = [p.sb(f"l0f_eP{j}_{lfx}", (128, 512), BF16) for j in range(8)]
            zrow = p.sb(f"l0f_zr_{lfx}", (1, 512), F32)
            o32 = p.sb(f"l0f_o32_{lfx}", (64, 512), F32)
            for w in range(NW):
                k0 = (w - 1) * WIN if w > 0 else 0
                nk = 1024 if w > 0 else 512
                nks = nk // 128
                for h in range(HEADS):
                    ch = h // 2
                    pb = (h % 2) * 64
                    qap = qb[ch][pb:pb + 64, w * WIN:(w + 1) * WIN]
                    for j in range(nks):
                        ps = P[j % 4]
                        p.mm(ps[:, 0:512],
                             kb[ch][pb:pb + 64, k0 + j * 128:k0 + (j + 1) * 128],
                             qap, tag="l0f_S")
                        p.act(expP[j][:, :], ps[:, 0:512], AF.Exp, tag="l0f_eS")
                    for j in range(nks):
                        joff = j if w > 0 else j + 4
                        p.tt("vector", expP[j][:, :], expP[j][:, :],
                             mask_tT[joff][:, :], ALU.mult, tag="l0f_mm")
                    for j in range(nks):
                        kc = (k0 + j * 128) // 128
                        p.mm(P[4][0:65, 0:512],
                             vt[kc][:, h * 65:(h + 1) * 65], expP[j][:, :],
                             start=(j == 0), stop=(j == nks - 1), tag="l0f_O")
                    p.copy("vector", zrow[:, :], P[4][64:65, 0:512], tag="l0f_Z")
                    p.recip(zrow[:, :], zrow[:, :], tag="l0f_Zr")
                    p.mm(P[5][0:64, 0:512], cm.ones_row[:, 0:64], zrow[:, :],
                         tag="l0f_Zrep")
                    p.copy("scalar", o32[:, :], P[4][0:64, 0:512], tag="l0f_Oe")
                    p.tt("vector", ofm[ch][pb:pb + 64, w * WIN:(w + 1) * WIN],
                         o32[:, :], P[5][0:64, 0:512], ALU.mult, tag="l0f_On")
            p.free(qb, kb, vt, expP, zrow, o32)
            wo = load_w_bf16(p, f"l0f_wo_{lfx}", W["wo"][l], I=512, O=D)
            zc = [p.sb(f"l0f_zc{k}_{lfx}", (128, TC), F32) for k in range(2)]
            for (t0, tl) in tchunks(T):
                for o in range(NI):
                    ps = P[o % 2]
                    p.mm_acc(ps[:, :tl],
                             [(wo[i][:, o * 128:(o + 1) * 128],
                               ofm[i][:, t0:t0 + tl]) for i in range(4)], tag="l0f_wo")
                    zl = zc[o % 2]
                    p.dma("sync", zl[:, :tl], zin[o * 128:(o + 1) * 128, t0:t0 + tl],
                          tag="l0f_ldz2")
                    p.tt("vector", zl[:, :tl], zl[:, :tl], ps[:, :tl], ALU.add,
                         tag="l0f_zadd")
                    p.dma("scalar", zmid_hbm[o * 128:(o + 1) * 128, t0:t0 + tl],
                          zl[:, :tl], tag="l0f_spz2")
            p.free(wo, ofm, zc)
            hp2 = [p.sb(f"l0fh2{i}_{lfx}", (128, T + 1), F32) for i in range(NI)]
            for i in range(NI):
                p.memset("vector", hp2[i][:, 0:1], 0.0, tag="l0f_h2p")
                p.dma("sync", hp2[i][:, 1:T + 1], zmid_hbm[i * 128:(i + 1) * 128, :],
                      tag="l0f_ldz3")
            shv = []
            for i in range(NI):
                shv.append(hp2[i][:, 1:T + 1] if i < 4 else hp2[i][:, 0:T])
            hws = [p.sb(f"l0fhw{i}_{lfx}", (128, T), F32) for i in range(NI)]
            for i in range(NI):
                p.copy("gpsimd", hws[i][:, :], shv[i], tag="l0f_hsh")
            p.free(hp2)
            rowsT2, rows2 = ln_rows(p, cm, hws, T, f"l0f2ln_{lfx}")
            lng = load_vec_cols(p, f"l0f_lng_{lfx}", W["ln_g"][l])
            lnb = load_vec_cols(p, f"l0f_lnb_{lfx}", W["ln_b"][l])
            hn = [p.sb(f"l0fhn{i}_{lfx}", (128, T), BF16) for i in range(NI)]
            t32b = [p.sb(f"l0ftb{k}_{lfx}", (128, TC), F32) for k in range(2)]
            for (t0, tl) in tchunks(T):
                rep_row(p, cm, P[2][:, :tl], rows2["ra"][:, t0:t0 + tl])
                rep_row(p, cm, P[3][:, :tl], rows2["rb"][:, t0:t0 + tl])
                for i in range(NI):
                    p.tt("vector", t32b[0][:, :tl], hws[i][:, t0:t0 + tl],
                         P[2][:, :tl], ALU.mult)
                    p.tt("vector", t32b[1][:, :tl], t32b[0][:, :tl], P[3][:, :tl],
                         ALU.add)
                    p.ts("vector", hn[i][:, t0:t0 + tl], t32b[1][:, :tl], lng[i],
                         ALU.mult, lnb[i], ALU.add, tag="l0f_hn")
            p.free(hws, rowsT2, *t32b)
            kk = [p.sb(f"l0fkk{o}_{lfx}", (128, T), BF16) for o in range(22)]
            gsc = [p.sb(f"l0fgs{k}_{lfx}", (128, TC), F32) for k in range(3)]
            w1buf = [p.sb(f"l0fw1{k}_{lfx}", (128, NI * 128), BF16) for k in range(2)]
            b1col = load_vec_cols(p, f"l0f_b1_{lfx}", W["b1p"][l], n=5632)
            for phase, orng in (("g", range(22, 44)), ("a", range(0, 22))):
                for o in orng:
                    wb = w1buf[o % 2]
                    p.dma("gpsimd", wb[:, :].rearrange("p (i n) -> p i n", i=NI),
                          W["w1p"][l][:, o * 128:(o + 1) * 128]
                          .rearrange("(i p) n -> p i n", p=128), tag="l0f_w1l")
                    for (t0, tl) in tchunks(T):
                        ps = P[2 + o % 2]
                        p.mm_acc(ps[:, :tl],
                                 [(wb[:, i * 128:(i + 1) * 128], hn[i][:, t0:t0 + tl])
                                  for i in range(NI)], tag="l0f_w1")
                        if phase == "g":
                            p.ts("vector", gsc[0][:, :tl], ps[:, :tl],
                                 b1col[o], ALU.add, tag="l0f_gb")
                            p.act(kk[o - 22][:, t0:t0 + tl], gsc[0][:, :tl], AF.Gelu,
                                  tag="l0f_gel")
                        else:
                            p.ts("vector", gsc[0][:, :tl], ps[:, :tl],
                                 b1col[o], ALU.add, tag="l0f_ab")
                            p.tt("vector", kk[o][:, t0:t0 + tl], gsc[0][:, :tl],
                                 kk[o][:, t0:t0 + tl], ALU.mult, tag="l0f_agel")
            p.free(w1buf, hn)
            w2t = []
            for ic in range(22):
                t = p.sb(f"l0fw2{ic}_{lfx}", (128, D), BF16)
                p.dma("gpsimd", t[:, :], W["w2p"][l][ic * 128:(ic + 1) * 128, :],
                      tag="l0f_w2l")
                w2t.append(t)
            b2col = load_vec_cols(p, f"l0f_b2_{lfx}", W["b2"][l])
            for (t0, tl) in tchunks(T):
                for o in range(NI):
                    ps = P[o % 2]
                    p.mm_acc(ps[:, :tl],
                             [(w2t[ic][:, o * 128:(o + 1) * 128], kk[ic][:, t0:t0 + tl])
                              for ic in range(22)], tag="l0f_w2")
                    zl = gsc[1 + o % 2]
                    p.dma("sync", zl[:, :tl], zmid_hbm[o * 128:(o + 1) * 128, t0:t0 + tl],
                          tag="l0f_ldz4")
                    p.ts("vector", gsc[0][:, :tl], ps[:, :tl], b2col[o], ALU.add,
                         tag="l0f_b2")
                    p.tt("vector", zl[:, :tl], zl[:, :tl], gsc[0][:, :tl], ALU.add,
                         tag="l0f_z3")
                    p.dma("scalar", zout[o * 128:(o + 1) * 128, t0:t0 + tl],
                          zl[:, :tl], tag="l0f_spz3")
            p.free(w2t, kk, gsc)
            zin, zout = zout, zin
        fb = [p.sb(f"l0f_cb{k}_{sfx}", (128, TC), F32) for k in range(4)]
        for (t0, tl) in tchunks(T):
            for o in range(NI):
                k = o % 2
                p.dma("sync", fb[k][:, :tl], zin[o * 128:(o + 1) * 128, t0:t0 + tl],
                      tag="l0f_ldf")
                p.dma("sync", fb[2 + k][:, :tl],
                      x_hbm[o * 128:(o + 1) * 128, s * T + t0:s * T + t0 + tl],
                      tag="l0f_ldxf")
                p.tt("vector", fb[k][:, :tl], fb[k][:, :tl], fb[2 + k][:, :tl],
                     ALU.add, tag="l0f_cx")
                p.dma("sync", fb[2 + k][:, :tl],
                      a_hbm[o * 128:(o + 1) * 128, s * T + t0:s * T + t0 + tl],
                      tag="l0f_ldaf")
                p.tt("vector", fb[k][:, :tl], fb[k][:, :tl], fb[2 + k][:, :tl],
                     ALU.add, tag="l0f_ca")
                p.dma("sync", out_hbm[o * 128:(o + 1) * 128, s * T + t0:s * T + t0 + tl],
                      fb[k][:, :tl], tag="l0f_out")
        p.free(fb)


def build_l3_attn(p: Prog, cm: Common, x_hbm, W, out_hbm, Tloc=4096, nseq=2):
    T = Tloc // nseq
    P = cm.P
    H, K = 16, 64
    NCk = T // 128

    maa = {nm: load_vec_cols(p, f"l3a_maa{nm}", W[f"maa_{nm}"])
           for nm in ("x", "w", "k", "v", "r", "g")}
    tdcol = load_vec_cols(p, "l3a_td", W["td"])
    lngrow = p.sb("l3a_lngrow", (1, D), F32)
    p.dma("sync", lngrow[:, :], W["ln_gamma"].rearrange("(o d) -> o d", o=1),
          tag="l3a_lngr")
    lnbrow = p.sb("l3a_lnbrow", (1, D), F32)
    p.dma("sync", lnbrow[:, :], W["ln_beta"].rearrange("(o d) -> o d", o=1),
          tag="l3a_lnbr")
    hrow = p.sb("l3a_hrow", (1, 64), F32)
    p.dma("sync", hrow[:, 0:16], W["hdb"].rearrange("(o h) -> o h", o=1), tag="l3a_hdb")
    p.dma("sync", hrow[:, 16:32], W["sscale"].rearrange("(o h) -> o h", o=1),
          tag="l3a_ss")
    p.act(hrow[:, 0:16], hrow[:, 0:16], AF.Exp, tag="l3a_ehdb")
    p.act(hrow[:, 16:32], hrow[:, 16:32], AF.Sigmoid, tag="l3a_sss")
    arow = p.sb("l3a_arow", (1, D), F32)
    p.dma("sync", arow[:, :], W["dmix"].rearrange("(o d) -> o d", o=1), tag="l3a_dmix")
    p.act(arow[:, :], arow[:, :], AF.Sigmoid, tag="l3a_alpha")
    arep = p.sb("l3a_arep", (128, H * K), F32)
    for h in range(H):
        p.mm(P[6][:, 0:K], cm.ones_row[:, :], arow[:, h * K:(h + 1) * K], tag="l3a_ar")
        p.copy("scalar", arep[:, h * K:(h + 1) * K], P[6][:, 0:K], tag="l3a_are")
    triincl = p.sb("l3a_trii", (128, 128), BF16)
    tri_np = (np.arange(128)[:, None] <= np.arange(128)[None, :])
    p.dma("sync", triincl[:, :], p.const("l3a_triih", tri_np.astype(ml_dtypes.bfloat16)),
          tag="l3a_trii")
    identf = cm.ident
    ident32 = p.sb("l3a_id32", (128, 128), F32)
    p.dma("sync", ident32[:, :], p.const("l3a_id32h", np.eye(128, dtype=np.float32)),
          tag="l3a_id32")

    for s in range(nseq):
        y_hbm = p.dscratch(f"l3a_y_s{s}", (T, D), BF16)
        g_hbm = p.dscratch(f"l3a_g_s{s}", (T, D), BF16)
        xcols = x_hbm[:, s * T:(s + 1) * T]
        sfx = f"s{s}"
        xful = [p.sb(f"l3ax{i}_{sfx}", (128, T), F32) for i in range(NI)]
        for i in range(NI):
            p.dma("sync", xful[i][:, :], xcols[i * 128:(i + 1) * 128, :], tag="l3a_ldx")
        rowsT, rows = ln_rows(p, cm, xful, T, f"l3aln_{sfx}")
        xnp = [p.sb(f"l3an{i}_{sfx}", (128, T + 2), BF16) for i in range(NI)]
        t32 = [p.sb(f"l3at{k}_{sfx}", (128, TC), F32) for k in range(3)]
        for i in range(NI):
            p.memset("vector", xnp[i][:, 0:1], 0.0, tag="l3a_p0")
            p.memset("vector", xnp[i][:, T + 1:T + 2], 0.0, tag="l3a_p1")
        for (t0, tl) in tchunks(T):
            rep_row(p, cm, P[2][:, :tl], rows["ra"][:, t0:t0 + tl])
            rep_row(p, cm, P[3][:, :tl], rows["rb"][:, t0:t0 + tl])
            for i in range(NI):
                p.tt("vector", t32[0][:, :tl], xful[i][:, t0:t0 + tl], P[2][:, :tl],
                     ALU.mult)
                p.tt("vector", xnp[i][:, 1 + t0:1 + t0 + tl], t32[0][:, :tl],
                     P[3][:, :tl], ALU.add, tag="l3a_xn")
        p.free(xful, rowsT)
        dx = [p.sb(f"l3ad{i}_{sfx}", (128, T), BF16) for i in range(NI)]
        for i in range(NI):
            for (t0, tl) in tchunks(T):
                p.tt("vector", t32[0][:, :tl], xnp[i][:, t0:t0 + tl],
                     xnp[i][:, t0 + 2:t0 + 2 + tl], ALU.add, tag="l3a_pn")
                p.stt("vector", dx[i][:, t0:t0 + tl], t32[0][:, :tl], 0.5,
                      xnp[i][:, 1 + t0:1 + t0 + tl], ALU.mult, ALU.subtract,
                      tag="l3a_dx")
        xmix = [p.sb(f"l3am{i}_{sfx}", (128, T), BF16) for i in range(NI)]
        for i in range(NI):
            for (t0, tl) in tchunks(T):
                p.stt("vector", t32[0][:, :tl], dx[i][:, t0:t0 + tl], maa["x"][i],
                      xnp[i][:, 1 + t0:1 + t0 + tl], ALU.mult, ALU.add, tag="l3a_xxx")
                p.copy("vector", xmix[i][:, t0:t0 + tl], t32[0][:, :tl], tag="l3a_xxc")
        w1t = load_w_bf16(p, f"l3a_w1_{sfx}", W["w1p"], I=D, O=640)
        mm5 = [p.sb(f"l3amm{c}_{sfx}", (128, T), BF16) for c in range(5)]
        for (t0, tl) in tchunks(T):
            for o in range(5):
                ps = P[o % 2]
                p.mm_acc(ps[:, :tl],
                         [(w1t[i][:, o * 128:(o + 1) * 128], xmix[i][:, t0:t0 + tl])
                          for i in range(NI)], tag="l3a_mix1")
                p.act(mm5[o][:, t0:t0 + tl], ps[:, :tl], AF.Tanh, tag="l3a_tanh")
        p.free(w1t, xmix)
        w2in = W["w2"]
        r_img = p.dscratch(f"l3a_rimg_{sfx}", (D, T), BF16)
        k_img = p.dscratch(f"l3a_kimg_{sfx}", (D, T), BF16)
        kt_img = p.dscratch(f"l3a_ktimg_{sfx}", (T, D), BF16)
        vt_img = p.dscratch(f"l3a_vtimg_{sfx}", (T, D), BF16)
        if s == 0:
            p._taps = {"r": r_img, "k": k_img, "kt": kt_img, "vt": vt_img,
                       "y": y_hbm, "g": g_hbm}
        stg = [p.sb(f"l3astg{k}_{sfx}", (128, TC), BF16) for k in range(2)]
        stgt = [p.sb(f"l3astgt{k}_{sfx}", (128, D), BF16) for k in range(2)]
        wtm = [p.sb(f"l3awt{c}_{sfx}", (128, D), F32) for c in range(2)]
        mwc = p.sb(f"l3amw_{sfx}", (128, NCk * H), F32)
        xX = [p.sb(f"l3aX{i}_{sfx}", (128, T), BF16) for i in range(NI)]

        def make_xX(nm, fidx):
            for (t0, tl) in tchunks(T):
                for o in range(NI):
                    ps = P[4 + o % 2]
                    p.mm(ps[:, :tl],
                         w2f[fidx][:, o * 128:(o + 1) * 128],
                         mm5[fidx][0:32, t0:t0 + tl], tag="l3a_m2")
                    p.ts("vector", t32[0][:, :tl], ps[:, :tl], maa[nm][o], ALU.add,
                         tag="l3a_mxa")
                    p.tt("vector", t32[1][:, :tl], t32[0][:, :tl], dx[o][:, t0:t0 + tl],
                         ALU.mult, tag="l3a_mxm")
                    p.tt("vector", t32[2][:, :tl], t32[1][:, :tl],
                         xnp[o][:, 1 + t0:1 + t0 + tl], ALU.add, tag="l3a_mxx")
                    p.copy("vector", xX[o][:, t0:t0 + tl], t32[2][:, :tl], tag="l3a_mxc")

        w2f = []
        for f in range(5):
            t = p.sb(f"l3aw2{f}_{sfx}", (32, D), BF16)
            p.dma("gpsimd", t[:, :], w2in[f], tag="l3a_w2l")
            w2f.append(t)

        make_xX("r", 3)
        Wr = load_w_bf16(p, f"l3a_Wr_{sfx}", W["W_r"])
        for (t0, tl) in tchunks(T):
            for o in range(NI):
                ps = P[o % 2]
                p.mm_acc(ps[:, :tl],
                         [(Wr[i][:, o * 128:(o + 1) * 128], xX[i][:, t0:t0 + tl])
                          for i in range(NI)], tag="l3a_pr")
                sg = stg[o % 2]
                p.ts("vector", sg[:, :tl], ps[:, :tl], 0.125, ALU.mult, tag="l3a_pre")
                p.dma("scalar", r_img[o * 128:(o + 1) * 128, t0:t0 + tl], sg[:, :tl],
                      tag="l3a_spr")
        p.free(Wr)
        make_xX("k", 1)
        Wk = load_w_bf16(p, f"l3a_Wk_{sfx}", W["W_k"])
        for (t0, tl) in tchunks(T):
            for o in range(NI):
                ps = P[o % 2]
                p.mm_acc(ps[:, :tl],
                         [(Wk[i][:, o * 128:(o + 1) * 128], xX[i][:, t0:t0 + tl])
                          for i in range(NI)], tag="l3a_pk")
                sg = stg[o % 2]
                p.copy("vector", sg[:, :tl], ps[:, :tl], tag="l3a_pke")
                p.dma("scalar", k_img[o * 128:(o + 1) * 128, t0:t0 + tl], sg[:, :tl],
                      tag="l3a_spk")
        for c in range(NCk):
            sg = stgt[c % 2]
            for oc in range(D // TC):
                ps = P[2 + oc % 2]
                p.mm_acc(ps[:, 0:TC],
                         [(xX[i][:, c * 128:(c + 1) * 128],
                           Wk[i][:, oc * TC:(oc + 1) * TC]) for i in range(NI)],
                         tag="l3a_pkt")
                p.copy("vector", sg[:, oc * TC:(oc + 1) * TC], ps[:, 0:TC],
                       tag="l3a_pkte")
            p.dma("scalar", kt_img[c * 128:(c + 1) * 128, :], sg[:, :], tag="l3a_spkt")
        p.free(Wk)
        make_xX("v", 2)
        Wv = load_w_bf16(p, f"l3a_Wv_{sfx}", W["W_v"])
        for c in range(NCk):
            sg = stgt[c % 2]
            for oc in range(D // TC):
                ps = P[2 + oc % 2]
                p.mm_acc(ps[:, 0:TC],
                         [(xX[i][:, c * 128:(c + 1) * 128],
                           Wv[i][:, oc * TC:(oc + 1) * TC]) for i in range(NI)],
                         tag="l3a_pv")
                p.copy("vector", sg[:, oc * TC:(oc + 1) * TC], ps[:, 0:TC],
                       tag="l3a_pve")
            p.dma("scalar", vt_img[c * 128:(c + 1) * 128, :], sg[:, :], tag="l3a_spvt")
        p.free(Wv)
        make_xX("g", 4)
        Wg = load_w_bf16(p, f"l3a_Wg_{sfx}", W["W_g"])
        gt = p.sb(f"l3ag_{sfx}", (128, D), BF16)
        for c in range(NCk):
            for oc in range(D // TC):
                ps = P[2 + oc % 2]
                p.mm_acc(ps[:, 0:TC],
                         [(xX[i][:, c * 128:(c + 1) * 128],
                           Wg[i][:, oc * TC:(oc + 1) * TC]) for i in range(NI)],
                         tag="l3a_pg")
                p.act(gt[:, oc * TC:(oc + 1) * TC], ps[:, 0:TC], AF.Silu,
                      tag="l3a_pge")
            p.dma("scalar", g_hbm[c * 128:(c + 1) * 128, :], gt[:, :], tag="l3a_spg")
        p.free(Wg)
        make_xX("w", 0)
        Wt1 = load_w_bf16(p, f"l3a_tdw1_{sfx}", W["tdw1"], I=D, O=64)
        ufm = p.sb(f"l3au_{sfx}", (128, T), BF16)
        for (t0, tl) in tchunks(T):
            ps = P[0]
            p.mm_acc(ps[0:64, :tl],
                     [(Wt1[i][:, 0:64], xX[i][:, t0:t0 + tl]) for i in range(NI)],
                     tag="l3a_td1")
            p.act(ufm[0:64, t0:t0 + tl], ps[0:64, :tl], AF.Tanh, tag="l3a_td1e")
        p.free(Wt1)
        Wt2 = p.sb(f"l3a_tdw2_{sfx}", (64, D), BF16)
        p.dma("gpsimd", Wt2[:, :], W["tdw2"], tag="l3a_td2l")
        tdrow = p.sb(f"l3a_tdrow_{sfx}", (1, D), F32)
        p.dma("sync", tdrow[:, :], W["td"].rearrange("(o d) -> o d", o=1), tag="l3a_tdr")
        hrep = p.sb(f"l3a_hrep_{sfx}", (128, 2 * H), F32)
        p.mm(P[6][:, 0:32], cm.ones_row[:, :], hrow[:, 0:32], tag="l3a_hr")
        p.copy("scalar", hrep[:, :], P[6][:, 0:32], tag="l3a_hre")
        for c in range(NCk):
            for oc in range(2):
                ps = P[2 + oc]
                p.mm(ps[:, 0:TC], ufm[0:64, c * 128:(c + 1) * 128],
                     Wt2[:, oc * TC:(oc + 1) * TC], tag="l3a_td2")
                p.mm(P[4 + oc][:, 0:TC], cm.ones_row[:, :],
                     tdrow[:, oc * TC:(oc + 1) * TC], tag="l3a_tdrep")
                p.copy("scalar", wtm[oc][:, 0:TC], P[4 + oc][:, 0:TC], tag="l3a_tdre")
                p.tt("vector", wtm[oc][:, 0:TC], wtm[oc][:, 0:TC], ps[:, 0:TC],
                     ALU.add, tag="l3a_wadd")
            wfull = wtm[0]
            for oc in range(2):
                p.act(wtm[oc][:, 0:TC], wtm[oc][:, 0:TC], AF.Exp, tag="l3a_wexp")
            mw1 = P[6]
            p.reduce("vector", t32[0][:, 0:H], 
                     bass.AP(tensor=wtm[0].tensor if False else None, offset=0, ap=[[1,1]])
                     if False else None, ALU.add) if False else None
            p.reduce("vector", mwc[:, c * H:c * H + 8],
                     wtm[0][:, 0:TC].rearrange("p (h k) -> p h k", k=K), ALU.add,
                     tag="l3a_mw0")
            p.reduce("vector", mwc[:, c * H + 8:c * H + 16],
                     wtm[1][:, 0:TC].rearrange("p (h k) -> p h k", k=K), ALU.add,
                     tag="l3a_mw1")
        p.tt("vector", mwc[:, :],
             mwc[:, :].rearrange("p (c h) -> p c h", h=H),
             hrep[:, 0:H].rearrange("p (o h) -> p o h", o=1).broadcast_to([128, NCk, H]),
             ALU.mult, tag="l3a_mwh") if False else None
        p.tt("vector", mwc[:, :].rearrange("p (c h) -> p c h", h=H),
             mwc[:, :].rearrange("p (c h) -> p c h", h=H),
             hrep[:, 0:H].rearrange("p (o h) -> p o h", o=1).broadcast_to([128, NCk, H]),
             ALU.mult, tag="l3a_mwh")
        p.ts("vector", mwc[:, :], mwc[:, :], -1.0 / K, ALU.mult, tag="l3a_mwn")
        Cf = p.sb(f"l3a_Cf_{sfx}", (128, NCk * H), F32)
        Cs = p.sb(f"l3a_Cs_{sfx}", (128, NCk * H), F32)
        carry = p.sb(f"l3a_carry_{sfx}", (1, H), F32)
        mwb = p.sb(f"l3a_mwb_{sfx}", (128, H), BF16)
        p.memset("vector", carry[:, :], 0.0, tag="l3a_c0")
        for c in range(NCk):
            p.copy("vector", mwb[:, :], mwc[:, c * H:(c + 1) * H], tag="l3a_cb")
            p.mm(P[4][:, 0:H], triincl[:, :], mwb[:, :], tag="l3a_cums")
            p.mm(P[5][:, 0:H], cm.ones_row[:, :], carry[:, :], tag="l3a_crep")
            p.copy("scalar", t32[0][:, 0:H], P[5][:, 0:H], tag="l3a_cre")
            p.tt("vector", Cf[:, c * H:(c + 1) * H], P[4][:, 0:H], t32[0][:, 0:H],
                 ALU.add, tag="l3a_cadd")
            p.mm(P[5][0:1, 0:H], cm.ones_col16[:, :], mwb[:, :], tag="l3a_csum")
            p.copy("vector", t32[1][0:1, 0:H], P[5][0:1, 0:H], tag="l3a_cse")
            p.tt("vector", carry[:, :], carry[:, :], t32[1][0:1, 0:H], ALU.add,
                 tag="l3a_cupd")
        p.tt("vector", Cs[:, :].rearrange("p (c h) -> p c h", h=H),
             Cf[:, :].rearrange("p (c h) -> p c h", h=H),
             hrep[:, H:2 * H].rearrange("p (o h) -> p o h", o=1)
             .broadcast_to([128, NCk, H]), ALU.mult, tag="l3a_cs")

        if s == 0:
            cf_tap = p.dscratch("l3a_cftap", (128, NCk * H), F32)
            p.dma("scalar", cf_tap, Cf[:, :], tag="dbg_cf")
            p._taps["cf"] = cf_tap
        p.free(mm5, dx, xX, xnp, w2f, ufm, Wt2, tdrow, wtm, stg, stgt, mwb)
        ytm = [p.sb(f"l3ay{c}_{sfx}", (128, D), BF16) for c in range(NCk)]
        rhb = p.sb(f"l3a_rhb_{sfx}", (64, T), BF16)
        khb = p.sb(f"l3a_khb_{sfx}", (64, T), BF16)
        kthb = p.sb(f"l3a_kthb_{sfx}", (128, NCk * K), BF16)
        vthb = p.sb(f"l3a_vthb_{sfx}", (128, NCk * K), BF16)
        Zt = p.sb(f"l3a_Z_{sfx}", (64, 2 * K), F32)
        zb16 = p.sb(f"l3a_zb_{sfx}", (64, 2 * K), BF16)
        sck = [p.sb(f"l3a_sk{j}_{sfx}", (128, K), BF16) for j in range(2)]
        scq = [p.sb(f"l3a_sq{j}_{sfx}", (128, 128), F32) for j in range(2)]
        expD = [p.sb(f"l3a_eD{j}_{sfx}", (128, 128), BF16) for j in range(2)]
        colb = p.sb(f"l3a_colb_{sfx}", (128, 8), F32)
        cc16 = p.sb(f"l3a_cc16_{sfx}", (128, 1), BF16)
        crows = p.sb(f"l3a_crows_{sfx}", (2, 2 * T), F32)
        zsc = p.sb(f"l3a_zsc_{sfx}", (64, 2 * K), F32)
        yac = p.sb(f"l3a_yac_{sfx}", (128, K), F32)
        t32s = p.sb(f"l3a_t32s_{sfx}", (128, K), F32)
        for h in range(H):
            p.dma("sync", rhb[:, :], r_img[h * 64:(h + 1) * 64, :], tag="l3a_ldr")
            p.dma("sync", khb[:, :], k_img[h * 64:(h + 1) * 64, :], tag="l3a_ldk")
            p.dma("sync", kthb[:, :].rearrange("p (c e) -> p c e", c=NCk),
                  kt_img.rearrange("(c p) e -> p c e", p=128)[:, :, h * K:(h + 1) * K],
                  tag="l3a_ldkt")
            p.dma("sync", vthb[:, :].rearrange("p (c e) -> p c e", c=NCk),
                  vt_img.rearrange("(c p) e -> p c e", p=128)[:, :, h * K:(h + 1) * K],
                  tag="l3a_ldvt")
            for c in range(NCk):
                p.mm(P[6][0:1, 0:128], Cf[:, c * H + h:c * H + h + 1], ident32[:, :],
                     tag="l3a_crf")
                p.copy("vector", crows[0:1, c * 128:(c + 1) * 128], P[6][0:1, 0:128],
                       tag="l3a_crfe")
                p.mm(P[6][0:1, 0:128], Cs[:, c * H + h:c * H + h + 1], ident32[:, :],
                     tag="l3a_crs")
                p.copy("vector", crows[0:1, T + c * 128:T + (c + 1) * 128],
                       P[6][0:1, 0:128], tag="l3a_crse")
            for direction in (0, 1):
                p.memset("vector", Zt[:, :], 0.0, tag="l3a_z0")
                seq = list(range(NCk)) if direction == 0 else list(range(NCk - 1, -1, -1))
                sgn = 1.0 if direction == 0 else -1.0
                for c in seq:
                    CfI = Cf[:, c * H + h:c * H + h + 1]
                    CsI = Cs[:, c * H + h:c * H + h + 1]
                    fr = c * 128 if direction == 0 else c * 128 + 127
                    p.mm(P[6][:, 0:1], cm.ones_row[:, :], crows[0:1, fr:fr + 1],
                         tag="l3a_frf")
                    p.mm(P[6][:, 1:2], cm.ones_row[:, :], crows[0:1, T + fr:T + fr + 1],
                         tag="l3a_frs")
                    p.copy("vector", colb[:, 0:2], P[6][:, 0:2], tag="l3a_fre")
                    p.tt("vector", colb[:, 2:3], CfI, colb[:, 0:1], ALU.subtract,
                         tag="l3a_d1")
                    p.tt("vector", colb[:, 3:4], CsI, colb[:, 1:2], ALU.subtract,
                         tag="l3a_d2")
                    p.act(colb[:, 4:5], colb[:, 2:3], AF.Exp, scale=sgn, tag="l3a_af")
                    p.act(colb[:, 5:6], colb[:, 3:4], AF.Exp, scale=sgn, tag="l3a_as")
                    rsl = rhb[:, c * 128:(c + 1) * 128]
                    vsl = vthb[:, c * K:(c + 1) * K]
                    p.copy("vector", zb16[:, :], Zt[:, :], tag="l3a_zc")
                    p.mm(P[4][:, 0:K], rsl, zb16[0:64, 0:K], tag="l3a_yf")
                    p.mm(P[4][:, K:2 * K], rsl, zb16[0:64, K:2 * K], tag="l3a_ys")
                    if direction == 0:
                        ksl = khb[:, c * 128:(c + 1) * 128]
                        p.mm(P[5][:, 0:128], ksl, rsl, tag="l3a_S")
                        p.mm(P[6][:, 0:128], cm.ones_row[:, :],
                             crows[0:1, c * 128:(c + 1) * 128], tag="l3a_crep")
                        p.copy("scalar", scq[0][:, :], P[6][:, 0:128], tag="l3a_crt")
                        p.ts("vector", scq[1][:, :], scq[0][:, :], CfI, ALU.subtract,
                             tag="l3a_dsub")
                        p.act(scq[1][:, :], scq[1][:, :], AF.Abs, tag="l3a_dabs")
                        p.act(expD[0][:, :], scq[1][:, :], AF.Exp, scale=-1.0,
                              tag="l3a_dexp")
                        p.ts("vector", scq[1][:, :], scq[1][:, :],
                             hrep[0:128, H + h:H + h + 1], ALU.mult, tag="l3a_dss")
                        p.act(expD[1][:, :], scq[1][:, :], AF.Exp, scale=-1.0,
                              tag="l3a_dexps")
                        p.copy("scalar", scq[0][:, :], P[5][:, 0:128], tag="l3a_Se")
                        p.tt("vector", expD[0][:, :], scq[0][:, :], expD[0][:, :],
                             ALU.mult, tag="l3a_sdf")
                        p.tt("vector", expD[1][:, :], scq[0][:, :], expD[1][:, :],
                             ALU.mult, tag="l3a_sds")
                        p.mm(P[6][:, 64:64 + K], expD[0][:, :], vsl, tag="l3a_ydf")
                        p.mm(P[6][:, 192:192 + K], expD[1][:, :], vsl, tag="l3a_yds")
                    p.copy("scalar", scq[0][:, 0:2 * K], P[4][:, 0:2 * K], tag="l3a_ye")
                    p.ts("vector", scq[0][:, 0:K], scq[0][:, 0:K], colb[:, 4:5],
                         ALU.mult, tag="l3a_yaf")
                    p.ts("vector", scq[0][:, K:2 * K], scq[0][:, K:2 * K],
                         colb[:, 5:6], ALU.mult, tag="l3a_yas")
                    if direction == 0:
                        p.tt("vector", scq[0][:, 0:K], scq[0][:, 0:K],
                             P[6][:, 64:64 + K], ALU.add, tag="l3a_ydadd")
                        p.tt("vector", scq[0][:, K:2 * K], scq[0][:, K:2 * K],
                             P[6][:, 192:192 + K], ALU.add, tag="l3a_ydadds")
                    p.tt("vector", yac[:, :], scq[0][:, 0:K], scq[0][:, K:2 * K],
                         ALU.subtract, tag="l3a_fms")
                    p.tt("vector", yac[:, :], yac[:, :], arep[:, h * K:(h + 1) * K],
                         ALU.mult, tag="l3a_bl")
                    p.tt("vector", yac[:, :], yac[:, :], scq[0][:, K:2 * K], ALU.add,
                         tag="l3a_bl2")
                    if direction == 0:
                        p.copy("vector", ytm[c][:, h * K:(h + 1) * K], yac[:, :],
                               tag="l3a_yst")
                    else:
                        p.tt("vector", ytm[c][:, h * K:(h + 1) * K],
                             ytm[c][:, h * K:(h + 1) * K], yac[:, :], ALU.add,
                             tag="l3a_yst2")
                    cn = c + 1 if direction == 0 else c - 1
                    if 0 <= cn < NCk:
                        frn = cn * 128 if direction == 0 else cn * 128 + 127
                        p.mm(P[6][:, 2:3], cm.ones_row[:, :],
                             crows[0:1, frn:frn + 1], tag="l3a_fn1")
                        p.mm(P[6][:, 3:4], cm.ones_row[:, :],
                             crows[0:1, T + frn:T + frn + 1], tag="l3a_fn2")
                        p.copy("vector", colb[:, 6:8], P[6][:, 2:4], tag="l3a_fne")
                        p.tt("vector", colb[:, 2:3], colb[:, 6:7], CfI, ALU.subtract,
                             tag="l3a_kd1")
                        p.tt("vector", colb[:, 3:4], colb[:, 7:8], CsI, ALU.subtract,
                             tag="l3a_kd2")
                        p.act(colb[:, 2:3], colb[:, 2:3], AF.Exp, scale=sgn,
                              tag="l3a_kf")
                        p.act(colb[:, 3:4], colb[:, 3:4], AF.Exp, scale=sgn,
                              tag="l3a_ks")
                        ksl_tm = kthb[:, c * K:(c + 1) * K]
                        p.ts("vector", sck[0][:, 0:K], ksl_tm, colb[:, 2:3], ALU.mult,
                             tag="l3a_ksc1")
                        p.ts("vector", sck[1][:, 0:K], ksl_tm, colb[:, 3:4], ALU.mult,
                             tag="l3a_ksc2")
                        p.mm(P[5][0:K, 0:K], sck[0][:, 0:K], vsl, tag="l3a_zf")
                        p.mm(P[5][0:K, K:2 * K], sck[1][:, 0:K], vsl, tag="l3a_zs")
                        p.tt("vector", colb[:, 6:7], colb[:, 6:7], colb[:, 0:1],
                             ALU.subtract, tag="l3a_md1")
                        p.tt("vector", colb[:, 7:8], colb[:, 7:8], colb[:, 1:2],
                             ALU.subtract, tag="l3a_md2")
                        p.act(colb[:, 6:7], colb[:, 6:7], AF.Exp, scale=sgn,
                              tag="l3a_m1")
                        p.act(colb[:, 7:8], colb[:, 7:8], AF.Exp, scale=sgn,
                              tag="l3a_m2")
                        p.ts("vector", zsc[:, 0:K], Zt[:, 0:K], colb[0:64, 6:7],
                             ALU.mult, tag="l3a_zm1")
                        p.ts("vector", zsc[:, K:2 * K], Zt[:, K:2 * K],
                             colb[0:64, 7:8], ALU.mult, tag="l3a_zm2")
                        p.tt("vector", Zt[:, :], zsc[:, :], P[5][0:K, 0:2 * K],
                             ALU.add, tag="l3a_zu")
        p.free(rhb, khb, kthb, vthb, Cf, Cs, carry, sck, scq, expD, colb, cc16,
               crows, zsc, yac, t32s, Zt, zb16, hrep, mwc, t32)
        EPSGN = 1e-5 * 64.0
        gnsc = [p.sb(f"l3agn{k}_{sfx}", (128, D), F32) for k in range(2)]
        stat2 = p.sb(f"l3ast_{sfx}", (128, 3 * H), F32)
        gld = p.sb(f"l3agld_{sfx}", (128, D), BF16)
        for c in range(NCk):
            ys = ytm[c]
            mloc = stat2[:, 0:H]; vloc = stat2[:, H:2 * H]
            p.reduce("vector", mloc, ys[:, :].rearrange("p (h k) -> p h k", k=K),
                     ALU.add, tag="l3a_gm")
            p.ts("vector", mloc, mloc, 1.0 / K, ALU.mult, tag="l3a_gm2")
            p.tt("vector", gnsc[0][:, :].rearrange("p (h k) -> p h k", k=K),
                 ys[:, :].rearrange("p (h k) -> p h k", k=K),
                 mloc.rearrange("p (h o) -> p h o", o=1).broadcast_to([128, H, K]),
                 ALU.subtract, tag="l3a_gc")
            p.tt("vector", gnsc[1][:, :], gnsc[0][:, :], gnsc[0][:, :], ALU.mult,
                 tag="l3a_g2")
            p.reduce("vector", vloc, gnsc[1][:, :].rearrange("p (h k) -> p h k", k=K),
                     ALU.add, tag="l3a_gv")
            p.ts("vector", vloc, vloc, 1.0 / K, ALU.mult, tag="l3a_gv2")
            p.act(vloc, vloc, AF.Sqrt, bias=p.kcol(EPSGN), tag="l3a_gs")
            p.recip(vloc, vloc, tag="l3a_gr")
            p.tt("vector", gnsc[0][:, :].rearrange("p (h k) -> p h k", k=K),
                 gnsc[0][:, :].rearrange("p (h k) -> p h k", k=K),
                 vloc.rearrange("p (h o) -> p h o", o=1).broadcast_to([128, H, K]),
                 ALU.mult, tag="l3a_gn")
            for oc in range(2):
                p.mm(P[4 + oc][:, 0:TC], cm.ones_row[:, :],
                     lngrow[:, oc * TC:(oc + 1) * TC], tag="l3a_ggr")
                p.tt("vector", gnsc[0][:, oc * TC:(oc + 1) * TC],
                     gnsc[0][:, oc * TC:(oc + 1) * TC], P[4 + oc][:, 0:TC],
                     ALU.mult, tag="l3a_gg")
                p.mm(P[4 + oc][:, 0:TC], cm.ones_row[:, :],
                     lnbrow[:, oc * TC:(oc + 1) * TC], tag="l3a_gbr")
                p.tt("vector", gnsc[0][:, oc * TC:(oc + 1) * TC],
                     gnsc[0][:, oc * TC:(oc + 1) * TC], P[4 + oc][:, 0:TC],
                     ALU.add, tag="l3a_gb")
            p.dma("sync", gld[:, :], g_hbm[c * 128:(c + 1) * 128, :], tag="l3a_gl")
            p.tt("vector", gnsc[0][:, :], gnsc[0][:, :], gld[:, :], ALU.mult,
                 tag="l3a_gmul")
            p.copy("vector", gld[:, :], gnsc[0][:, :], tag="l3a_ycast")
            p.dma("scalar", y_hbm[c * 128:(c + 1) * 128, :], gld[:, :], tag="l3a_spy")
        p.free(ytm, gnsc, stat2, gld)
        Wo = load_w_bf16(p, f"l3a_Wo_{sfx}", W["W_o"])
        ygf = [p.sb(f"l3aygf{i}_{sfx}", (128, T), BF16) for i in range(NI)]
        ybuf = p.sb(f"l3ayb_{sfx}", (128, D), BF16)
        tps = P[7][:, 0:64].bitcast(BF16)
        for c in range(NCk):
            p.dma("sync", ybuf[:, :], y_hbm[c * 128:(c + 1) * 128, :], tag="l3a_ldy")
            for i in range(NI):
                p.transpose(tps, ybuf[:, i * 128:(i + 1) * 128], cm.ident[:, :],
                            tag="l3a_tp")
                p.copy("vector", ygf[i][:, c * 128:(c + 1) * 128], tps, tag="l3a_tpe")
        ocols = out_hbm[:, s * T:(s + 1) * T]
        ot32 = p.sb(f"l3aot_{sfx}", (128, TC), F32)
        for (t0, tl) in tchunks(T):
            for o in range(NI):
                ps = P[o % 2]
                p.mm_acc(ps[:, :tl],
                         [(Wo[i][:, o * 128:(o + 1) * 128], ygf[i][:, t0:t0 + tl])
                          for i in range(NI)], tag="l3a_wo")
                p.copy("vector", ot32[:, :tl], ps[:, :tl], tag="l3a_woe")
                p.dma("sync", ocols[o * 128:(o + 1) * 128, t0:t0 + tl], ot32[:, :tl],
                      tag="l3a_out")
        p.free(Wo, ygf, ybuf, ot32)


def build_l2_attn(p: Prog, cm: Common, x_hbm, W, a_hbm, Tloc=4096, nseq=2, dbg=False):
    T = Tloc // nseq
    C = 128
    NCH = T // C
    P = cm.P
    H, HK, HV = 16, 64, 64

    qh_img = p.dscratch("l2a_qh", (1024, Tloc), BF16)
    kh_img = p.dscratch("l2a_kh", (1024, Tloc), BF16)
    v_img = p.dscratch("l2a_v", (1024, Tloc), BF16)
    gs_img = p.dscratch("l2a_gs", (1024, Tloc), BF16)
    o_img = p.dscratch("l2a_o", (1024, Tloc), BF16)

    lv_list = [64, 32, 16, 8]
    d0m = {}
    for Wl in lv_list + [C]:
        t = np.arange(T)
        m1 = (t % Wl == 0).astype(np.float32)
        d0 = 1.0 - m1
        d0m[Wl] = (p.const(f"l2a_d0_{Wl}", np.tile(d0, (64, 1)).astype(np.float32)),
                   p.const(f"l2a_m1_{Wl}", np.tile(m1, (64, 1)).astype(np.float32)))
    tl_ = np.arange(C)
    masks_h = {}
    for Wl in lv_list:
        M = (((tl_[:, None] // Wl) % 2 == 1) &
             ((tl_[None, :] // Wl) == (tl_[:, None] // Wl) - 1)).astype(np.float32)
        masks_h[f"M{Wl}"] = np.tile(M, (1, 4))
        masks_h[f"MT{Wl}"] = np.tile(M.T, (1, 4))
    Mi = ((tl_[:, None] // 8 == tl_[None, :] // 8) &
          (tl_[None, :] < tl_[:, None])).astype(np.float32)
    masks_h["M8i"] = np.tile(Mi, (1, 4))
    masks_h["MT8i"] = np.tile(Mi.T, (1, 4))
    msk = {}
    for k, v_ in masks_h.items():
        msk[k] = p.sb(f"l2a_{k}", (128, 512), I32)
        p.dma("sync", msk[k][:, :], p.const(f"l2a_{k}h",
              np.ascontiguousarray(v_.astype(np.int32))),
              tag=f"l2a_ld{k}")
    hsel_h = np.zeros((128, 2), ml_dtypes.bfloat16)
    hsel_h[0:64, 0] = 1; hsel_h[64:128, 1] = 1
    hsel = p.sb("l2a_hsel", (128, 2), BF16)
    p.dma("sync", hsel[:, :], p.const("l2a_hselh", hsel_h))
    hselT = p.sb("l2a_hselT", (2, 128), BF16)
    p.dma("sync", hselT[:, :], p.const("l2a_hselTh", np.ascontiguousarray(hsel_h.T)))
    ones_h = p.const("l2a_onesTh", np.ones((64, T), np.float32))

    dtt = p.sb("l2a_dt", (64, 16), F32)
    p.dma("sync", dtt[:, :], W["dt_bias"].rearrange("(h c) -> c h", c=64),
          tag="l2a_lddt", allow_slow_non_contiguous=True)
    alog = p.sb("l2a_alog", (1, 16), F32)
    p.dma("sync", alog[:, :], W["A_log"].rearrange("(o e) -> o e", o=1), tag="l2a_lda")
    expA = p.sb("l2a_expA", (1, 16), F32)
    p.act(expA[:, :], alog[:, :], AF.Exp, tag="l2a_expA")
    nacol = p.sb("l2a_nacol", (64, 16), F32)
    p.mm(P[2][0:64, 0:16], cm.ones_row[0:1, 0:64], expA[:, :], tag="l2a_acrep")
    p.ts("vector", nacol[:, :], P[2][0:64, 0:16], -1.0, ALU.mult, tag="l2a_nac")
    onw = p.sb("l2a_onw", (1, 64), F32)
    p.dma("sync", onw[:, :], W["o_norm_w"].rearrange("(o e) -> o e", o=1), tag="l2a_ldnw")
    wrep = p.sb("l2a_wrep", (128, 64), F32)
    p.mm(P[2][:, 0:64], cm.ones_row[0:1, :], onw[:, :], tag="l2a_nwrep")
    p.copy("vector", wrep[:, :], P[2][:, 0:64], tag="l2a_nwrepe")
    bg2c = load_vec_cols(p, "l2a_bg2", W["bg2"])
    convw = {}
    convb = {}
    for nm in ("q", "k", "v"):
        cw = p.sb(f"l2a_cw{nm}", (128, 32), F32)
        p.dma("sync", cw[:, :].rearrange("p (c k) -> p c k", c=8),
              W[nm + "cw"].rearrange("(c p) k -> p c k", p=128),
              tag=f"l2a_ldcw{nm}", allow_slow_non_contiguous=True)
        convw[nm] = cw
        convb[nm] = load_vec_cols(p, f"l2a_cb{nm}", W[nm + "cb"])
    Wf2t = p.sb("l2a_wf2", (64, 1024), BF16)
    p.dma("gpsimd", Wf2t[:, :], W["Wf2"], tag="l2a_ldwf2")
    Wg2t = p.sb("l2a_wg2", (64, 1024), BF16)
    p.dma("gpsimd", Wg2t[:, :], W["Wg2"], tag="l2a_ldwg2")
    Wf1t = load_w_bf16(p, "l2a_wf1", W["Wf1"], I=1024, O=64)
    Wg1t = load_w_bf16(p, "l2a_wg1", W["Wg1"], I=1024, O=64)
    Wbt = load_w_bf16(p, "l2a_wb", W["Wb"], I=1024, O=16)

    h1_b = [p.sb(f"l2a_h1_{b}", (64, T), BF16) for b in range(nseq)]
    beta_b = [p.sb(f"l2a_beta_{b}", (128, 16 * NCH), F32) for b in range(nseq)]

    ev = [p.sb(f"l2a_ev{k}", (128, TC), F32) for k in range(2)]
    ev16 = [p.sb(f"l2a_ev16{k}", (128, TC), BF16) for k in range(2)]
    hg = p.sb("l2a_hg", (64, T), BF16)
    for b in range(nseq):
        h0 = b * T
        xful = [p.sb(p.uname("l2a_xf"), (128, T), F32) for _ in range(NI)]
        for i in range(NI):
            p.dma("sync", xful[i][:, :], x_hbm[i * 128:(i + 1) * 128, h0:h0 + T],
                  tag="l2a_ldx")
        xn = ln_apply_bf16(p, cm, xful, T, f"l2aln{b}")
        p.free(*xful)
        cbuf = p.sb(p.uname("l2a_cbuf"), (128, T + 3), F32)
        ta = p.sb(p.uname("l2a_ta"), (128, TC), F32)
        tb = p.sb(p.uname("l2a_tb"), (128, TC), F32)
        sl = p.sb(p.uname("l2a_sl"), (128, TC), F32)
        sq16 = p.sb(p.uname("l2a_sq16"), (128, TC), BF16)
        nr = p.sb(p.uname("l2a_nr"), (2, TC), F32)
        nrf = p.sb(p.uname("l2a_nrf"), (2, TC), F32)
        nrr = p.sb(p.uname("l2a_nrr"), (2, TC), BF16)
        for nm, img in (("q", qh_img), ("k", kh_img), ("v", v_img)):
            wt = load_w_bf16(p, p.uname(f"l2a_w{nm}"), W["W" + nm])
            for oc in range(8):
                p.memset("vector", cbuf[:, 0:3], 0.0, tag="l2a_cz")
                for (t0, tl) in tchunks(T):
                    ps = P[(t0 // TC) % 2]
                    p.mm_acc(ps[:, :tl],
                             [(wt[i][:, oc * 128:(oc + 1) * 128], xn[i][:, t0:t0 + tl])
                              for i in range(NI)], tag="l2a_proj")
                    p.copy("scalar", cbuf[:, 3 + t0:3 + t0 + tl], ps[:, :tl],
                           tag="l2a_projev")
                for (t0, tl) in tchunks(T):
                    wc = [convw[nm][:, oc * 4 + j:oc * 4 + j + 1] for j in range(4)]
                    p.ts("vector", ta[:, :tl], cbuf[:, t0:t0 + tl], wc[0], ALU.mult,
                         tag="l2a_cv0")
                    p.stt("vector", tb[:, :tl], cbuf[:, 1 + t0:1 + t0 + tl], wc[1],
                          ta[:, :tl], ALU.mult, ALU.add, tag="l2a_cv1")
                    p.stt("vector", ta[:, :tl], cbuf[:, 2 + t0:2 + t0 + tl], wc[2],
                          tb[:, :tl], ALU.mult, ALU.add, tag="l2a_cv2")
                    p.stt("vector", tb[:, :tl], cbuf[:, 3 + t0:3 + t0 + tl], wc[3],
                          ta[:, :tl], ALU.mult, ALU.add, tag="l2a_cv3")
                    if nm == "v":
                        p.act(ev16[0][:, :tl], tb[:, :tl], AF.Silu,
                              bias=convb[nm][oc], tag="l2a_vsilu")
                        p.dma("scalar", img[oc * 128:(oc + 1) * 128, h0 + t0:h0 + t0 + tl],
                              ev16[0][:, :tl], tag="l2a_vspill")
                    else:
                        p.act(sl[:, :tl], tb[:, :tl], AF.Silu, bias=convb[nm][oc],
                              tag="l2a_qksilu")
                        p.act(sq16[:, :tl], sl[:, :tl], AF.Square, tag="l2a_sq")
                        p.mm(P[6][0:2, :tl], hsel[:, :], sq16[:, :tl], tag="l2a_nmm")
                        p.act(nr[:, :tl], P[6][0:2, :tl], AF.Sqrt, tag="l2a_nsqrt")
                        if nm == "q":
                            p.act(nr[:, :tl], nr[:, :tl], AF.Copy, bias=8e-6, scale=8.0,
                                  tag="l2a_nsc")
                        else:
                            p.act(nr[:, :tl], nr[:, :tl], AF.Copy, bias=1e-6, scale=1.0,
                                  tag="l2a_nsc")
                        p.recip(nrf[:, :tl], nr[:, :tl], tag="l2a_nrec")
                        p.copy("vector", nrr[:, :tl], nrf[:, :tl], tag="l2a_nrec16")
                        p.mm(P[2][:, :tl], hselT[:, :], nrr[:, :tl], tag="l2a_nrep")
                        p.tt("vector", ev16[0][:, :tl], sl[:, :tl], P[2][:, :tl],
                             ALU.mult, tag="l2a_qknorm")
                        p.dma("scalar", img[oc * 128:(oc + 1) * 128, h0 + t0:h0 + t0 + tl],
                              ev16[0][:, :tl], tag="l2a_qkspill")
            p.free(*wt)
        p.free(cbuf, ta, tb, sl, sq16, nr, nrf, nrr)
        for (t0, tl) in tchunks(T):
            p.mm_acc(P[0][0:64, :tl], [(Wf1t[i][:, :], xn[i][:, t0:t0 + tl]) for i in range(NI)],
                     tag="l2a_h1")
            p.copy("scalar", h1_b[b][:, t0:t0 + tl], P[0][0:64, :tl], tag="l2a_h1e")
            p.mm_acc(P[1][0:64, :tl], [(Wg1t[i][:, :], xn[i][:, t0:t0 + tl]) for i in range(NI)],
                     tag="l2a_hg")
            p.copy("scalar", hg[:, t0:t0 + tl], P[1][0:64, :tl], tag="l2a_hge")
        for oc in range(8):
            for (t0, tl) in tchunks(T):
                p.mm(P[(t0 // TC) % 2][:, :tl], Wg2t[:, oc * 128:(oc + 1) * 128],
                     hg[:, t0:t0 + tl], tag="l2a_gmm")
                p.act(ev16[1][:, :tl], P[(t0 // TC) % 2][:, :tl], AF.Sigmoid,
                      bias=bg2c[oc], tag="l2a_gsig")
                p.dma("scalar", gs_img[oc * 128:(oc + 1) * 128, h0 + t0:h0 + t0 + tl],
                      ev16[1][:, :tl], tag="l2a_gspill")
        for c in range(NCH):
            p.mm_acc(P[1][0:128, 0:16],
                     [(xn[i][:, c * 128:(c + 1) * 128], Wbt[i][:, :]) for i in range(NI)],
                     tag="l2a_bmm")
            p.act(beta_b[b][:, c * 16:(c + 1) * 16], P[1][0:128, 0:16], AF.Sigmoid,
                  tag="l2a_bsig")
        p.free(*xn)
    p.free(*Wf1t, *Wg1t, *Wbt, Wg2t, *ev, *ev16, hg)

    Dbuf = p.sb("l2a_D", (64, T + 1), F32)
    Fbuf = p.sb("l2a_F", (64, T + C), F32)
    tmpA = p.sb("l2a_tmpA", (64, T), F32)
    tmpB = p.sb("l2a_tmpB", (64, T), F32)
    d0s = p.sb("l2a_d0s", (64, T), F32)
    m1s_ = p.sb("l2a_m1s", (64, T), F32)
    qhs = p.sb("l2a_qhs", (64, T), BF16)
    khs = p.sb("l2a_khs", (64, T), BF16)
    vhs = p.sb("l2a_vhs", (64, T), BF16)
    gsl = p.sb("l2a_gsl", (64, T), BF16)
    ekt = p.sb("l2a_ekt", (64, T), BF16)
    ktf = p.sb("l2a_ktf", (64, T), BF16)
    qtf = p.sb("l2a_qtf", (64, T), BF16)
    ahh = p.sb("l2a_ah", (64, T), BF16)
    kt0 = p.sb("l2a_kt0", (64, T), BF16)
    qt0 = p.sb("l2a_qt0", (64, T), BF16)
    kEb = p.sb("l2a_kE", (64, T), BF16)
    qkb = p.sb("l2a_qkb", (64, T), BF16)
    App = [p.sb(f"l2a_App{k}", (128, T), F32) for k in range(2)]
    Btp = [p.sb(f"l2a_Btp{k}", (128, T), F32) for k in range(2)]
    Nt16 = p.sb("l2a_Nt16", (128, T), BF16)
    Bt16 = p.sb("l2a_Bt16", (128, T), BF16)
    ktm = p.sb("l2a_ktm", (128, NCH * 64), BF16)
    vtm = p.sb("l2a_vtm", (128, NCH * 64), F32)
    qkcols = p.sb("l2a_qkc", (128, NCH), F32)
    dcols = p.sb("l2a_dc", (64, NCH), F32)
    S0f = [p.sb(f"l2a_S0{k}", (64, 64), F32) for k in range(2)]
    S0b = p.sb("l2a_S0b", (64, 64), BF16)
    Pub = [p.sb(f"l2a_Pu{k}", (128, 128), BF16) for k in range(2)]
    Ptb = [p.sb(f"l2a_Pt{k}", (128, 128), BF16) for k in range(2)]
    Xb = [p.sb(f"l2a_X{k}", (128, 64), BF16) for k in range(2)]
    u0t = p.sb("l2a_u0t", (128, 64), F32)
    osb = p.sb("l2a_osb", (128, 64), F32)
    sq64 = p.sb("l2a_sq64", (128, 64), F32)
    ncol = p.sb("l2a_ncol", (128, 1), F32)
    ncol2 = p.sb("l2a_ncol2", (128, 1), F32)
    obb = p.sb("l2a_obb", (128, 64), BF16)
    otr16 = p.sb("l2a_otr16", (64, 128), BF16)
    uf32 = p.sb("l2a_uf32", (128, 64), F32)
    ofmh = p.sb("l2a_ofmh", (64, T), BF16)
    taps = {}

    for b in range(nseq):
        h0 = b * T
        for h in range(H):
            u = f"b{b}h{h}"
            p.dma("sync", qhs[:, :], qh_img[64 * h:64 * h + 64, h0:h0 + T], tag="l2a_lq")
            p.dma("sync", khs[:, :], kh_img[64 * h:64 * h + 64, h0:h0 + T], tag="l2a_lk")
            p.dma("sync", vhs[:, :], v_img[64 * h:64 * h + 64, h0:h0 + T], tag="l2a_lv")
            p.dma("sync", gsl[:, :], gs_img[64 * h:64 * h + 64, h0:h0 + T], tag="l2a_lg")
            for (t0, tl) in tchunks(T):
                ps = P[(t0 // TC) % 2]
                p.mm(ps[0:64, :tl], Wf2t[:, 64 * h:64 * h + 64], h1_b[b][:, t0:t0 + tl],
                     tag="l2a_graw")
                p.ts("vector", tmpA[:, t0:t0 + tl], ps[0:64, :tl], dtt[:, h:h + 1],
                     ALU.add, tag="l2a_gdt")
            p.act(tmpB[:, :], tmpA[:, :], AF.Exp, tag="l2a_sp1")
            p.act(tmpA[:, :], tmpB[:, :], AF.Ln, bias=1.0, tag="l2a_sp2")
            p.ts("vector", tmpB[:, :], tmpA[:, :], nacol[:, h:h + 1], ALU.mult,
                 tag="l2a_gsc")
            p.memset("vector", Dbuf[:, 0:1], 0.0, tag="l2a_d0z")
            p.dma("sync", d0s[:, :], ones_h, tag="l2a_ldones")
            p.scan(Dbuf[:, 1:T + 1], d0s[:, :], tmpB[:, :], 0.0, ALU.mult, ALU.add,
                   tag="l2a_dscan")
            Dcur = Dbuf[:, 1:T + 1]
            Dprev = Dbuf[:, 0:T]
            p.memset("vector", Fbuf[:, T:T + C], -1e30, tag="l2a_fext")
            p.memset("vector", App[0][:, :], 0.0, tag="l2a_az")
            p.memset("vector", Btp[0][:, :], 0.0, tag="l2a_bz")
            par = 0
            for li, Wl in enumerate(lv_list + [-8]):
                intra = Wl == -8
                Wl = abs(Wl)
                if not intra:
                    p.dma("sync", d0s[:, :], d0m[Wl][0], tag="l2a_ld0")
                    p.dma("sync", m1s_[:, :], d0m[Wl][1], tag="l2a_lm1")
                    p.tt("vector", tmpA[:, :], Dcur, m1s_[:, :], ALU.mult, tag="l2a_d1")
                    p.scan(Fbuf[:, 0:T], d0s[:, :], tmpA[:, :], 0.0, ALU.mult, ALU.add,
                           tag="l2a_fscan")
                    p.tt("vector", tmpB[:, :], Dcur, Fbuf[:, 0:T], ALU.subtract,
                         tag="l2a_eksub")
                    p.act(ekt[:, :], tmpB[:, :], AF.Exp, tag="l2a_ekexp")
                    mA, mB = msk[f"M{Wl}"], msk[f"MT{Wl}"]
                else:
                    p.act(ekt[:, :], tmpB[:, :], AF.Exp, bias=60.0, tag="l2a_ek8i")
                    mA, mB = msk["M8i"], msk["MT8i"]
                p.tt("vector", ktf[:, :], ekt[:, :], khs[:, :], ALU.mult, tag="l2a_ktf")
                p.tt("vector", qtf[:, :], ekt[:, :], qhs[:, :], ALU.mult, tag="l2a_qtf")
                if not intra:
                    p.tt("vector", tmpA[:, :], Fbuf[:, Wl:Wl + T], Dcur, ALU.subtract,
                         tag="l2a_easub")
                    p.act(ekt[:, :], tmpA[:, :], AF.Exp, tag="l2a_eaexp")
                else:
                    p.act(ekt[:, :], tmpB[:, :], AF.Exp, bias=-60.0, scale=-1.0,
                          tag="l2a_ea8i")
                p.tt("vector", ahh[:, :], ekt[:, :], khs[:, :], ALU.mult, tag="l2a_ahm")
                for g in range(4):
                    for cc in range(4):
                        c = g * 4 + cc
                        cs = slice(c * C, (c + 1) * C)
                        p.mm(P[0][:, cc * 128:(cc + 1) * 128], ktf[:, cs], ahh[:, cs],
                             tag="l2a_Amm")
                        p.mm(P[1][:, cc * 128:(cc + 1) * 128], ahh[:, cs], qtf[:, cs],
                             tag="l2a_Bmm")
                    gs_ = slice(g * 512, (g + 1) * 512)
                    p.select(App[1 - par][:, gs_], mA[:, :], P[0][:, :], App[par][:, gs_],
                             tag="l2a_Asel")
                    p.select(Btp[1 - par][:, gs_], mB[:, :], P[1][:, :], Btp[par][:, gs_],
                             tag="l2a_Bsel")
                par = 1 - par
            p.copy("vector", Bt16[:, :], Btp[par][:, :], tag="l2a_bt16")
            for c in range(NCH):
                bcol = beta_b[b][:, c * 16 + h:c * 16 + h + 1]
                p.ts("vector", Nt16[:, c * C:(c + 1) * C], App[par][:, c * C:(c + 1) * C],
                     bcol, ALU.mult, -1.0, ALU.mult, tag="l2a_nbeta")
            p.dma("sync", d0s[:, :], d0m[C][0], tag="l2a_ld0c")
            p.dma("sync", m1s_[:, :], d0m[C][1], tag="l2a_lm1c")
            p.tt("vector", tmpA[:, :], Dprev, m1s_[:, :], ALU.mult, tag="l2a_d1c")
            p.scan(Fbuf[:, 0:T], d0s[:, :], tmpA[:, :], 0.0, ALU.mult, ALU.add,
                   tag="l2a_f0scan")
            p.tt("vector", tmpB[:, :], Dcur, Fbuf[:, 0:T], ALU.subtract, tag="l2a_e0sub")
            p.act(ekt[:, :], tmpB[:, :], AF.Exp, tag="l2a_e0exp")
            p.tt("vector", kt0[:, :], ekt[:, :], khs[:, :], ALU.mult, tag="l2a_kt0")
            p.tt("vector", qt0[:, :], ekt[:, :], qhs[:, :], ALU.mult, tag="l2a_qt0")
            p.tt("vector", tmpA[:, :], Fbuf[:, C:C + T], Dcur, ALU.subtract,
                 tag="l2a_kEsub")
            p.act(ekt[:, :], tmpA[:, :], AF.Exp, tag="l2a_kEexp")
            p.tt("vector", kEb[:, :], ekt[:, :], khs[:, :], ALU.mult, tag="l2a_kEm")
            fgrid0 = Fbuf[:, 0:T].rearrange("p (c w) -> p c w", w=C)
            fgrid1 = Fbuf[:, C:C + T].rearrange("p (c w) -> p c w", w=C)
            p.tt("vector", dcols[:, :].rearrange("p (c o) -> p c o", o=1),
                 fgrid1[:, :, 0:1], fgrid0[:, :, 0:1], ALU.subtract, tag="l2a_dcsub")
            p.act(dcols[:, :], dcols[:, :], AF.Exp, tag="l2a_dcexp")
            p.tt("vector", qkb[:, :], qhs[:, :], khs[:, :], ALU.mult, tag="l2a_qk")
            for c in range(NCH):
                cs = slice(c * C, (c + 1) * C)
                p.mm(P[7][:, 256:257], qkb[:, cs], cm.ones_col16[0:64, :], tag="l2a_qkmm")
                p.copy("scalar", qkcols[:, c:c + 1], P[7][:, 256:257], tag="l2a_qke")
                tps2 = P[2][:, 0:32].bitcast(BF16)
                p.transpose(tps2, kEb[:, cs], cm.ident[0:64, 0:64], tag="l2a_ktr")
                p.copy("scalar", ktm[:, c * 64:(c + 1) * 64], tps2, tag="l2a_ktre")
                tps4 = P[4][:, 0:32].bitcast(BF16)
                p.transpose(tps4, vhs[:, cs], cm.ident[0:64, 0:64], tag="l2a_vtr")
                p.copy("scalar", vtm[:, c * 64:(c + 1) * 64], tps4, tag="l2a_vtre")
            p.memset("vector", S0f[0][:, :], 0.0, tag="l2a_s0z")
            p.memset("vector", S0b[:, :], 0.0, tag="l2a_s0bz")
            for c in range(NCH):
                cs = slice(c * C, (c + 1) * C)
                bcol = beta_b[b][:, c * 16 + h:c * 16 + h + 1]
                p.mm(P[4][:, 64:128], kt0[:, cs], S0b[:, :], tag="l2a_ks0")
                p.tt("vector", u0t[:, :], vtm[:, c * 64:(c + 1) * 64], P[4][:, 64:128],
                     ALU.subtract, tag="l2a_u0s")
                p.ts("vector", Xb[0][:, :], u0t[:, :], bcol, ALU.mult, tag="l2a_u0b")
                tpn = P[2][:, 128:192].bitcast(BF16)
                p.transpose(tpn, Nt16[:, cs], cm.ident[:, :], tag="l2a_ntr")
                p.copy("scalar", Ptb[0][:, :], tpn, tag="l2a_ntre")
                pu_cur = Nt16[:, cs]
                pt_cur = Ptb[0][:, :]
                for j in range(7):
                    p.mm_acc(P[5][:, 0:64],
                             [(cm.ident[:, :], Xb[j % 2][:, :]), (pt_cur, Xb[j % 2][:, :])],
                             tag="l2a_app")
                    p.copy("scalar", Xb[(j + 1) % 2][:, :], P[5][:, 0:64], tag="l2a_appe")
                    if j < 6:
                        p.mm(P[6][:, 0:128], pt_cur, pu_cur, tag="l2a_sqU")
                        p.mm(P[6][:, 128:256], pu_cur, pt_cur, tag="l2a_sqT")
                        pu_nxt = Pub[j % 2][:, :]
                        pt_nxt = (Ptb[1 - j % 2] if j > 0 else Ptb[1])[:, :]
                        p.copy("scalar", pu_nxt, P[6][:, 0:128], tag="l2a_squE")
                        p.copy("scalar", pt_nxt, P[6][:, 128:256], tag="l2a_sqtE")
                        pu_cur, pt_cur = pu_nxt, pt_nxt
                uX = Xb[7 % 2]
                p.copy("vector", uf32[:, :], uX[:, :], tag="l2a_uf32")
                p.mm_acc(P[7][:, 0:64],
                         [(qt0[:, cs], S0b[:, :]), (Bt16[:, cs], uX[:, :])], tag="l2a_omm")
                p.stt("vector", osb[:, :], uf32[:, :], qkcols[:, c:c + 1], P[7][:, 0:64],
                      ALU.mult, ALU.add, tag="l2a_odiag")
                p.act(sq64[:, :], osb[:, :], AF.Square, tag="l2a_gn1")
                p.reduce("vector", ncol[:, :], sq64[:, :], ALU.add, tag="l2a_gn2")
                p.act(ncol2[:, :], ncol[:, :], AF.Sqrt, bias=1e-5, scale=1.0 / 64,
                      tag="l2a_gn3")
                p.recip(ncol[:, :], ncol2[:, :], tag="l2a_gn3b")
                p.ts("vector", osb[:, :], osb[:, :], ncol[:, :], ALU.mult, tag="l2a_gn4")
                p.tt("vector", obb[:, :], osb[:, :], wrep[:, :], ALU.mult, tag="l2a_gn5")
                tpo = P[2][0:64, 0:64].bitcast(BF16)
                p.transpose(tpo, obb[:, :], cm.ident[:, :], tag="l2a_otr")
                p.copy("scalar", otr16[:, :], tpo, tag="l2a_otre")
                p.tt("vector", ofmh[:, cs], otr16[:, :], gsl[:, cs], ALU.mult,
                     tag="l2a_ogate")
                p.mm(P[3][0:64, 0:64], ktm[:, c * 64:(c + 1) * 64], uX[:, :],
                     tag="l2a_smm")
                p.stt("vector", S0f[(c + 1) % 2][:, :], S0f[c % 2][:, :],
                      dcols[:, c:c + 1], P[3][0:64, 0:64], ALU.mult, ALU.add,
                      tag="l2a_sup")
                p.copy("vector", S0b[:, :], S0f[(c + 1) % 2][:, :], tag="l2a_sb16")
            p.dma("sync", o_img[64 * h:64 * h + 64, h0:h0 + T], ofmh[:, :],
                  tag="l2a_ospill")
            if dbg and b == 0 and h == 0:
                for nm_, tl_ap in (("D", Dbuf), ("A", App[par]), ("Bt", Btp[par]),
                                   ("kt0", kt0), ("kE", kEb), ("beta", beta_b[0])):
                    d_ = p.dout(f"tap_{nm_}", tuple(tl_ap.shape),
                                F32 if tl_ap.dtype == F32 else BF16)
                    p.dma("sync", d_, tl_ap[:, :], tag=f"tap_{nm_}")
    p.free(Dbuf, Fbuf, tmpA, tmpB, d0s, m1s_, qhs, khs, vhs, gsl, ekt, ktf, qtf, ahh,
           kt0, qt0, kEb, qkb, *App, *Btp, Nt16, Bt16, ktm, vtm, qkcols, dcols,
           *S0f, S0b, *Pub, *Ptb, *Xb, u0t, osb, sq64, ncol, ncol2, obb, otr16, uf32, ofmh, Wf2t,
           *h1_b, *beta_b)

    Wot = load_w_bf16(p, "l2a_wo", W["Wo"])
    otile = [p.sb(f"l2a_ot{i}", (128, TC), BF16) for i in range(NI)]
    aev = [p.sb(f"l2a_aev{k}", (128, TC), F32) for k in range(2)]
    for (t0, tl) in tchunks(Tloc):
        for i in range(NI):
            p.dma("sync", otile[i][:, :tl], o_img[i * 128:(i + 1) * 128, t0:t0 + tl],
                  tag="l2a_ldo")
        for oc in range(8):
            ps = P[oc % 2]
            p.mm_acc(ps[:, :tl],
                     [(Wot[i][:, oc * 128:(oc + 1) * 128], otile[i][:, :tl])
                      for i in range(NI)], tag="l2a_wo")
            p.copy("scalar", aev[oc % 2][:, :tl], ps[:, :tl], tag="l2a_woev")
            p.dma("sync", a_hbm[oc * 128:(oc + 1) * 128, t0:t0 + tl], aev[oc % 2][:, :tl],
                  tag="l2a_aout")
    p.free(*Wot, *otile, *aev)


def build_l1_attn(p: Prog, cm: Common, x_hbm, W, a_hbm, Tloc=4096, nseq=2, dbg=False):
    T = Tloc // nseq
    DL = 32
    P = cm.P
    PAD = DL - 1
    SEQW = PAD + T

    v_img = p.dscratch("l1a_v", (1024, Tloc), BF16)
    ao_img = p.dscratch("l1a_ao", (1024, Tloc), BF16)

    cnt = np.minimum(np.arange(T) + 1, DL).astype(np.float32)
    cntrec_h = p.const("l1a_cntrec", np.tile(1.0 / cnt, nseq)[None, :])
    cntrow = p.sb("l1a_cntrow", (1, TC), F32)

    def ldbd(nm, neg=False):
        t = p.sb(f"l1a_{nm}", (128, 128), BF16)
        if neg:
            tf = p.sb(f"l1a_{nm}f", (128, 128), F32)
            p.dma("sync", tf[:, :], W[nm], tag=f"l1a_ld{nm}")
            p.ts("vector", t[:, :], tf[:, :], -1.0, ALU.mult, tag=f"l1a_neg{nm}")
            p.free(tf)
        else:
            p.dma("gpsimd", t[:, :], W[nm], tag=f"l1a_ld{nm}")
        return t
    bd_state = ldbd("bd_state")
    bd_strain = ldbd("bd_strain")
    bd_strain_neg = p.sb("l1a_bdsn", (128, 128), BF16)
    p.ts("vector", bd_strain_neg[:, :], bd_strain[:, :], -1.0, ALU.mult,
         tag="l1a_negbds")
    bd_statej = ldbd("bd_statej")
    bd_statei = ldbd("bd_statei")
    bd_damage = ldbd("bd_damage")
    bd_pos = ldbd("bd_pos")
    bdwb = p.sb("l1a_bdwb", (128, 8), BF16)
    p.dma("gpsimd", bdwb[:, :], W["bdwb"], tag="l1a_ldwb")
    bdwd = p.sb("l1a_bdwd", (128, 8), BF16)
    p.dma("gpsimd", bdwd[:, :], W["bdwd"], tag="l1a_ldwd")
    relf = p.sb("l1a_relf", (128, DL), BF16)
    p.dma("gpsimd", relf[:, :], W["rel_fm"], tag="l1a_ldrel")
    repc = p.sb("l1a_repc", (8, 1024), BF16)
    p.dma("gpsimd", repc[:, :], W["repc"], tag="l1a_ldrepc")
    bstate8 = p.sb("l1a_bst8", (128, 1), F32)
    p.dma("sync", bstate8[:, :], W["bstate8"], tag="l1a_ldbst")
    bstrain8 = p.sb("l1a_bsr8", (128, 1), F32)
    p.dma("sync", bstrain8[:, :], W["bstrain8"], tag="l1a_ldbsr")
    bdamage8 = p.sb("l1a_bdm8", (128, 1), F32)
    p.dma("sync", bdamage8[:, :], W["bdamage8"], tag="l1a_ldbdm")
    dmgb8 = p.sb("l1a_dmgb8", (8, 1), F32)
    p.dma("sync", dmgb8[:, :], W["dmgb8"], tag="l1a_lddmg")
    bdispc = load_vec_cols(p, "l1a_bdisp", W["b_disp"], n=256)
    bvalc = load_vec_cols(p, "l1a_bval", W["b_val"])
    bcprojc = load_vec_cols(p, "l1a_bcproj", W["b_cproj"])

    rp = p.sb("l1a_rp", (128, DL), F32)
    p.mm(P[5][:, 0:DL], bd_pos[:, :], relf[:, :], tag="l1a_rp")
    p.ts("vector", rp[:, :], P[5][:, 0:DL], bstrain8[:, :], ALU.add, tag="l1a_rpe")
    if dbg:
        rp_e = p.sb("l1a_rp_early", (128, DL), F32)
        p.copy("vector", rp_e[:, :], rp[:, :], tag="l1a_rpcap")

    dfm = [p.sb(f"l1a_d{i}", (128, Tloc), BF16) for i in range(2)]
    ev = [p.sb(f"l1a_ev{k}", (128, TC), F32) for k in range(2)]
    ev16 = p.sb("l1a_ev16", (128, TC), BF16)

    for b in range(nseq):
        h0 = b * T
        xful = [p.sb(p.uname("l1a_xf"), (128, T), F32) for _ in range(NI)]
        for i in range(NI):
            p.dma("sync", xful[i][:, :], x_hbm[i * 128:(i + 1) * 128, h0:h0 + T],
                  tag="l1a_ldx")
        xn = ln_apply_bf16(p, cm, xful, T, f"l1aln{b}")
        p.free(*xful)
        wdt = load_w_bf16(p, p.uname("l1a_wd"), W["w_disp"], I=1024, O=256)
        for oc in range(2):
            for (t0, tl) in tchunks(T):
                ps = P[(t0 // TC) % 2]
                p.mm_acc(ps[:, :tl],
                         [(wdt[i][:, oc * 128:(oc + 1) * 128], xn[i][:, t0:t0 + tl])
                          for i in range(NI)], tag="l1a_dmm")
                p.ts("vector", dfm[oc][:, h0 + t0:h0 + t0 + tl], ps[:, :tl],
                     bdispc[oc], ALU.add, tag="l1a_dev")
        p.free(*wdt)
        wvt = load_w_bf16(p, p.uname("l1a_wv"), W["w_val"])
        for oc in range(8):
            for (t0, tl) in tchunks(T):
                ps = P[(t0 // TC) % 2]
                p.mm_acc(ps[:, :tl],
                         [(wvt[i][:, oc * 128:(oc + 1) * 128], xn[i][:, t0:t0 + tl])
                          for i in range(NI)], tag="l1a_vmm")
                p.ts("vector", ev16[:, :tl], ps[:, :tl], bvalc[oc], ALU.add,
                     tag="l1a_vev")
                p.dma("scalar", v_img[oc * 128:(oc + 1) * 128, h0 + t0:h0 + t0 + tl],
                      ev16[:, :tl], tag="l1a_vspill")
        p.free(*wvt, *xn)

    ubuf_ = [p.sb(f"l1a_u{o}", (128, nseq * SEQW), F32) for o in range(2)]
    pbuf_ = [p.sb(f"l1a_p{o}", (128, nseq * SEQW), F32) for o in range(2)]
    wbuf_ = [p.sb(f"l1a_w{o}", (128, Tloc), F32) for o in range(2)]
    cumbuf = p.sb("l1a_cum", (128, nseq * (DL + T)), F32)
    qt = p.sb("l1a_qt", (128, TC), F32)
    st = p.sb("l1a_st", (128, TC), F32)
    s16 = p.sb("l1a_s16", (128, TC), BF16)
    ones512 = p.sb("l1a_ones512", (128, TC), F32)
    p.dma("sync", ones512[:, :], p.const("l1a_ones512h", np.ones((128, TC), np.float32)))
    for b in range(nseq):
        h0 = b * T
        for oc in range(2):
            ubuf, pbuf, wbuf = ubuf_[oc], pbuf_[oc], wbuf_[oc]
            cb0 = b * (DL + T) + DL
            p.memset("vector", cumbuf[:, b * (DL + T):cb0], 0.0, tag="l1a_cz")
            p.memset("vector", ubuf[:, b * SEQW:b * SEQW + PAD], 0.0, tag="l1a_uz")
            p.memset("vector", pbuf[:, b * SEQW:b * SEQW + PAD], 0.0, tag="l1a_pz")
            for (t0, tl) in tchunks(T):
                p.mm(P[2][:, :tl], bd_state[:, :], dfm[oc][:, h0 + t0:h0 + t0 + tl],
                     tag="l1a_qmm")
                p.copy("scalar", qt[:, :tl], P[2][:, :tl], tag="l1a_qev")
                init = 0.0 if t0 == 0 else cumbuf[:, cb0 + t0 - 1:cb0 + t0]
                p.scan(cumbuf[:, cb0 + t0:cb0 + t0 + tl], ones512[:, :tl], qt[:, :tl],
                       init, ALU.mult, ALU.add, tag="l1a_qscan")
                p.dma("sync", cntrow[:, :tl], cntrec_h[:, t0:t0 + tl], tag="l1a_ldcnt")
                rep_row(p, cm, P[3][:, :tl], cntrow[:, :tl], tag="l1a_cntrep")
                p.tt("vector", st[:, :tl], cumbuf[:, cb0 + t0:cb0 + t0 + tl],
                     cumbuf[:, cb0 + t0 - DL:cb0 + t0 - DL + tl], ALU.subtract,
                     tag="l1a_wsum")
                p.tt("vector", st[:, :tl], st[:, :tl], P[3][:, :tl], ALU.mult,
                     tag="l1a_wsc")
                p.stt("vector", st[:, :tl], st[:, :tl], bstate8[:, :], qt[:, :tl],
                      ALU.add, ALU.subtract, tag="l1a_smk")
                p.copy("vector", s16[:, :tl], st[:, :tl], tag="l1a_s16")
                dsl = dfm[oc][:, h0 + t0:h0 + t0 + tl]
                p.mm_acc(P[4][:, :tl], [(bd_strain[:, :], dsl), (bd_statej[:, :],
                         s16[:, :tl])], tag="l1a_umm")
                p.copy("scalar", ubuf[:, b * SEQW + PAD + t0:b * SEQW + PAD + t0 + tl],
                       P[4][:, :tl], tag="l1a_uev")
                p.mm_acc(P[5][:, :tl], [(bd_strain_neg[:, :], dsl), (bd_statei[:, :],
                         s16[:, :tl])], tag="l1a_wmm")
                p.copy("scalar", wbuf[:, h0 + t0:h0 + t0 + tl], P[5][:, :tl],
                       tag="l1a_wev")
                p.mm(P[6][:, :tl], bd_damage[:, :], dsl, tag="l1a_pmm")
                p.copy("scalar", pbuf[:, b * SEQW + PAD + t0:b * SEQW + PAD + t0 + tl],
                       P[6][:, :tl], tag="l1a_pev")
    p.free(qt, st, s16, *dfm, cumbuf)

    vpads = [p.sb(f"l1a_vp{v}", (128, PAD + TC), F32) for v in range(8)]
    oacc = [[p.sb(f"l1a_oa{v}{k}", (128, TC), F32) for k in range(2)] for v in range(8)]
    Zp = [[p.sb(f"l1a_Z{o}{k}", (8, TC), F32) for k in range(2)] for o in range(2)]
    t1 = p.sb("l1a_t1", (128, TC), F32)
    ga = [p.sb(f"l1a_ga{o}", (128, TC), BF16) for o in range(2)]
    da = [p.sb(f"l1a_da{o}", (128, TC), BF16) for o in range(2)]
    sg = p.sb("l1a_sg", (8, TC), F32)
    lg = [p.sb(f"l1a_lg{o}", (8, TC), F32) for o in range(2)]
    e32 = [p.sb(f"l1a_e32{o}", (8, TC), F32) for o in range(2)]
    e16 = [p.sb(f"l1a_e16{o}", (8, TC), BF16) for o in range(2)]
    zr16 = p.sb("l1a_zr16", (8, TC), BF16)
    tmpv = p.sb("l1a_tmpv", (128, TC), F32)
    ao16 = p.sb("l1a_ao16", (128, TC), BF16)
    for b in range(nseq):
        h0 = b * T
        for (t0, tl) in tchunks(T):
            seq_start = t0 == 0
            for v in range(8):
                if seq_start:
                    p.memset("vector", vpads[v][:, 0:PAD], 0.0, tag="l1a_vpz")
                    p.dma("gpsimd", vpads[v][:, PAD:PAD + tl],
                          v_img[v * 128:(v + 1) * 128, h0 + t0:h0 + t0 + tl],
                          tag="l1a_vpl")
                else:
                    p.dma("gpsimd", vpads[v][:, 0:PAD + tl],
                          v_img[v * 128:(v + 1) * 128, h0 + t0 - PAD:h0 + t0 + tl],
                          tag="l1a_vpl2")
                p.memset("vector", oacc[v][0][:, :tl], 0.0, tag="l1a_oaz")
            for o in range(2):
                p.memset("vector", Zp[o][0][:, :tl], 0.0, tag="l1a_zz")
            for j in range(DL):
                for o in range(2):
                    up = ubuf_[o][:, b * SEQW + j + t0:b * SEQW + j + t0 + tl]
                    pp = pbuf_[o][:, b * SEQW + j + t0:b * SEQW + j + t0 + tl]
                    p.tt("vector", t1[:, :tl], up, wbuf_[o][:, h0 + t0:h0 + t0 + tl],
                         ALU.add, tag="l1a_ba")
                    p.act(ga[o][:, :tl], t1[:, :tl], AF.Gelu, bias=rp[:, j:j + 1],
                          tag="l1a_bg")
                    p.tt("vector", t1[:, :tl], pp, pbuf_[o][:, b * SEQW + PAD + t0:
                         b * SEQW + PAD + t0 + tl], ALU.subtract, tag="l1a_dd")
                    p.act(da[o][:, :tl], t1[:, :tl], AF.Gelu, bias=bdamage8[:, :],
                          tag="l1a_dg")
                    p.mm(P[0][0:8, :tl], bdwb[:, :], ga[o][:, :tl], tag="l1a_bl")
                    p.mm(P[1][0:8, :tl], bdwd[:, :], da[o][:, :tl], tag="l1a_dl")
                    p.act(sg[:, :tl], P[1][0:8, :tl], AF.Sigmoid, bias=dmgb8[:, :],
                          tag="l1a_sig")
                    p.stt("vector", lg[o][:, :tl], sg[:, :tl], -10.0, P[0][0:8, :tl],
                          ALU.mult, ALU.add, tag="l1a_lg")
                    p.act(e32[o][:, :tl], lg[o][:, :tl], AF.Exp, tag="l1a_exp")
                    if seq_start and j < PAD:
                        p.memset("vector", e32[o][:, 0:PAD - j], 0.0, tag="l1a_cm")
                    p.copy("vector", e16[o][:, :tl], e32[o][:, :tl], tag="l1a_e16")
                    zi = (j + 1) % 2
                    p.tt("vector", Zp[o][zi][:, :tl], Zp[o][1 - zi][:, :tl],
                         e32[o][:, :tl], ALU.add, tag="l1a_zacc")
                for v in range(8):
                    o = v // 4
                    p.mm(P[2 + v % 2][:, :tl], repc[:, v * 128:(v + 1) * 128],
                         e16[o][:, :tl], tag="l1a_rep")
                    p.tt("vector", tmpv[:, :tl], P[2 + v % 2][:, :tl],
                         vpads[v][:, j:j + tl], ALU.mult, tag="l1a_om")
                    zi = (j + 1) % 2
                    p.tt("vector", oacc[v][zi][:, :tl], oacc[v][1 - zi][:, :tl],
                         tmpv[:, :tl], ALU.add, tag="l1a_oadd")
            for o in range(2):
                p.recip(lg[o][:, :tl], Zp[o][DL % 2][:, :tl], tag="l1a_zrec")
            for v in range(8):
                o = v // 4
                p.copy("vector", zr16[:, :tl], lg[o][:, :tl], tag="l1a_zr16")
                p.mm(P[2 + v % 2][:, :tl], repc[:, v * 128:(v + 1) * 128],
                     zr16[:, :tl], tag="l1a_zrep")
                p.tt("vector", ao16[:, :tl], oacc[v][DL % 2][:, :tl],
                     P[2 + v % 2][:, :tl], ALU.mult, tag="l1a_onrm")
                p.dma("scalar", ao_img[v * 128:(v + 1) * 128, h0 + t0:h0 + t0 + tl],
                      ao16[:, :tl], tag="l1a_aospill")
    if dbg:
        p._taps = {"u0": ubuf_[0], "u1": ubuf_[1],
                   "p0": pbuf_[0], "p1": pbuf_[1], "w0": wbuf_[0], "w1": wbuf_[1],
                   "rp": rp, "rpe": rp_e, "bdpos": bd_pos, "relf": relf,
                   "bst8": bstrain8}
        vtap = p.dout("tap_vimg", (1024, Tloc), BF16)
        p.dma("sync", vtap, v_img, tag="tapv")
        atap = p.dout("tap_aoimg", (1024, Tloc), BF16)
        p.dma("sync", atap, ao_img, tag="tapao")
    p.free(*vpads, t1, sg, zr16, tmpv, ao16, *ubuf_, *pbuf_, *wbuf_, *ga, *da, *lg,
           *e32, *e16)
    for v in range(8):
        p.free(*oacc[v])
    for o in range(2):
        p.free(*Zp[o])

    wct = load_w_bf16(p, "l1a_wc", W["w_cproj"])
    otile = [p.sb(f"l1a_ot{i}", (128, TC), BF16) for i in range(NI)]
    aev = [p.sb(f"l1a_aev{k}", (128, TC), F32) for k in range(2)]
    for (t0, tl) in tchunks(Tloc):
        for i in range(NI):
            p.dma("sync", otile[i][:, :tl], ao_img[i * 128:(i + 1) * 128, t0:t0 + tl],
                  tag="l1a_ldao")
        for oc in range(8):
            ps = P[oc % 2]
            p.mm_acc(ps[:, :tl],
                     [(wct[i][:, oc * 128:(oc + 1) * 128], otile[i][:, :tl])
                      for i in range(NI)], tag="l1a_cp")
            p.ts("vector", aev[oc % 2][:, :tl], ps[:, :tl], bcprojc[oc], ALU.add,
                 tag="l1a_cpe")
            p.dma("sync", a_hbm[oc * 128:(oc + 1) * 128, t0:t0 + tl], aev[oc % 2][:, :tl],
                  tag="l1a_aout")
    p.free(*wct, *otile, *aev)



def _prep_l0f(W):
    w1 = W["w1"]; b1 = W["b1"]; w2 = W["w2"]
    outs = {}
    w1ps, b1ps = [], []
    for l in range(2):
        a, g = w1[l][:, :2730], w1[l][:, 2730:]
        ba, bg = b1[l][:2730], b1[l][2730:]
        pad = np.zeros((1024, 86), np.float32)
        w1ps.append(np.concatenate([a, pad, g, pad], 1))
        b1ps.append(np.concatenate([ba, np.zeros(86, np.float32), bg,
                                    np.zeros(86, np.float32)]))
    outs["w_w1p"] = np.ascontiguousarray(np.stack(w1ps))
    outs["w_b1p"] = np.ascontiguousarray(np.stack(b1ps))
    outs["w_w2p"] = np.ascontiguousarray(
        np.concatenate([w2, np.zeros((2, 86, 1024), np.float32)], 1))
    outs["w_wqkv"] = W["wqkv"]; outs["w_wo"] = W["wo"]
    outs["w_ln_g"] = W["ln_g"]; outs["w_ln_b"] = W["ln_b"]; outs["w_b2"] = W["b2"]
    for l in range(2):
        outs[f"w_qs{l}"] = np.ascontiguousarray(np.tile(W["q_scale"][l], 2)[:, None])
        outs[f"w_ks{l}"] = np.ascontiguousarray(np.tile(W["k_scale"][l], 2)[:, None])
    return outs


def _prep_l1a(W):
    I8 = np.eye(8, dtype=np.float32)
    h = {}
    h["bd_state"] = np.kron(I8, W["w_state"]).astype(np.float32)
    h["bd_strain"] = np.kron(I8, W["w_strain"]).astype(np.float32)
    h["bd_statei"] = np.kron(I8, W["w_state_i"]).astype(np.float32)
    h["bd_statej"] = np.kron(I8, W["w_state_j"]).astype(np.float32)
    h["bd_damage"] = np.kron(I8, W["w_damage"]).astype(np.float32)
    h["bd_pos"] = np.kron(I8, W["w_pos"]).astype(np.float32)
    h["bdwb"] = np.kron(I8, W["w_bond_out"]).astype(np.float32)
    h["bdwd"] = np.kron(I8, W["w_damage_out"]).astype(np.float32)
    h["rel_fm"] = np.ascontiguousarray(np.tile(W["rel_pos_emb"].T, (8, 1)).astype(np.float32))
    h["bstate8"] = np.ascontiguousarray(np.tile(W["b_state"], 8)[:, None].astype(np.float32))
    h["bstrain8"] = np.ascontiguousarray(np.tile(W["b_strain"], 8)[:, None].astype(np.float32))
    h["bdamage8"] = np.ascontiguousarray(np.tile(W["b_damage"], 8)[:, None].astype(np.float32))
    h["dmgb8"] = np.full((8, 1), W["b_damage_out"][0], np.float32)
    repc = np.zeros((8, 1024), np.float32)
    for v in range(8):
        for hh in range(2):
            repc[(2 * v + hh) % 8, v * 128 + hh * 64:v * 128 + (hh + 1) * 64] = 1
    h["repc"] = repc
    for k in ("w_disp", "b_disp", "w_val", "b_val", "w_cproj", "b_cproj"):
        h[k] = W[k]
    return h


def kernel(**inputs):
    import time
    NCORE = 8
    Tloc = 4096
    t00 = time.time()
    x = np.asarray(inputs["L0_attn_x"], np.float32)
    xfm = [np.ascontiguousarray(x[2 * r:2 * r + 2].reshape(Tloc, D).T)
           for r in range(NCORE)]

    def sub(pre, drop=()):
        out = {}
        for k, v in inputs.items():
            if k.startswith(pre):
                nm = k[len(pre):]
                if nm not in drop:
                    out[nm] = np.asarray(v, np.float32)
        return out

    cores = list(range(NCORE))

    def launch(name, build_fn, wmap, ims_extra=None, a_in=None, out_key="out"):
        p = Prog(name)
        cm = Common(p)
        x_h = p.din("x", (D, Tloc))
        args = [p, cm, x_h]
        if a_in is not None:
            a_h = p.din("a", (D, Tloc))
            args.append(a_h)
        Wap = {}
        for k, v in wmap.items():
            Wap[k] = p.din("w_" + k, v.shape)
        if ims_extra is not None:
            for k, shp, dt in ims_extra["specs"]:
                Wap[k] = p.din("w_" + k, shp, dt)
        o_h = p.dout(out_key, (D, Tloc))
        args += [Wap, o_h]
        build_fn(*args, Tloc=Tloc)
        p.build()
        ims = []
        for r in cores:
            im = {"x": xfm[r]}
            if a_in is not None:
                im["a"] = a_in[r]
            for k, v in wmap.items():
                im["w_" + k] = v
            if ims_extra is not None:
                for k, v in ims_extra["per_core"][r].items():
                    im["w_" + k] = v
            ims.append(im)
        import os as _os
        if _os.environ.get("KDRY"):
            print(f"[kernel] {name} built (dry) ops={len(p.ops)} t={time.time()-t00:.1f}s",
                  flush=True)
            return [np.zeros((D, Tloc), np.float32) for _ in range(NCORE)]
        res = p.run(ims, core_ids=cores)
        print(f"[kernel] {name} done t={time.time()-t00:.1f}s", flush=True)
        return [np.asarray(res.results[r][out_key]) for r in range(NCORE)]

    w = sub("L0_attn_", drop=("x",))
    a = launch("k_l0a", build_l0_attn, w, out_key="out")
    pw = _prep_l0f(sub("L0_ffn_"))
    wmap = {k[2:]: v for k, v in pw.items()}
    xfm = launch("k_l0f", build_l0_ffn, wmap, a_in=a, out_key="out")

    w = _prep_l1a(sub("L1_attn_"))
    a = launch("k_l1a", build_l1_attn, w, out_key="a")
    wf = sub("L1_ffn_")
    wmap = {"gate_w": wf["gate_w"], "gate_b": wf["gate_b"]}
    per_core = []
    for r in cores:
        per_core.append({
            "w1m": np.ascontiguousarray(wf["w1"][r]),
            "b1m": np.ascontiguousarray(wf["b1"][r]),
            "w2m": np.ascontiguousarray(wf["w2"][r]),
            "b2m": np.ascontiguousarray(wf["b2"][r]),
            "rowsel": np.full((8, 1), r * 32, np.int32),
            "myrows": (r * 640 + np.arange(5)[None, :] * 128
                       + np.arange(128)[:, None]).astype(np.int32),
        })
    ims_extra = {"specs": [("w1m", (D, 4096), F32), ("b1m", (4096,), F32),
                           ("w2m", (4096, D), F32), ("b2m", (D,), F32),
                           ("rowsel", (8, 1), I32), ("myrows", (128, 5), I32)],
                 "per_core": per_core}
    xfm = launch("k_l1f", lambda p, cm, x_h, a_h, Wap, o_h, Tloc:
                 build_l1_ffn(p, cm, x_h, a_h, Wap, o_h, Tloc=Tloc, n_cores=NCORE),
                 wmap, ims_extra=ims_extra, a_in=a, out_key="out")

    w = sub("L2_attn_")
    a = launch("k_l2a", build_l2_attn, w, out_key="a")
    wmap = sub("L2_ffn_")
    xfm = launch("k_l2f", build_l2_ffn, wmap, a_in=a, out_key="out")

    wl = sub("L3_attn_")
    w1p = np.zeros((D, 640), np.float32)
    for f in range(5):
        w1p[:, 128 * f:128 * f + 32] = wl["time_maa_w1"][:, 32 * f:32 * (f + 1)]
    wmap = {"w1p": w1p, "w2": wl["time_maa_w2"], "td": wl["time_decay"].ravel(),
            "tdw1": wl["time_decay_w1"], "tdw2": wl["time_decay_w2"],
            "hdb": wl["head_decay_bias"], "ss": wl["slow_scale"],
            "dmix": wl["decay_mix"], "ln_gamma": wl["ln_gamma"],
            "ln_beta": wl["ln_beta"]}
    for nm in ("x", "w", "k", "v", "r", "g"):
        wmap[f"maa_{nm}"] = wl[f"time_maa_{nm}"].ravel()
    for nm in ("W_r", "W_k", "W_v", "W_g", "W_o"):
        wmap[nm] = wl[nm]

    def b_l3a(p, cm, x_h, Wap, o_h, Tloc):
        W2 = dict(Wap)
        W2["sscale"] = W2.pop("ss")
        build_l3_attn(p, cm, x_h, W2, o_h, Tloc=Tloc, nseq=2)
    a = launch("k_l3a", b_l3a, wmap, out_key="a")
    wf = sub("L3_ffn_")
    wmap = {
        "centT": np.ascontiguousarray(wf["centroids"].T),
        "gamma": wf["gamma"], "beta": wf["beta"],
        "W1T": np.ascontiguousarray(np.concatenate([wf["W1"][e].T for e in range(8)], 0)),
        "W2T": np.ascontiguousarray(np.concatenate([wf["W2"][e].T for e in range(8)], 0)),
        "b1T": np.ascontiguousarray(np.concatenate(
            [wf["b1"][e].reshape(32, 128).T for e in range(8)], 0)),
        "b2T": np.ascontiguousarray(np.concatenate(
            [wf["b2"][e].reshape(8, 128).T for e in range(8)], 0)),
    }
    xfm = launch("k_l3f", build_l3_ffn, wmap, a_in=a, out_key="out")

    out = np.empty((16, 2048, 1024), np.float32)
    for r in range(NCORE):
        out[2 * r:2 * r + 2] = xfm[r].T.reshape(2, 2048, 1024)
    return out
